# version 06
# baseline (speedup 1.0000x reference)
import numpy as np
from contextlib import ExitStack

import concourse.bass as bass
import concourse.mybir as mybir
from concourse.bass_utils import run_bass_kernel_spmd

F32 = mybir.dt.float32
BF16 = mybir.dt.bfloat16
ALU = mybir.AluOpType
ACT = mybir.ActivationFunctionType
AX = mybir.AxisListType

B = 2
L = 2048
D = 1024
DI = 2048
S = 16
R = 64
DH = 4096
T = 512
NT = L // T
DCH = DI // 128
DMCH = D // 128
HCH = DH // 128
NSUB = T // 128
SG = 4
SGS = S // SG
RND = 12582912.0
EPS = 1e-5

DEV_DRAINS = False
DEBUG_SCRATCH = False

_DRAIN_OPS = frozenset((
    "tensor_tensor", "tensor_copy", "tensor_scalar", "tensor_single_scalar",
    "scalar_tensor_tensor", "tensor_tensor_scan", "tensor_reduce",
    "reduce_sum", "reduce_max", "memset", "activation", "copy",
    "reciprocal", "iota", "tensor_relu", "tensor_add", "tensor_mul",
    "tensor_sub", "tensor_max", "tensor_scalar_max",
))


class _DrainProxy:
    def __init__(self, eng):
        self._e = eng

    def __getattr__(self, name):
        f = getattr(self._e, name)
        if name in _DRAIN_OPS:
            def g(*a, **k):
                r = f(*a, **k)
                self._e.drain()
                return r
            return g
        return f


class Ctx:
    def __init__(self, nc, block, es):
        self.nc = nc
        self.block = block
        self.es = es
        self.sems = {}

    def mksem(self, *names):
        for n in names:
            self.sems[n] = [self.es.enter_context(self.nc.semaphore(n)), 0]

    def sem(self, name):
        return self.sems[name][0]

    def fence_inc(self, e, name, n=1):
        s = self.sems[name]
        s[1] += n
        e.engine_nop().then_inc(s[0], n)

    def bump(self, name, n=1):
        s = self.sems[name]
        s[1] += n
        return (s[0], n)

    def inc(self, instr, name, n=1):
        s = self.sems[name]
        s[1] += n
        instr.then_inc(s[0], n)
        return s[1]

    def val(self, name):
        return self.sems[name][1]

    def barrier(self, on, store_sems):
        @on("sync")
        def _(e):
            for s_ in store_sems:
                self.wait(e, s_, self.val(s_))
            self.sems["tileb"][1] += 1
            e.sem_inc(self.sems["tileb"][0], 1)
        tb = self.val("tileb")
        for eng in ("vector", "scalar", "tensor", "gpsimd"):
            @on(eng)
            def _(e, tb=tb):
                self.wait(e, "tileb", tb)

    def wait(self, eng, name, v=None):
        s = self.sems[name]
        eng.wait_ge(s[0], s[1] if v is None else v)


def build():
    nc = bass.Bass("TRN2", target_bir_lowering=False,
                   detect_race_conditions=DEV_DRAINS)

    _scr_kind2 = "ExternalOutput" if DEBUG_SCRATCH else "Internal"
    x_in = nc.dram_tensor("x_in", [B, L, D], F32, kind="ExternalInput").ap()
    w_in = nc.dram_tensor("w_in", [D, 2 * DI], F32, kind="ExternalInput").ap()
    conv_w = nc.dram_tensor("conv_w", [4, DI], F32, kind="ExternalInput").ap()
    conv_b = nc.dram_tensor("conv_b", [DI], F32, kind="ExternalInput").ap()
    w_x = nc.dram_tensor("w_x", [DI, R + 2 * S], F32, kind="ExternalInput").ap()
    w_dt = nc.dram_tensor("w_dt", [R, DI], F32, kind="ExternalInput").ap()
    b_dt = nc.dram_tensor("b_dt", [DI], F32, kind="ExternalInput").ap()
    a_log = nc.dram_tensor("a_log", [DI, S], F32, kind="ExternalInput").ap()
    d_vec = nc.dram_tensor("d_vec", [DI], F32, kind="ExternalInput").ap()
    w_out = nc.dram_tensor("w_out", [DI, D], F32, kind="ExternalInput").ap()
    f_w1 = nc.dram_tensor("f_w1", [DH, D], F32, kind="ExternalInput").ap()
    f_cw = nc.dram_tensor("f_cw", [DH, 3], F32, kind="ExternalInput").ap()
    f_cb = nc.dram_tensor("f_cb", [DH], F32, kind="ExternalInput").ap()
    f_w2 = nc.dram_tensor("f_w2", [D, DH], F32, kind="ExternalInput").ap()
    y_out = nc.dram_tensor("y_out", [B, L, D], F32, kind="ExternalOutput").ap()

    _scr_kind0 = "ExternalOutput" if DEBUG_SCRATCH else "Internal"
    wih_s = nc.dram_tensor("wih_s", [2 * DI // 128, 8, 128, 128], BF16, kind=_scr_kind0).ap()
    wil_s = nc.dram_tensor("wil_s", [2 * DI // 128, 8, 128, 128], BF16, kind=_scr_kind0).ap()
    woh_s = nc.dram_tensor("woh_s", [DI, D], BF16, kind="Internal").ap()
    wol_s = nc.dram_tensor("wol_s", [DI, D], BF16, kind="Internal").ap()
    w1qt_s = nc.dram_tensor("w1qt_s", [DMCH, 128, DH], BF16, kind=_scr_kind2).ap()
    w2qt_s = nc.dram_tensor("w2qt_s", [DH, D], BF16, kind=_scr_kind2).ap()
    bc_s = nc.dram_tensor("bc_s", [2, 2 * S, T], F32, kind="Internal").ap()
    _scr_kind = "ExternalOutput" if DEBUG_SCRATCH else "Internal"
    x2_s = nc.dram_tensor("x2_s", [B, L, D], F32, kind=_scr_kind).ap()
    h_scr = nc.dram_tensor("h_scr", [DH, L], F32, kind=_scr_kind).ap()
    dbg_c = nc.dram_tensor("dbg_c", [128, 4096], F32, kind=_scr_kind).ap() if DEBUG_SCRATCH else None
    dbg_b = nc.dram_tensor("dbg_b", [128, 8192], BF16, kind=_scr_kind).ap() if DEBUG_SCRATCH else None
    s2r_s = nc.dram_tensor("s2r_s", [B, L], F32, kind="Internal").ap()
    i1r_s = nc.dram_tensor("i1r_s", [B, L], F32, kind="Internal").ap()

    es = ExitStack()
    with es:
        def sb(name, shape, dt=F32):
            return es.enter_context(nc.sbuf_tensor(name, shape, dt))

        wxt = sb("wxt", [128, DCH, 96])
        wdtt = sb("wdtt", [64, DI])
        cwt = sb("cwt", [128, DCH, 4])
        cbt = sb("cbt", [128, DCH])
        bdtt = sb("bdtt", [128, DCH])
        dvt = sb("dvt", [128, DCH])
        ang = sb("ang", [128, DCH, S])
        fcwt = sb("fcwt", [128, HCH, 3])
        fcbt = sb("fcbt", [128, HCH])
        idn = sb("idn", [128, 128])
        idnb = sb("idnb", [128, 128], BF16)
        onescol = sb("onescol", [1, 128])
        ones128 = sb("ones128", [128, 1])
        k1c = sb("k1c", [128, 1])
        k2c = sb("k2c", [128, 1])
        sw1b = sb("sw1b", [128, 1])
        sw2b = sb("sw2b", [128, 1])
        hstate = sb("hstate", [128, DCH, S])
        xtail = sb("xtail", [128, DCH, 3])
        scr1 = sb("scr1", [128, 1024])
        scrb = sb("scrb", [128, 2048], BF16)
        scrs = sb("scrs", [128, 16])
        cst = sb("cst", [128, 2])
        projt = sb("projt", [96, T])
        inv2t = sb("inv2t", [128, 16])

        block = es.enter_context(nc.Block())
        ctx = Ctx(nc, block, es)
        ctx.mksem("dmw", "dma", "dmo", "pe", "dve", "sce", "gp",
                  "dwx0", "dwx1", "dwo0", "dwo1", "dh0", "dh1", "dw20", "dw21",
                  "dho", "dxo", "dyo", "tileb")

        def on(name):
            dec = getattr(block, name)
            if not DEV_DRAINS or name in ("sync", "tensor"):
                return dec

            def wrap(fn):
                return dec(lambda e: fn(_DrainProxy(e)))
            return wrap

        SET_LNEXP = 6
        SET_SILU = 18

        def act_set(e, set_id):
            e.drain()
            e.add_instruction(mybir.InstLoadActFuncSet(
                name=nc.get_next_instruction_name(), act_func_set_id=set_id, ins=[], outs=[]))
            e.drain()

        def tpose(e, out, in_, ident, start, stop):
            return e.matmul(out, in_, ident, is_transpose=True, start=start, stop=stop)

        def emit_ln(sqbuf, lnfbuf):
            @on("vector")
            def _(e):
                e.reduce_sum(out=scrs[:, 0:1], in_=scr1[:, :], axis=AX.X); ctx.fence_inc(e, "dve")
            dve_s = ctx.val("dve")

            @on("scalar")
            def _(e, dve_s=dve_s):
                ctx.wait(e, "dve", dve_s)
                e.activation(out=sqbuf, in_=scr1[:, :], func=ACT.Square).then_inc(*ctx.bump("sce", 1))
            sce_sq = ctx.val("sce")

            @on("vector")
            def _(e, sce_sq=sce_sq):
                ctx.wait(e, "sce", sce_sq)
                e.reduce_sum(out=scrs[:, 1:2], in_=sqbuf, axis=AX.X)
                e.drain()
                e.tensor_single_scalar(out=scrs[:, 2:3], in_=scrs[:, 0:1], scalar=1.0 / D, op=ALU.mult)
                e.tensor_single_scalar(out=scrs[:, 3:4], in_=scrs[:, 1:2], scalar=1.0 / D, op=ALU.mult)
                e.drain()
                e.tensor_tensor(out=scrs[:, 4:5], in0=scrs[:, 2:3], in1=scrs[:, 2:3], op=ALU.mult)
                e.drain()
                e.tensor_tensor(out=scrs[:, 4:5], in0=scrs[:, 3:4], in1=scrs[:, 4:5], op=ALU.subtract)
                e.drain(); ctx.fence_inc(e, "dve")
            dve_v = ctx.val("dve")

            @on("scalar")
            def _(e, dve_v=dve_v):
                ctx.wait(e, "dve", dve_v)
                e.activation(out=scrs[:, 5:6], in_=scrs[:, 4:5], func=ACT.Ln, bias=cst[:, 0:1])
                e.drain()
                e.activation(out=scrs[:, 6:7], in_=scrs[:, 5:6], func=ACT.Exp, scale=-0.5)
                e.drain()
                e.activation(out=scrs[:, 6:7], in_=scrs[:, 6:7], func=ACT.Identity).then_inc(*ctx.bump("sce", 1))
            sce_r = ctx.val("sce")

            @on("vector")
            def _(e, sce_r=sce_r):
                ctx.wait(e, "sce", sce_r)
                e.tensor_tensor(out=scrs[:, 7:8], in0=scrs[:, 2:3], in1=scrs[:, 6:7], op=ALU.mult)
                e.drain()
                e.tensor_single_scalar(out=scrs[:, 7:8], in_=scrs[:, 7:8], scalar=-1.0, op=ALU.mult)
                e.drain(); ctx.fence_inc(e, "dve")
            dve_mb = ctx.val("dve")

            @on("scalar")
            def _(e, dve_mb=dve_mb):
                ctx.wait(e, "dve", dve_mb)
                e.activation(out=lnfbuf, in_=scr1[:, :], func=ACT.Identity,
                             scale=scrs[:, 6:7], bias=scrs[:, 7:8]).then_inc(*ctx.bump("sce", 1))
            return ctx.val("sce")

        with nc.sbuf_tensor("pscrA", [128, DH], F32) as pscrA, \
             nc.sbuf_tensor("pscrB", [128, DH], BF16) as pscrB, \
             nc.sbuf_tensor("pscrC", [128, DH], BF16) as pscrC, \
             nc.sbuf_tensor("stag", [128, HCH, 128], BF16) as stag, \
             nc.sbuf_tensor("cw4", [4, DI], F32) as cw4, \
             nc.sbuf_tensor("vec3", [1, 3 * DI + DH], F32) as vec3, \
             nc.sbuf_tensor("msum", [128, 2], F32) as msum, \
             nc.sbuf_tensor("smv", [1, 8], F32) as smv, \
             nc.psum_tensor("pps0", [128, 512], F32) as pps0, \
             nc.psum_tensor("ppsb0", [128, 512], BF16) as ppsb0, \
             nc.psum_tensor("ppsb1", [128, 512], BF16) as ppsb1:

            @on("sync")
            def _(e):
                e.dma_start(out=wxt[:, :, :], in_=w_x.rearrange("(c p) m -> p c m", p=128)).then_inc(*ctx.bump("dmw", 16))
                e.dma_start(out=wdtt[:, :], in_=w_dt).then_inc(*ctx.bump("dmw", 16))
                e.dma_start(out=ang[:, :, :], in_=a_log.rearrange("(c p) s -> p c s", p=128)).then_inc(*ctx.bump("dmw", 16))
                e.dma_start(out=fcwt[:, :, :], in_=f_cw.rearrange("(c p) k -> p c k", p=128)).then_inc(*ctx.bump("dmw", 16))
                e.dma_start(out=cw4[:, :], in_=conv_w).then_inc(*ctx.bump("dmw", 16))
                e.dma_start(out=vec3[:, 0:DI], in_=conv_b.unsqueeze(0)).then_inc(*ctx.bump("dmw", 16))
                e.dma_start(out=vec3[:, DI:2 * DI], in_=b_dt.unsqueeze(0)).then_inc(*ctx.bump("dmw", 16))
                e.dma_start(out=vec3[:, 2 * DI:3 * DI], in_=d_vec.unsqueeze(0)).then_inc(*ctx.bump("dmw", 16))
                e.dma_start(out=vec3[:, 3 * DI:3 * DI + DH], in_=f_cb.unsqueeze(0)).then_inc(*ctx.bump("dmw", 16))
            dmw0 = ctx.val("dmw")

            @on("gpsimd")
            def _(e):
                e.iota(out=scr1[:, 0:128], pattern=[[-1, 128]], base=0,
                       channel_multiplier=1, allow_small_or_imprecise_dtypes=True); ctx.fence_inc(e, "gp")
            gp_i = ctx.val("gp")

            @on("vector")
            def _(e, gp_i=gp_i, dmw0=dmw0):
                ctx.wait(e, "gp", gp_i)
                e.tensor_single_scalar(out=idn[:], in_=scr1[:, 0:128], scalar=0.0, op=ALU.is_equal)
                e.tensor_copy(idnb[:], idn[:])
                e.memset(onescol[:], 1.0)
                e.memset(ones128[:], 1.0)
                e.memset(scrs[:, 15:16], 0.0)
                e.memset(cst[:, 0:1], EPS)
                e.memset(cst[:, 1:2], 1.0)
                ctx.wait(e, "dmw", dmw0)
                e.engine_nop().then_inc(*ctx.bump("dve", 1))
            dve0 = ctx.val("dve")

            @on("scalar")
            def _(e, dve0=dve0):
                ctx.wait(e, "dve", dve0)
                e.activation(out=ang[:, :, :], in_=ang[:, :, :], func=ACT.Exp, scale=1.0).then_inc(*ctx.bump("sce", 1))
            sce_a0 = ctx.val("sce")

            @on("vector")
            def _(e, sce_a0=sce_a0):
                ctx.wait(e, "sce", sce_a0)
                e.tensor_single_scalar(out=ang[:, :, :], in_=ang[:, :, :], scalar=-1.0, op=ALU.mult); ctx.fence_inc(e, "dve")
            dve_ang = ctx.val("dve")

            @on("tensor")
            def _(e, dve_ang=dve_ang):
                ctx.wait(e, "dve", dve_ang)
                n_tot = DCH + DCH + DCH + DCH + HCH
                k = 0
                for c in range(DCH):
                    tpose(e, pps0[:, c * 4:(c + 1) * 4], cw4[:, c * 128:(c + 1) * 128], idn[0:4, 0:4], k == 0, k == n_tot - 1); k += 1
                for c in range(DCH):
                    tpose(e, pps0[:, 64 + c:65 + c], vec3[:, c * 128:(c + 1) * 128], idn[0:1, 0:1], k == 0, k == n_tot - 1); k += 1
                for c in range(DCH):
                    tpose(e, pps0[:, 80 + c:81 + c], vec3[:, DI + c * 128:DI + (c + 1) * 128], idn[0:1, 0:1], k == 0, k == n_tot - 1); k += 1
                for c in range(DCH):
                    tpose(e, pps0[:, 96 + c:97 + c], vec3[:, 2 * DI + c * 128:2 * DI + (c + 1) * 128], idn[0:1, 0:1], k == 0, k == n_tot - 1); k += 1
                for c in range(HCH):
                    mm = tpose(e, pps0[:, 112 + c:113 + c], vec3[:, 3 * DI + c * 128:3 * DI + (c + 1) * 128], idn[0:1, 0:1], k == 0, k == n_tot - 1); k += 1
                ctx.inc(mm, "pe", 1)
            pe_tp = ctx.val("pe")

            @on("scalar")
            def _(e, pe_tp=pe_tp):
                ctx.wait(e, "pe", pe_tp)
                e.copy(out=cwt[:, :, :], in_=pps0[:, 0:64].rearrange("p (c k) -> p c k", k=4))
                e.copy(out=cbt[:, :], in_=pps0[:, 64:80])
                e.copy(out=bdtt[:, :], in_=pps0[:, 80:96])
                e.copy(out=dvt[:, :], in_=pps0[:, 96:112])
                e.copy(out=fcbt[:, :], in_=pps0[:, 112:144]).then_inc(*ctx.bump("sce", 1))

            for (src, dsth, dstl, rows, cols) in ((w_in, wih_s, wil_s, D, 2 * DI),
                                                  (w_out, woh_s, wol_s, DI, D)):
                for j in range(rows // 128):
                    @on("sync")
                    def _(e, src=src, j=j, cols=cols):
                        ctx.wait(e, "dmo", ctx.val("dmo"))
                        e.dma_start(out=pscrA[:, 0:cols], in_=src[j * 128:(j + 1) * 128, :]).then_inc(*ctx.bump("dma", 16))
                    dma_t = ctx.val("dma")

                    @on("scalar")
                    def _(e, dma_t=dma_t, cols=cols):
                        ctx.wait(e, "dma", dma_t)
                        e.copy(out=pscrB[:, 0:cols], in_=pscrA[:, 0:cols]).then_inc(*ctx.bump("sce", 1))
                    sce_t = ctx.val("sce")

                    @on("vector")
                    def _(e, sce_t=sce_t, cols=cols):
                        ctx.wait(e, "sce", sce_t)
                        e.tensor_tensor(out=pscrC[:, 0:cols], in0=pscrA[:, 0:cols], in1=pscrB[:, 0:cols], op=ALU.subtract); ctx.fence_inc(e, "dve")
                    dve_t = ctx.val("dve")

                    @on("sync")
                    def _(e, dve_t=dve_t, dsth=dsth, dstl=dstl, j=j, cols=cols, rows=rows):
                        ctx.wait(e, "dve", dve_t)
                        if rows == D:
                            e.dma_start(out=dsth[:, j, :, :].transpose([1, 0, 2]),
                                        in_=pscrB[:, 0:cols].rearrange("p (cc m) -> p cc m", m=128)).then_inc(*ctx.bump("dmo", 16))
                            e.dma_start(out=dstl[:, j, :, :].transpose([1, 0, 2]),
                                        in_=pscrC[:, 0:cols].rearrange("p (cc m) -> p cc m", m=128)).then_inc(*ctx.bump("dmo", 16))
                        else:
                            e.dma_start(out=dsth[j * 128:(j + 1) * 128, :], in_=pscrB[:, 0:cols]).then_inc(*ctx.bump("dmo", 16))
                            e.dma_start(out=dstl[j * 128:(j + 1) * 128, :], in_=pscrC[:, 0:cols]).then_inc(*ctx.bump("dmo", 16))

            @on("vector")
            def _(e):
                ctx.wait(e, "dve", ctx.val("dve"))
                e.memset(msum[:], 0.0)
                e.engine_nop().then_inc(*ctx.bump("dve", 1))

            for col_ix, (src, nch, cols) in enumerate(((f_w1, HCH, D), (f_w2, DMCH, DH))):
                for j in range(nch):
                    @on("sync")
                    def _(e, src=src, j=j, cols=cols):
                        ctx.wait(e, "dve", ctx.val("dve"))
                        e.dma_start(out=pscrA[:, 0:cols], in_=src[j * 128:(j + 1) * 128, :]).then_inc(*ctx.bump("dma", 16))
                    dma_t = ctx.val("dma")

                    @on("vector")
                    def _(e, dma_t=dma_t, cols=cols, col_ix=col_ix):
                        ctx.wait(e, "dma", dma_t)
                        e.reduce_sum(out=scrs[:, 0:1], in_=pscrA[:, 0:cols], axis=AX.X, apply_absolute_value=True)
                        e.drain()
                        e.tensor_tensor(out=msum[:, col_ix:col_ix + 1], in0=msum[:, col_ix:col_ix + 1],
                                        in1=scrs[:, 0:1], op=ALU.add)
                        e.drain(); ctx.fence_inc(e, "dve")
            dve_ms = ctx.val("dve")

            @on("tensor")
            def _(e, dve_ms=dve_ms):
                ctx.wait(e, "dve", dve_ms)
                e.matmul(pps0[0:1, 0:1], msum[:, 0:1], ones128[:, :], start=True, stop=False)
                mm = e.matmul(pps0[0:1, 1:2], msum[:, 1:2], ones128[:, :], start=False, stop=True)
                ctx.inc(mm, "pe", 1)
            pe_ms = ctx.val("pe")

            @on("vector")
            def _(e, pe_ms=pe_ms):
                ctx.wait(e, "pe", pe_ms)
                e.tensor_copy(smv[:, 0:2], pps0[0:1, 0:2])
                e.drain()
                e.tensor_single_scalar(out=smv[:, 2:3], in_=smv[:, 0:1], scalar=1.0 / (DH * D), op=ALU.mult)
                e.tensor_single_scalar(out=smv[:, 3:4], in_=smv[:, 1:2], scalar=1.0 / (DH * D), op=ALU.mult)
                e.drain()
                e.tensor_single_scalar(out=smv[:, 2:3], in_=smv[:, 2:3], scalar=1e-5, op=ALU.max)
                e.tensor_single_scalar(out=smv[:, 3:4], in_=smv[:, 3:4], scalar=1e-5, op=ALU.max)
                e.drain()
                e.reciprocal(out=smv[:, 4:5], in_=smv[:, 2:3])
                e.reciprocal(out=smv[:, 5:6], in_=smv[:, 3:4])
                e.tensor_single_scalar(out=smv[:, 6:7], in_=smv[:, 2:3], scalar=1.0 / 127.0, op=ALU.mult)
                e.tensor_single_scalar(out=smv[:, 7:8], in_=smv[:, 3:4], scalar=1.0 / 127.0, op=ALU.mult)
                e.drain(); ctx.fence_inc(e, "dve")
            dve_sw = ctx.val("dve")

            @on("tensor")
            def _(e, dve_sw=dve_sw):
                ctx.wait(e, "dve", dve_sw)
                mm = e.matmul(pps0[:, 0:8], onescol[:, :], smv[:, 0:8], start=True, stop=True)
                ctx.inc(mm, "pe", 1)
            pe_bc = ctx.val("pe")

            @on("scalar")
            def _(e, pe_bc=pe_bc):
                ctx.wait(e, "pe", pe_bc)
                e.copy(out=sw1b[:, :], in_=pps0[:, 4:5])
                e.copy(out=sw2b[:, :], in_=pps0[:, 5:6])
                e.copy(out=k1c[:, :], in_=pps0[:, 6:7])
                e.copy(out=k2c[:, :], in_=pps0[:, 7:8]).then_inc(*ctx.bump("sce", 1))
            sce_sw = ctx.val("sce")

            for wi, (src, swb, nch, cols, dst) in enumerate(
                    ((f_w1, sw1b, HCH, D, w1qt_s), (f_w2, sw2b, DMCH, DH, w2qt_s))):
                njt = cols // 128
                for j in range(nch):
                    @on("sync")
                    def _(e, src=src, j=j, cols=cols):
                        ctx.wait(e, "dmo", ctx.val("dmo"))
                        ctx.wait(e, "dve", ctx.val("dve"))
                        e.dma_start(out=pscrA[:, 0:cols], in_=src[j * 128:(j + 1) * 128, :]).then_inc(*ctx.bump("dma", 16))
                    dma_t = ctx.val("dma")

                    @on("vector")
                    def _(e, dma_t=dma_t, cols=cols, swb=swb, first=(wi == 0 and j == 0), sce_sw=sce_sw):
                        ctx.wait(e, "dma", dma_t)
                        if first:
                            ctx.wait(e, "sce", sce_sw)
                        ctx.wait(e, "pe", ctx.val("pe"))
                        e.tensor_scalar(out=pscrB[:, 0:cols], in0=pscrA[:, 0:cols], scalar1=swb[:, :],
                                        scalar2=0.5, op0=ALU.mult, op1=ALU.is_gt)
                        e.tensor_scalar(out=scr1[:, 0:cols] if False else pscrC[:, 0:cols],
                                        in0=pscrA[:, 0:cols], scalar1=swb[:, :],
                                        scalar2=-0.5, op0=ALU.mult, op1=ALU.is_lt)
                        e.tensor_tensor(out=pscrC[:, 0:cols], in0=pscrB[:, 0:cols], in1=pscrC[:, 0:cols],
                                        op=ALU.subtract); ctx.fence_inc(e, "dve")
                    dve_t = ctx.val("dve")

                    nblk = (njt + 3) // 4
                    for blk in range(nblk):
                        n_in = min(4, njt - blk * 4)
                        pb = ppsb0 if blk % 2 == 0 else ppsb1

                        @on("tensor")
                        def _(e, dve_t=dve_t, blk=blk, n_in=n_in, pb=pb):
                            if blk == 0:
                                ctx.wait(e, "dve", dve_t)
                            if blk >= 2:
                                ctx.wait(e, "sce", ctx.val("sce"))
                            for q in range(n_in):
                                jj = blk * 4 + q
                                mm = tpose(e, pb[:, q * 128:(q + 1) * 128], pscrC[:, jj * 128:(jj + 1) * 128],
                                           idnb[:, :], q == 0, q == n_in - 1)
                            ctx.inc(mm, "pe", 1)
                        pe_b = ctx.val("pe")

                        @on("scalar")
                        def _(e, pe_b=pe_b, blk=blk, n_in=n_in, pb=pb):
                            ctx.wait(e, "pe", pe_b)
                            e.copy(out=stag[:, blk * 4:blk * 4 + n_in, :], in_=pb[:, 0:n_in * 128].rearrange("p (q x) -> p q x", x=128)).then_inc(*ctx.bump("sce", 1))
                    sce_st = ctx.val("sce")

                    @on("sync")
                    def _(e, sce_st=sce_st, wi=wi, j=j, njt=njt, dst=dst):
                        ctx.wait(e, "sce", sce_st)
                        if wi == 0:
                            e.dma_start(out=dst[:, :, j * 128:(j + 1) * 128].transpose([1, 0, 2]),
                                        in_=stag[:, 0:njt, :]).then_inc(*ctx.bump("dmo", 16))
                        else:
                            e.dma_start(out=dst[:, j * 128:(j + 1) * 128].rearrange("(a p) m -> a p m", p=128).transpose([1, 0, 2]),
                                        in_=stag[:, 0:njt, :]).then_inc(*ctx.bump("dmo", 16))

            @on("sync")
            def _(e):
                ctx.wait(e, "dmo", ctx.val("dmo"))

        ctx.barrier(on, ("dmo", "dma", "dmw"))
        if DEBUG_SCRATCH:
            @on("sync")
            def _(e):
                ctx.wait(e, "sce", ctx.val("sce"))
                ctx.wait(e, "dve", ctx.val("dve"))
                e.dma_start(out=dbg_c[:, 0:256], in_=ang[:, :, :].rearrange("p c s -> p (c s)")).then_inc(*ctx.bump("dmo", 16))
                e.dma_start(out=dbg_c[:, 256:320], in_=cwt[:, :, :].rearrange("p c k -> p (c k)")).then_inc(*ctx.bump("dmo", 16))
                e.dma_start(out=dbg_c[:, 320:336], in_=cbt[:, :]).then_inc(*ctx.bump("dmo", 16))
                e.dma_start(out=dbg_c[:, 336:352], in_=bdtt[:, :]).then_inc(*ctx.bump("dmo", 16))
                e.dma_start(out=dbg_c[:, 352:368], in_=dvt[:, :]).then_inc(*ctx.bump("dmo", 16))
                e.dma_start(out=dbg_c[:, 368:400], in_=fcbt[:, :]).then_inc(*ctx.bump("dmo", 16))
                e.dma_start(out=dbg_c[:, 400:496], in_=fcwt[:, :, :].rearrange("p c k -> p (c k)")).then_inc(*ctx.bump("dmo", 16))
                e.dma_start(out=dbg_c[:, 512:2048], in_=wxt[:, :, :].rearrange("p c m -> p (c m)")).then_inc(*ctx.bump("dmo", 16))
        with nc.sbuf_tensor("xcv", [128, DCH, 3 + T], F32) as xcv, \
             nc.sbuf_tensor("zsl", [128, DCH, T], F32) as zsl, \
             nc.sbuf_tensor("dta", [128, DCH, T], F32) as dta, \
             nc.sbuf_tensor("bcgB", [128, 2, SGS, T], F32) as bcgB, \
             nc.sbuf_tensor("agrp", [128, SGS, T], F32) as agrp, \
             nc.sbuf_tensor("ugrp", [128, 2, SGS, T], F32) as ugrp, \
             nc.sbuf_tensor("wxzb", [128, 2, 2, 8, 128], BF16) as wxzb:

            for b in range(B):
                @on("vector")
                def _(e):
                    ctx.wait(e, "gp", ctx.val("gp"))
                    e.memset(hstate[:], 0.0)
                    e.memset(xtail[:], 0.0)
                    e.engine_nop().then_inc(*ctx.bump("dve", 1))

                pe_wxz_prev = {}
                pe_wo_prev = {}
                for tile in range(NT):
                    t0 = tile * T
                    ctx.barrier(on, ("dxo", "dyo", "dho"))

                    with nc.sbuf_tensor(f"ln1T_{b}_{tile}", [128, 2, DMCH, T], BF16) as ln1T, \
                         nc.psum_tensor(f"psbH_{b}_{tile}", [128, 512], BF16) as psbH, \
                         nc.psum_tensor(f"psbL_{b}_{tile}", [128, 512], BF16) as psbL:
                        for i in range(NSUB):
                            @on("sync")
                            def _(e, b=b, t0=t0, i=i):
                                ctx.wait(e, "dve", ctx.val("dve"))
                                e.dma_start(out=scr1[:, :], in_=x_in[b, t0 + i * 128:t0 + (i + 1) * 128, :]).then_inc(*ctx.bump("dma", 16))
                            dma_x = ctx.val("dma")

                            @on("vector")
                            def _(e, dma_x=dma_x):
                                ctx.wait(e, "dma", dma_x)
                                e.engine_nop().then_inc(*ctx.bump("dve", 1))

                            sq = ugrp[:].rearrange("p a s t -> p (a s t)")[:, 0:1024]
                            lnf = ugrp[:].rearrange("p a s t -> p (a s t)")[:, 1024:2048]
                            sce_ln = emit_ln(sq, lnf)

                            @on("vector")
                            def _(e, sce_ln=sce_ln, lnf=lnf):
                                ctx.wait(e, "sce", sce_ln)
                                ctx.wait(e, "pe", ctx.val("pe"))
                                e.tensor_copy(scrb[:, 0:1024], lnf)
                                e.tensor_tensor(out=scrb[:, 1024:2048], in0=lnf, in1=scrb[:, 0:1024], op=ALU.subtract); ctx.fence_inc(e, "dve")
                            dve_c = ctx.val("dve")

                            for hl in range(2):
                                for half in range(2):
                                    pb = psbH if half == 0 else psbL

                                    @on("tensor")
                                    def _(e, dve_c=dve_c, hl=hl, half=half, pb=pb):
                                        if hl == 0 and half == 0:
                                            ctx.wait(e, "dve", dve_c)
                                        ctx.wait(e, "sce", ctx.val("sce"))
                                        for q in range(4):
                                            j = half * 4 + q
                                            mm = tpose(e, pb[:, q * 128:(q + 1) * 128],
                                                       scrb[:, hl * 1024 + j * 128:hl * 1024 + (j + 1) * 128],
                                                       idnb[:, :], q == 0, q == 3)
                                        ctx.inc(mm, "pe", 1)
                                    pe_h = ctx.val("pe")

                                    @on("scalar")
                                    def _(e, pe_h=pe_h, hl=hl, half=half, i=i, pb=pb):
                                        ctx.wait(e, "pe", pe_h)
                                        e.copy(out=ln1T[:, hl, half * 4:(half + 1) * 4, i * 128:(i + 1) * 128],
                                               in_=pb[:, :].rearrange("p (q x) -> p q x", x=128)).then_inc(*ctx.bump("sce", 1))

                        sce_ln1T = ctx.val("sce")

                        with nc.psum_tensor(f"psP_{b}_{tile}", [96, T], F32) as psP, \
                             nc.psum_tensor(f"psX0_{b}_{tile}", [128, T], F32) as psX0, \
                             nc.psum_tensor(f"psX1_{b}_{tile}", [128, T], F32) as psX1, \
                             nc.psum_tensor(f"psZ0_{b}_{tile}", [128, T], F32) as psZ0, \
                             nc.psum_tensor(f"psZ1_{b}_{tile}", [128, T], F32) as psZ1:
                            sce_evac_prev = {}
                            for c in range(DCH):
                                par = c % 2

                                wsem = "dwx0" if par == 0 else "dwx1"

                                @on("sync")
                                def _(e, c=c, wsem=wsem):
                                    ctx.wait(e, wsem, ctx.val(wsem))
                                    pv = pe_wxz_prev.get("both")
                                    if pv is not None:
                                        ctx.wait(e, "pe", pv)
                                    for hix in range(2):
                                        cc = c + hix * (DI // 128)
                                        e.dma_start(out=wxzb[:, hix, 0, :, :],
                                                    in_=wih_s[cc, :, :, :].transpose([1, 0, 2])).then_inc(*ctx.bump(wsem, 16))
                                        e.dma_start(out=wxzb[:, hix, 1, :, :],
                                                    in_=wil_s[cc, :, :, :].transpose([1, 0, 2])).then_inc(*ctx.bump(wsem, 16))
                                dmw_c = ctx.val(wsem)

                                for hix in range(2):
                                    ps = (psX0 if par == 0 else psX1) if hix == 0 else (psZ0 if par == 0 else psZ1)

                                    @on("tensor")
                                    def _(e, dmw_c=dmw_c, ps=ps, par=par, hix=hix, sce_ln1T=sce_ln1T, wsem=wsem):
                                        ctx.wait(e, wsem, dmw_c)
                                        ctx.wait(e, "sce", max(sce_ln1T, sce_evac_prev.get(par) or 0))
                                        n = 0
                                        for (wl, al) in ((0, 0), (0, 1), (1, 0)):
                                            for k in range(8):
                                                n += 1
                                                mm = e.matmul(ps[:, :], wxzb[:, hix, wl, k, :], ln1T[:, al, k, :],
                                                              start=(n == 1), stop=(n == 24))
                                        ctx.inc(mm, "pe", 1)
                                pe_wxz_prev["both"] = ctx.val("pe")
                                pe_xz = ctx.val("pe")

                                psx = psX0 if par == 0 else psX1
                                psz = psZ0 if par == 0 else psZ1

                                @on("scalar")
                                def _(e, pe_xz=pe_xz, c=c, psx=psx, psz=psz):
                                    ctx.wait(e, "pe", pe_xz)
                                    ctx.wait(e, "dve", ctx.val("dve"))
                                    e.copy(out=xcv[:, c, 3:3 + T], in_=psx[:, :])
                                    e.activation(out=zsl[:, c, :], in_=psz[:, :], func=ACT.Silu).then_inc(*ctx.bump("sce", 1))
                                sce_ev = ctx.val("sce")
                                sce_evac_prev[par] = sce_ev

                                @on("vector")
                                def _(e, sce_ev=sce_ev, c=c):
                                    ctx.wait(e, "sce", sce_ev)
                                    e.tensor_copy(xcv[:, c, 0:3], xtail[:, c, :])
                                    cacc = scr1[:, 0:T]
                                    z1 = scrs[:, 15:16]
                                    e.scalar_tensor_tensor(out=cacc, in0=xcv[:, c, 0:T], scalar=cwt[:, c, 0:1],
                                                           in1=z1.broadcast_to([128, T]), op0=ALU.mult, op1=ALU.add)
                                    for k in range(1, 4):
                                        e.scalar_tensor_tensor(out=cacc, in0=xcv[:, c, k:k + T], scalar=cwt[:, c, k:k + 1],
                                                               in1=cacc, op0=ALU.mult, op1=ALU.add)
                                    e.tensor_copy(xtail[:, c, :], xcv[:, c, T:T + 3]); ctx.fence_inc(e, "dve")
                                dve_cv = ctx.val("dve")

                                @on("scalar")
                                def _(e, dve_cv=dve_cv, c=c):
                                    ctx.wait(e, "dve", dve_cv)
                                    e.activation(out=xcv[:, c, 3:3 + T], in_=scr1[:, 0:T], func=ACT.Silu,
                                                 bias=cbt[:, c:c + 1]).then_inc(*ctx.bump("sce", 1))
                                sce_sl = ctx.val("sce")

                            sce_all_silu = ctx.val("sce")

                            @on("tensor")
                            def _(e, sce_all_silu=sce_all_silu):
                                ctx.wait(e, "sce", sce_all_silu)
                                for c in range(DCH):
                                    mm = e.matmul(psP[:, :], wxt[:, c, :], xcv[:, c, 3:3 + T],
                                                  start=(c == 0), stop=(c == DCH - 1))
                                ctx.inc(mm, "pe", 1)
                            pe_pj = ctx.val("pe")

                            @on("scalar")
                            def _(e, pe_pj=pe_pj):
                                ctx.wait(e, "pe", pe_pj)
                                e.copy(out=projt[:, :], in_=psP[:, :]).then_inc(*ctx.bump("sce", 1))
                            sce_pj = ctx.val("sce")

                            @on("sync")
                            def _(e, sce_pj=sce_pj, tile=tile):
                                ctx.wait(e, "sce", sce_pj)
                                e.dma_start(out=bc_s[tile % 2, :, :], in_=projt[64:96, :]).then_inc(*ctx.bump("dma", 16))
                            dma_bc = ctx.val("dma")

                            if DEBUG_SCRATCH and b == 0 and tile == 0:
                                @on("sync")
                                def _(e):
                                    e.dma_start(out=dbg_b[:, 0:8192], in_=ln1T[:, :, :, :].rearrange("p a j t -> p (a j t)")).then_inc(*ctx.bump("dmo", 16))
                                    e.dma_start(out=dbg_c[:, 2048:2560], in_=xcv[:, 0, 3:3 + T]).then_inc(*ctx.bump("dmo", 16))
                                    e.dma_start(out=dbg_c[:, 2560:3072], in_=zsl[:, 0, :]).then_inc(*ctx.bump("dmo", 16))
                                    e.dma_start(out=dbg_c[0:96, 3072:3584], in_=projt[:, :]).then_inc(*ctx.bump("dmo", 16))

                    with nc.sbuf_tensor(f"yscan_{b}_{tile}", [128, DCH, T], F32) as yscan, \
                         nc.psum_tensor(f"psD0_{b}_{tile}", [128, T], F32) as psD0, \
                         nc.psum_tensor(f"psD1_{b}_{tile}", [128, T], F32) as psD1:

                        sce_sp_prev = {}
                        for c in range(DCH):
                            par = c % 2
                            psD = psD0 if par == 0 else psD1

                            @on("tensor")
                            def _(e, c=c, psD=psD, spw=sce_sp_prev.get(c % 2)):
                                ctx.wait(e, "sce", spw if spw is not None else ctx.val("sce"))
                                mm = e.matmul(psD[:, :], wdtt[:, c * 128:(c + 1) * 128], projt[0:64, :],
                                              start=True, stop=True)
                                ctx.inc(mm, "pe", 1)
                            pe_dt = ctx.val("pe")

                            @on("scalar")
                            def _(e, pe_dt=pe_dt, c=c, psD=psD):
                                ctx.wait(e, "pe", pe_dt)
                                ctx.wait(e, "dve", ctx.val("dve"))
                                e.activation(out=scr1[:, 0:T], in_=psD[:, :], func=ACT.Relu, bias=bdtt[:, c:c + 1])
                                e.activation(out=scr1[:, 512:1024], in_=psD[:, :], func=ACT.Abs, bias=bdtt[:, c:c + 1])
                                e.activation(out=scr1[:, 512:1024], in_=scr1[:, 512:1024], func=ACT.Exp, scale=-1.0)
                                e.activation(out=scr1[:, 512:1024], in_=scr1[:, 512:1024], func=ACT.Ln, bias=cst[:, 1:2]).then_inc(*ctx.bump("sce", 1))
                            sce_sp = ctx.val("sce")
                            sce_sp_prev[c % 2] = sce_sp

                            @on("vector")
                            def _(e, sce_sp=sce_sp, c=c):
                                ctx.wait(e, "sce", sce_sp)
                                e.tensor_tensor(out=dta[:, c, :], in0=scr1[:, 0:T], in1=scr1[:, 512:1024], op=ALU.add); ctx.fence_inc(e, "dve")

                        for sg in range(SG):
                            @on("sync")
                            def _(e, sg=sg, tile=tile, dma_bc=dma_bc):
                                ctx.wait(e, "dma", dma_bc)
                                if sg > 0:
                                    ctx.wait(e, "gp", ctx.val("gp"))
                                e.dma_start(out=bcgB[:, 0, :, :],
                                            in_=bc_s[tile % 2, sg * SGS:(sg + 1) * SGS, :].partition_broadcast(128)).then_inc(*ctx.bump("dma", 16))
                                e.dma_start(out=bcgB[:, 1, :, :],
                                            in_=bc_s[tile % 2, S + sg * SGS:S + (sg + 1) * SGS, :].partition_broadcast(128)).then_inc(*ctx.bump("dma", 16))
                            dma_g = ctx.val("dma")

                            @on("vector")
                            def _(e):
                                ctx.wait(e, "gp", ctx.val("gp"))
                                e.tensor_tensor(out=scr1[:, 0:T], in0=dta[:, 0, :], in1=xcv[:, 0, 3:3 + T], op=ALU.mult); ctx.fence_inc(e, "dve")
                            dve_dtx = {0: ctx.val("dve")}
                            gp_u = {}
                            gp_cm = {}
                            dve_scan = {}
                            dve_tree = {}

                            for c in range(DCH):
                                par = c % 2

                                @on("gpsimd")
                                def _(e, c=c, par=par, dma_g=dma_g, dtx_c=dve_dtx[c],
                                      tree_war=dve_tree.get(c - 2)):
                                    ctx.wait(e, "dve", dtx_c)
                                    ctx.wait(e, "dma", dma_g)
                                    if tree_war is not None:
                                        ctx.wait(e, "dve", tree_war)
                                    e.tensor_tensor(out=ugrp[:, par, :, :],
                                                    in0=scr1[:, par * 512:par * 512 + T].unsqueeze(1).broadcast_to([128, SGS, T]),
                                                    in1=bcgB[:, 0, :, :], op=ALU.mult); ctx.fence_inc(e, "gp")
                                gp_u[c] = ctx.val("gp")

                                @on("scalar")
                                def _(e, c=c, sg=sg, scan_war=dve_scan.get(c - 1)):
                                    if scan_war is not None:
                                        ctx.wait(e, "dve", scan_war)
                                    else:
                                        ctx.wait(e, "dve", ctx.val("dve"))
                                    for si in range(SGS):
                                        a_ = e.activation(out=agrp[:, si, :], in_=dta[:, c, :], func=ACT.Exp,
                                                          scale=ang[:, c, sg * SGS + si:sg * SGS + si + 1])
                                    ctx.inc(a_, "sce", 1)
                                sce_a = ctx.val("sce")

                                @on("vector")
                                def _(e, c=c, sg=sg, par=par, gp_u_c=gp_u[c], sce_a=sce_a,
                                      cm_prev=gp_cm.get(c - 1)):
                                    ctx.wait(e, "gp", gp_u_c)
                                    ctx.wait(e, "sce", sce_a)
                                    for si in range(SGS):
                                        e.tensor_tensor_scan(out=ugrp[:, par, si, :], data0=agrp[:, si, :],
                                                             data1=ugrp[:, par, si, :],
                                                             initial=hstate[:, c, sg * SGS + si:sg * SGS + si + 1],
                                                             op0=ALU.mult, op1=ALU.add)
                                    e.drain()
                                    e.tensor_copy(hstate[:, c, sg * SGS:(sg + 1) * SGS],
                                                  ugrp[:, par, :, T - 1:T].rearrange("p s o -> p (s o)"))
                                    e.drain(); ctx.fence_inc(e, "dve")
                                dve_scan[c] = ctx.val("dve")

                                if c + 1 < DCH:
                                    @on("vector")
                                    def _(e, c=c):
                                        e.tensor_tensor(out=scr1[:, ((c + 1) % 2) * 512:((c + 1) % 2) * 512 + T],
                                                        in0=dta[:, c + 1, :], in1=xcv[:, c + 1, 3:3 + T], op=ALU.mult); ctx.fence_inc(e, "dve")
                                    dve_dtx[c + 1] = ctx.val("dve")

                                def emit_cmul(cc):
                                    pr = cc % 2

                                    @on("gpsimd")
                                    def _(e, pr=pr, scan_c=dve_scan[cc]):
                                        ctx.wait(e, "dve", scan_c)
                                        e.tensor_tensor(out=ugrp[:, pr, :, :], in0=ugrp[:, pr, :, :],
                                                        in1=bcgB[:, 1, :, :], op=ALU.mult); ctx.fence_inc(e, "gp")
                                    gp_cm[cc] = ctx.val("gp")

                                def emit_tree(cc):
                                    pr = cc % 2

                                    @on("vector")
                                    def _(e, gp_cm_c=gp_cm[cc], cc=cc, sg=sg, pr=pr):
                                        ctx.wait(e, "gp", gp_cm_c)
                                        if sg == 0:
                                            e.tensor_tensor(out=yscan[:, cc, :], in0=ugrp[:, pr, 0, :], in1=ugrp[:, pr, 1, :], op=ALU.add)
                                        else:
                                            e.tensor_tensor(out=yscan[:, cc, :], in0=yscan[:, cc, :], in1=ugrp[:, pr, 0, :], op=ALU.add)
                                            e.tensor_tensor(out=yscan[:, cc, :], in0=yscan[:, cc, :], in1=ugrp[:, pr, 1, :], op=ALU.add)
                                        e.tensor_tensor(out=yscan[:, cc, :], in0=yscan[:, cc, :], in1=ugrp[:, pr, 2, :], op=ALU.add)
                                        e.tensor_tensor(out=yscan[:, cc, :], in0=yscan[:, cc, :], in1=ugrp[:, pr, 3, :], op=ALU.add)
                                        ctx.fence_inc(e, "dve")
                                    dve_tree[cc] = ctx.val("dve")

                                emit_cmul(c)
                                if c >= 1:
                                    emit_tree(c - 1)
                                if c == DCH - 1:
                                    emit_tree(c)

                        for c in range(DCH):
                            @on("vector")
                            def _(e, c=c):
                                f1 = scr1[:, 0:T]
                                f2 = scr1[:, 512:1024]
                                e.scalar_tensor_tensor(out=f1, in0=xcv[:, c, 3:3 + T], scalar=dvt[:, c:c + 1],
                                                       in1=yscan[:, c, :], op0=ALU.mult, op1=ALU.add)
                                e.tensor_tensor(out=f2, in0=f1, in1=zsl[:, c, :], op=ALU.mult)
                                yb = xcv[:, c, :].bitcast(BF16)
                                e.tensor_copy(yb[:, 0:T], f2)
                                e.tensor_tensor(out=yb[:, T:2 * T], in0=f2, in1=yb[:, 0:T], op=ALU.subtract); ctx.fence_inc(e, "dve")
                        dve_fin = ctx.val("dve")

                        if DEBUG_SCRATCH and b == 0 and tile == 0:
                            @on("sync")
                            def _(e, dve_fin=dve_fin):
                                ctx.wait(e, "dve", dve_fin)
                                e.dma_start(out=dbg_c[:, 3584:4096], in_=dta[:, 0, :]).then_inc(*ctx.bump("dmo", 16))
                                e.dma_start(out=dbg_c[:, 1536:2048], in_=yscan[:, 0, :]).then_inc(*ctx.bump("dmo", 16))

                    with nc.sbuf_tensor(f"wob_{b}_{tile}", [128, 2, 2, D], BF16) as wob, \
                         nc.sbuf_tensor(f"obufE_{b}_{tile}", [128, 2, D], F32) as obuf, \
                         nc.psum_tensor(f"psE0_{b}_{tile}", [128, 512], F32) as psE0, \
                         nc.psum_tensor(f"psE1_{b}_{tile}", [128, 512], F32) as psE1, \
                         nc.psum_tensor(f"psE2_{b}_{tile}", [128, 512], F32) as psE2, \
                         nc.psum_tensor(f"psE3_{b}_{tile}", [128, 512], F32) as psE3, \
                         nc.psum_tensor(f"psE4_{b}_{tile}", [128, 512], F32) as psE4, \
                         nc.psum_tensor(f"psE5_{b}_{tile}", [128, 512], F32) as psE5, \
                         nc.psum_tensor(f"psE6_{b}_{tile}", [128, 512], F32) as psE6, \
                         nc.psum_tensor(f"psE7_{b}_{tile}", [128, 512], F32) as psE7:
                        psE = [psE0, psE1, psE2, psE3, psE4, psE5, psE6, psE7]
                        for c in range(DCH):
                            par = c % 2

                            osem = "dwo0" if par == 0 else "dwo1"

                            @on("sync")
                            def _(e, c=c, par=par, dve_fin=dve_fin, osem=osem):
                                ctx.wait(e, osem, ctx.val(osem))
                                if c == 0:
                                    ctx.wait(e, "dve", dve_fin)
                                pv = pe_wo_prev.get(par)
                                if pv is not None:
                                    ctx.wait(e, "pe", pv)
                                e.dma_start(out=wob[:, par, 0, :], in_=woh_s[c * 128:(c + 1) * 128, :]).then_inc(*ctx.bump(osem, 16))
                                e.dma_start(out=wob[:, par, 1, :], in_=wol_s[c * 128:(c + 1) * 128, :]).then_inc(*ctx.bump(osem, 16))
                            dmw_o = ctx.val(osem)

                            @on("tensor")
                            def _(e, c=c, par=par, dmw_o=dmw_o, dve_fin=dve_fin, osem=osem):
                                ctx.wait(e, osem, dmw_o)
                                if c == 0:
                                    ctx.wait(e, "dve", dve_fin)
                                yb = xcv[:, c, :].bitcast(BF16)
                                for mb in range(4):
                                    for (yl, wl) in ((0, 0), (0, 1), (1, 0)):
                                        lhs = yb[:, yl * T + mb * 128:yl * T + (mb + 1) * 128]
                                        for nb in range(2):
                                            mm = e.matmul(psE[mb * 2 + nb][:, :], lhs, wob[:, par, wl, nb * 512:(nb + 1) * 512],
                                                          start=(c == 0 and yl == 0 and wl == 0),
                                                          stop=(c == DCH - 1 and yl == 1))
                                ctx.inc(mm, "pe", 1)
                            pe_wo_prev[par] = ctx.val("pe")
                        pe_wo = ctx.val("pe")

                        dxo_prev = {}
                        for mb in range(4):
                            @on("sync")
                            def _(e, b=b, t0=t0, mb=mb):
                                ctx.wait(e, "dve", ctx.val("dve"))
                                e.dma_start(out=scr1[:, :], in_=x_in[b, t0 + mb * 128:t0 + (mb + 1) * 128, :]).then_inc(*ctx.bump("dma", 16))
                            dma_r = ctx.val("dma")

                            @on("vector")
                            def _(e, pe_wo=pe_wo, dma_r=dma_r, mb=mb, dxo_safe=dxo_prev.get(mb % 2)):
                                ctx.wait(e, "pe", pe_wo)
                                ctx.wait(e, "dma", dma_r)
                                if dxo_safe is not None:
                                    ctx.wait(e, "dxo", dxo_safe)
                                for nb in range(2):
                                    e.tensor_tensor(out=obuf[:, mb % 2, nb * 512:(nb + 1) * 512], in0=psE[mb * 2 + nb][:, :],
                                                    in1=scr1[:, nb * 512:(nb + 1) * 512], op=ALU.add)
                                e.engine_nop().then_inc(*ctx.bump("dve", 1))
                            dve_x2 = ctx.val("dve")

                            @on("sync")
                            def _(e, dve_x2=dve_x2, b=b, t0=t0, mb=mb):
                                ctx.wait(e, "dxo", ctx.val("dxo"))
                                ctx.wait(e, "dve", dve_x2)
                                e.dma_start(out=x2_s[b, t0 + mb * 128:t0 + (mb + 1) * 128, :], in_=obuf[:, mb % 2, :]).then_inc(*ctx.bump("dxo", 16))
                            dxo_prev[mb % 2] = ctx.val("dxo")

        with ExitStack() as ffn_es:
            wqr = ffn_es.enter_context(nc.sbuf_tensor("wqr", [128, DMCH, DH], BF16))
            k1T = ffn_es.enter_context(nc.sbuf_tensor("k1T", [128, DMCH, L], BF16))
            hlin = ffn_es.enter_context(nc.sbuf_tensor("hlin", [128, 2 + L], F32))
            hbuf = ffn_es.enter_context(nc.sbuf_tensor("hbuf", [128, L], F32))
            macc = ffn_es.enter_context(nc.sbuf_tensor("macc", [128, L], F32))
            i1bc = ffn_es.enter_context(nc.sbuf_tensor("i1bc", [128, L], F32))
            s2bc = ffn_es.enter_context(nc.sbuf_tensor("s2bc", [128, L], F32))
            k2b = ffn_es.enter_context(nc.sbuf_tensor("k2b", [128, 2, T], BF16))
            obufF = ffn_es.enter_context(nc.sbuf_tensor("obufF", [128, 2, D], F32))
            w2b = ffn_es.enter_context(nc.sbuf_tensor("w2b", [128, 2, D], BF16))

            for b in range(B):
                ctx.barrier(on, ("dxo", "dyo", "dho", "dma", "dmo"))

                with nc.psum_tensor(f"fpsbH_{b}", [128, 512], BF16) as psbH, \
                     nc.psum_tensor(f"fpsbL_{b}", [128, 512], BF16) as psbL:
                    for tt in range(L // 128):
                        @on("sync")
                        def _(e, b=b, tt=tt):
                            ctx.wait(e, "dve", ctx.val("dve"))
                            ctx.wait(e, "dma", ctx.val("dma"))
                            e.dma_start(out=scr1[:, :], in_=x2_s[b, tt * 128:(tt + 1) * 128, :]).then_inc(*ctx.bump("dma", 16))
                        dma_x = ctx.val("dma")

                        @on("vector")
                        def _(e, dma_x=dma_x):
                            ctx.wait(e, "dma", dma_x)
                            e.engine_nop().then_inc(*ctx.bump("dve", 1))

                        sce_ln = emit_ln(hlin[:, 0:1024], hlin[:, 1024:2048])

                        @on("vector")
                        def _(e, sce_ln=sce_ln, b=b, tt=tt):
                            ctx.wait(e, "sce", sce_ln)
                            ctx.wait(e, "pe", ctx.val("pe"))
                            ln2 = hlin[:, 1024:2048]
                            e.reduce_max(out=scrs[:, 8:9], in_=ln2, axis=AX.X, apply_absolute_value=True)
                            e.drain()
                            e.tensor_single_scalar(out=scrs[:, 8:9], in_=scrs[:, 8:9], scalar=1e-5, op=ALU.max)
                            e.drain()
                            e.reciprocal(out=scrs[:, 9:10], in_=scrs[:, 8:9])
                            e.drain()
                            e.tensor_single_scalar(out=scrs[:, 9:10], in_=scrs[:, 9:10], scalar=127.0, op=ALU.mult)
                            e.drain()
                            e.tensor_scalar(out=hlin[:, 0:1024], in0=ln2, scalar1=scrs[:, 9:10], scalar2=RND,
                                            op0=ALU.mult, op1=ALU.add)
                            e.tensor_single_scalar(out=scrb[:, 0:1024], in_=hlin[:, 0:1024], scalar=RND, op=ALU.subtract)
                            e.tensor_tensor(out=scrs[:, 10:11], in0=scrs[:, 8:9], in1=k1c[:, :], op=ALU.mult)
                            e.drain(); ctx.fence_inc(e, "dve")
                        dve_k = ctx.val("dve")

                        @on("sync")
                        def _(e, dve_k=dve_k, b=b, tt=tt):
                            ctx.wait(e, "dve", dve_k)
                            e.dma_start(out=i1r_s[b, tt * 128:(tt + 1) * 128].unsqueeze(1), in_=scrs[:, 10:11]).then_inc(*ctx.bump("dma", 16))

                        for half in range(2):
                            pb = psbH if half == 0 else psbL

                            @on("tensor")
                            def _(e, dve_k=dve_k, half=half, pb=pb):
                                if half == 0:
                                    ctx.wait(e, "dve", dve_k)
                                ctx.wait(e, "sce", ctx.val("sce"))
                                for q in range(4):
                                    j = half * 4 + q
                                    mm = tpose(e, pb[:, q * 128:(q + 1) * 128], scrb[:, j * 128:(j + 1) * 128],
                                               idnb[:, :], q == 0, q == 3)
                                ctx.inc(mm, "pe", 1)
                            pe_t = ctx.val("pe")

                            @on("scalar")
                            def _(e, pe_t=pe_t, tt=tt, half=half, pb=pb):
                                ctx.wait(e, "pe", pe_t)
                                e.copy(out=k1T[:, half * 4:(half + 1) * 4, tt * 128:(tt + 1) * 128],
                                       in_=pb[:, :].rearrange("p (q x) -> p q x", x=128)).then_inc(*ctx.bump("sce", 1))

                @on("sync")
                def _(e, b=b):
                    ctx.wait(e, "pe", ctx.val("pe"))
                    ctx.wait(e, "dma", ctx.val("dma"))
                    ctx.wait(e, "dmw", ctx.val("dmw"))
                    for j in range(DMCH):
                        e.dma_start(out=wqr[:, j, :], in_=w1qt_s[j, :, :]).then_inc(*ctx.bump("dmw", 16))
                    e.dma_start(out=i1bc[:, :], in_=i1r_s[b, :].unsqueeze(0).partition_broadcast(128).squeeze(1)).then_inc(*ctx.bump("dma", 16))
                dmw_w1 = ctx.val("dmw")
                dma_i1 = ctx.val("dma")

                @on("vector")
                def _(e):
                    ctx.wait(e, "gp", ctx.val("gp"))
                    e.memset(hlin[:, 0:1], 0.0)
                    e.memset(hlin[:, 1 + L:2 + L], 0.0)
                    e.memset(macc[:], 0.0)
                    e.engine_nop().then_inc(*ctx.bump("dve", 1))

                with nc.psum_tensor(f"psH_{b}", [128, 4 * 512], F32) as psH:
                    sce_h_prev = None
                    for hc in range(HCH):
                        @on("tensor")
                        def _(e, hc=hc, dmw_w1=dmw_w1):
                            ctx.wait(e, "dmw", dmw_w1)
                            if hc == 0:
                                ctx.wait(e, "sce", ctx.val("sce"))
                            if hc > 0:
                                ctx.wait(e, "dve", ctx.val("dve"))
                            for nb in range(4):
                                for k in range(8):
                                    mm = e.matmul(psH[:, nb * 512:(nb + 1) * 512], wqr[:, k, hc * 128:(hc + 1) * 128],
                                                  k1T[:, k, nb * 512:(nb + 1) * 512], start=(k == 0), stop=(k == 7))
                            ctx.inc(mm, "pe", 1)
                        pe_h = ctx.val("pe")

                        @on("vector")
                        def _(e, pe_h=pe_h, dma_i1=dma_i1, sce_h_prev=sce_h_prev):
                            ctx.wait(e, "pe", pe_h)
                            ctx.wait(e, "dma", dma_i1)
                            ctx.wait(e, "dho", ctx.val("dho"))
                            if sce_h_prev is not None:
                                ctx.wait(e, "sce", sce_h_prev)
                            for nb in range(4):
                                e.tensor_tensor(out=hlin[:, 1 + nb * 512:1 + (nb + 1) * 512],
                                                in0=psH[:, nb * 512:(nb + 1) * 512],
                                                in1=i1bc[:, nb * 512:(nb + 1) * 512], op=ALU.mult)
                            z1 = scrs[:, 15:16]
                            e.scalar_tensor_tensor(out=hbuf[:, :], in0=hlin[:, 0:L], scalar=fcwt[:, hc, 0:1],
                                                   in1=z1.broadcast_to([128, L]), op0=ALU.mult, op1=ALU.add)
                            e.scalar_tensor_tensor(out=hbuf[:, :], in0=hlin[:, 1:1 + L], scalar=fcwt[:, hc, 1:2],
                                                   in1=hbuf[:, :], op0=ALU.mult, op1=ALU.add)
                            e.scalar_tensor_tensor(out=hbuf[:, :], in0=hlin[:, 2:2 + L], scalar=fcwt[:, hc, 2:3],
                                                   in1=hbuf[:, :], op0=ALU.mult, op1=ALU.add); ctx.fence_inc(e, "dve")
                        dve_cv = ctx.val("dve")

                        @on("scalar")
                        def _(e, dve_cv=dve_cv, hc=hc):
                            ctx.wait(e, "dve", dve_cv)
                            e.activation(out=hbuf[:, :], in_=hbuf[:, :], func=ACT.Silu, bias=fcbt[:, hc:hc + 1])
                            e.activation(out=hlin[:, 1:1 + L], in_=hbuf[:, :], func=ACT.Abs).then_inc(*ctx.bump("sce", 1))
                        sce_h = ctx.val("sce")
                        sce_h_prev = sce_h

                        @on("vector")
                        def _(e, sce_h=sce_h):
                            ctx.wait(e, "sce", sce_h)
                            e.tensor_tensor(out=macc[:, :], in0=macc[:, :], in1=hlin[:, 1:1 + L], op=ALU.max); ctx.fence_inc(e, "dve")

                        @on("sync")
                        def _(e, sce_h=sce_h, hc=hc):
                            ctx.wait(e, "dho", ctx.val("dho"))
                            ctx.wait(e, "sce", sce_h)
                            e.dma_start(out=h_scr[hc * 128:(hc + 1) * 128, :], in_=hbuf[:, :]).then_inc(*ctx.bump("dho", 16))

                with nc.psum_tensor(f"psM_{b}", [128, 512], F32) as psM:
                    dve_red = ctx.val("dve")
                    for blk in range(4):
                        @on("tensor")
                        def _(e, blk=blk, dve_red=dve_red):
                            if blk == 0:
                                ctx.wait(e, "dve", ctx.val("dve"))
                            else:
                                ctx.wait(e, "dve", ctx.val("dve"))
                            for q in range(4):
                                it = blk * 4 + q
                                mm = tpose(e, psM[:, q * 128:(q + 1) * 128], macc[:, it * 128:(it + 1) * 128],
                                           idn[:, :], q == 0, q == 3)
                            ctx.inc(mm, "pe", 1)
                        pe_m = ctx.val("pe")

                        @on("vector")
                        def _(e, pe_m=pe_m, blk=blk):
                            ctx.wait(e, "pe", pe_m)
                            for q in range(4):
                                it = blk * 4 + q
                                e.reduce_max(out=inv2t[:, it:it + 1], in_=psM[:, q * 128:(q + 1) * 128], axis=AX.X)
                            e.drain()
                            e.engine_nop().then_inc(*ctx.bump("dve", 1))

                    @on("vector")
                    def _(e, b=b):
                        e.drain()
                        e.tensor_single_scalar(out=inv2t[:, :], in_=inv2t[:, :], scalar=1e-5, op=ALU.max)
                        e.drain()
                        e.reciprocal(out=scr1[:, 0:16], in_=inv2t[:, :])
                        e.drain()
                        e.tensor_single_scalar(out=scr1[:, 0:16], in_=scr1[:, 0:16], scalar=127.0, op=ALU.mult)
                        e.tensor_tensor(out=inv2t[:, :], in0=inv2t[:, :],
                                        in1=k2c[:, :].broadcast_to([128, 16]), op=ALU.mult)
                        e.drain(); ctx.fence_inc(e, "dve")
                    dve_s2 = ctx.val("dve")

                    @on("tensor")
                    def _(e, dve_s2=dve_s2):
                        ctx.wait(e, "dve", dve_s2)
                        mm = tpose(e, psM[0:16, 0:128], scr1[:, 0:16], idn[:, :], True, True)
                        ctx.inc(mm, "pe", 1)
                    pe_s2 = ctx.val("pe")

                    @on("scalar")
                    def _(e, pe_s2=pe_s2):
                        ctx.wait(e, "pe", pe_s2)
                        e.copy(out=scr1[0:16, 512:640], in_=psM[0:16, 0:128]).then_inc(*ctx.bump("sce", 1))
                    sce_s2 = ctx.val("sce")

                    @on("sync")
                    def _(e, sce_s2=sce_s2, b=b):
                        ctx.wait(e, "sce", sce_s2)
                        e.dma_start(out=s2r_s[b, :].rearrange("(it x) -> it x", x=128), in_=scr1[0:16, 512:640]).then_inc(*ctx.bump("dma", 16))
                    dma_s2w = ctx.val("dma")

                    @on("sync")
                    def _(e, dma_s2w=dma_s2w, b=b):
                        ctx.wait(e, "dma", dma_s2w)
                        e.dma_start(out=s2bc[:, :], in_=s2r_s[b, :].unsqueeze(0).partition_broadcast(128).squeeze(1)).then_inc(*ctx.bump("dma", 16))
                    dma_s2b = ctx.val("dma")

                with nc.psum_tensor(f"psO0_{b}", [128, 512], F32) as psO0, \
                     nc.psum_tensor(f"psO1_{b}", [128, 512], F32) as psO1, \
                     nc.psum_tensor(f"psO2_{b}", [128, 512], F32) as psO2, \
                     nc.psum_tensor(f"psO3_{b}", [128, 512], F32) as psO3, \
                     nc.psum_tensor(f"psO4_{b}", [128, 512], F32) as psO4, \
                     nc.psum_tensor(f"psO5_{b}", [128, 512], F32) as psO5, \
                     nc.psum_tensor(f"psO6_{b}", [128, 512], F32) as psO6, \
                     nc.psum_tensor(f"psO7_{b}", [128, 512], F32) as psO7:
                    psO = [psO0, psO1, psO2, psO3, psO4, psO5, psO6, psO7]
                    dve_evac_last = None
                    for tile in range(NT):
                        t0 = tile * T
                        pe_mm_prev = {}
                        for hc in range(HCH):
                            par = hc % 2

                            hsem = "dh0" if par == 0 else "dh1"
                            w2sem = "dw20" if par == 0 else "dw21"

                            @on("sync")
                            def _(e, hc=hc, par=par, t0=t0, dma_s2b=dma_s2b, hsem=hsem, w2sem=w2sem):
                                ctx.wait(e, hsem, ctx.val(hsem))
                                ctx.wait(e, w2sem, ctx.val(w2sem))
                                ctx.wait(e, "dho", ctx.val("dho"))
                                ctx.wait(e, "dma", dma_s2b)
                                pv = pe_mm_prev.get(par)
                                if pv is not None:
                                    ctx.wait(e, "pe", pv)
                                ctx.wait(e, "dve", ctx.val("dve"))
                                e.dma_start(out=hbuf[:, par * T:(par + 1) * T],
                                            in_=h_scr[hc * 128:(hc + 1) * 128, t0:t0 + T]).then_inc(*ctx.bump(hsem, 16))
                                e.dma_start(out=w2b[:, par, :], in_=w2qt_s[hc * 128:(hc + 1) * 128, :]).then_inc(*ctx.bump(w2sem, 16))
                            dma_h = ctx.val(hsem)
                            dmw_2 = ctx.val(w2sem)

                            @on("vector")
                            def _(e, dma_h=dma_h, par=par, t0=t0, pv=pe_mm_prev.get(par), hsem=hsem):
                                ctx.wait(e, hsem, dma_h)
                                if pv is not None:
                                    ctx.wait(e, "pe", pv)
                                e.tensor_tensor(out=scr1[:, 0:T], in0=hbuf[:, par * T:(par + 1) * T],
                                                in1=s2bc[:, t0:t0 + T], op=ALU.mult)
                                e.tensor_scalar(out=k2b[:, par, :], in0=scr1[:, 0:T], scalar1=RND, scalar2=RND,
                                                op0=ALU.add, op1=ALU.subtract); ctx.fence_inc(e, "dve")
                            dve_k2 = ctx.val("dve")

                            @on("tensor")
                            def _(e, dve_k2=dve_k2, dmw_2=dmw_2, hc=hc, par=par, tile=tile, dve_evac_last=dve_evac_last, w2sem=w2sem):
                                ctx.wait(e, "dve", dve_k2)
                                ctx.wait(e, w2sem, dmw_2)
                                if hc == 0 and tile > 0 and dve_evac_last is not None:
                                    ctx.wait(e, "dve", dve_evac_last)
                                for mb in range(4):
                                    for nb in range(2):
                                        mm = e.matmul(psO[mb * 2 + nb][:, :], k2b[:, par, mb * 128:(mb + 1) * 128],
                                                      w2b[:, par, nb * 512:(nb + 1) * 512],
                                                      start=(hc == 0), stop=(hc == HCH - 1))
                                ctx.inc(mm, "pe", 1)
                            pe_mm_prev[par] = ctx.val("pe")
                        pe_o = ctx.val("pe")

                        dyo_prev = {}
                        for mb in range(4):
                            @on("sync")
                            def _(e, b=b, t0=t0, mb=mb):
                                ctx.wait(e, "dve", ctx.val("dve"))
                                e.dma_start(out=scr1[:, :], in_=x2_s[b, t0 + mb * 128:t0 + (mb + 1) * 128, :]).then_inc(*ctx.bump("dma", 16))
                            dma_r = ctx.val("dma")

                            @on("vector")
                            def _(e, pe_o=pe_o, dma_r=dma_r, mb=mb, tile=tile, dyo_safe=dyo_prev.get(mb % 2)):
                                ctx.wait(e, "pe", pe_o)
                                ctx.wait(e, "dma", dma_r)
                                if dyo_safe is not None:
                                    ctx.wait(e, "dyo", dyo_safe)
                                for nb in range(2):
                                    e.scalar_tensor_tensor(out=obufF[:, mb % 2, nb * 512:(nb + 1) * 512], in0=psO[mb * 2 + nb][:, :],
                                                           scalar=inv2t[:, tile * 4 + mb:tile * 4 + mb + 1],
                                                           in1=scr1[:, nb * 512:(nb + 1) * 512], op0=ALU.mult, op1=ALU.add)
                                e.engine_nop().then_inc(*ctx.bump("dve", 1))
                            dve_o = ctx.val("dve")
                            dve_evac_last = dve_o

                            @on("sync")
                            def _(e, dve_o=dve_o, b=b, t0=t0, mb=mb):
                                ctx.wait(e, "dyo", ctx.val("dyo"))
                                ctx.wait(e, "dve", dve_o)
                                e.dma_start(out=y_out[b, t0 + mb * 128:t0 + (mb + 1) * 128, :], in_=obufF[:, mb % 2, :]).then_inc(*ctx.bump("dyo", 16))
                            dyo_prev[mb % 2] = ctx.val("dyo")

        @on("sync")
        def _(e):
            for s_ in ("dyo", "dxo", "dho", "dmo", "dma", "dmw",
                       "dwx0", "dwx1", "dwo0", "dwo1", "dh0", "dh1", "dw20", "dw21"):
                ctx.wait(e, s_, ctx.val(s_))

    return nc


def kernel(L0_attn_hidden_states, L0_attn_W_in, L0_attn_conv_w, L0_attn_conv_b,
           L0_attn_W_x, L0_attn_W_dt, L0_attn_b_dt, L0_attn_A_log, L0_attn_D,
           L0_attn_W_out, L0_ffn_w1, L0_ffn_conv_w, L0_ffn_conv_b, L0_ffn_w2):
    x = np.ascontiguousarray(np.asarray(L0_attn_hidden_states, dtype=np.float32))
    weights = dict(
        w_in=np.ascontiguousarray(np.asarray(L0_attn_W_in, dtype=np.float32)),
        conv_w=np.ascontiguousarray(np.asarray(L0_attn_conv_w, dtype=np.float32)),
        conv_b=np.ascontiguousarray(np.asarray(L0_attn_conv_b, dtype=np.float32)),
        w_x=np.ascontiguousarray(np.asarray(L0_attn_W_x, dtype=np.float32)),
        w_dt=np.ascontiguousarray(np.asarray(L0_attn_W_dt, dtype=np.float32)),
        b_dt=np.ascontiguousarray(np.asarray(L0_attn_b_dt, dtype=np.float32)),
        a_log=np.ascontiguousarray(np.asarray(L0_attn_A_log, dtype=np.float32)),
        d_vec=np.ascontiguousarray(np.asarray(L0_attn_D, dtype=np.float32)),
        w_out=np.ascontiguousarray(np.asarray(L0_attn_W_out, dtype=np.float32)),
        f_w1=np.ascontiguousarray(np.asarray(L0_ffn_w1, dtype=np.float32)),
        f_cw=np.ascontiguousarray(np.asarray(L0_ffn_conv_w, dtype=np.float32).reshape(DH, 3)),
        f_cb=np.ascontiguousarray(np.asarray(L0_ffn_conv_b, dtype=np.float32)),
        f_w2=np.ascontiguousarray(np.asarray(L0_ffn_w2, dtype=np.float32)),
    )
    nc = build()
    in_maps = []
    for i in range(8):
        m = dict(weights)
        m["x_in"] = x[2 * i:2 * i + 2]
        in_maps.append(m)
    res = run_bass_kernel_spmd(nc, in_maps, core_ids=list(range(8)))
    return np.concatenate([r["y_out"] for r in res.results], axis=0)


# version 07
# speedup vs baseline: 1.0658x; 1.0658x over previous
import numpy as np
from contextlib import ExitStack

import concourse.bass as bass
import concourse.mybir as mybir
from concourse.bass_utils import run_bass_kernel_spmd

F32 = mybir.dt.float32
BF16 = mybir.dt.bfloat16
ALU = mybir.AluOpType
ACT = mybir.ActivationFunctionType
AX = mybir.AxisListType

B = 2
L = 2048
D = 1024
DI = 2048
S = 16
R = 64
DH = 4096
T = 512
NT = L // T
DCH = DI // 128
DMCH = D // 128
HCH = DH // 128
NSUB = T // 128
SG = 4
SGS = S // SG
RND = 12582912.0
EPS = 1e-5

DEV_DRAINS = False
DEBUG_SCRATCH = False

_DRAIN_OPS = frozenset((
    "tensor_tensor", "tensor_copy", "tensor_scalar", "tensor_single_scalar",
    "scalar_tensor_tensor", "tensor_tensor_scan", "tensor_reduce",
    "reduce_sum", "reduce_max", "memset", "activation", "copy",
    "reciprocal", "iota", "tensor_relu", "tensor_add", "tensor_mul",
    "tensor_sub", "tensor_max", "tensor_scalar_max",
))


class _DrainProxy:
    def __init__(self, eng):
        self._e = eng

    def __getattr__(self, name):
        f = getattr(self._e, name)
        if name in _DRAIN_OPS:
            def g(*a, **k):
                r = f(*a, **k)
                self._e.drain()
                return r
            return g
        return f


class Ctx:
    def __init__(self, nc, block, es):
        self.nc = nc
        self.block = block
        self.es = es
        self.sems = {}

    def mksem(self, *names):
        for n in names:
            self.sems[n] = [self.es.enter_context(self.nc.semaphore(n)), 0]

    def sem(self, name):
        return self.sems[name][0]

    def fence_inc(self, e, name, n=1):
        s = self.sems[name]
        s[1] += n
        e.engine_nop().then_inc(s[0], n)

    def bump(self, name, n=1):
        s = self.sems[name]
        s[1] += n
        return (s[0], n)

    def inc(self, instr, name, n=1):
        s = self.sems[name]
        s[1] += n
        instr.then_inc(s[0], n)
        return s[1]

    def val(self, name):
        return self.sems[name][1]

    def barrier(self, on, store_sems):
        @on("sync")
        def _(e):
            for s_ in store_sems:
                self.wait(e, s_, self.val(s_))
            self.sems["tileb"][1] += 1
            e.sem_inc(self.sems["tileb"][0], 1)
        tb = self.val("tileb")
        for eng in ("vector", "scalar", "tensor", "gpsimd"):
            @on(eng)
            def _(e, tb=tb):
                self.wait(e, "tileb", tb)

    def wait(self, eng, name, v=None):
        s = self.sems[name]
        eng.wait_ge(s[0], s[1] if v is None else v)


def build():
    nc = bass.Bass("TRN2", target_bir_lowering=False,
                   detect_race_conditions=DEV_DRAINS)

    _scr_kind2 = "ExternalOutput" if DEBUG_SCRATCH else "Internal"
    x_in = nc.dram_tensor("x_in", [B, L, D], F32, kind="ExternalInput").ap()
    w_in = nc.dram_tensor("w_in", [D, 2 * DI], F32, kind="ExternalInput").ap()
    conv_w = nc.dram_tensor("conv_w", [4, DI], F32, kind="ExternalInput").ap()
    conv_b = nc.dram_tensor("conv_b", [DI], F32, kind="ExternalInput").ap()
    w_x = nc.dram_tensor("w_x", [DI, R + 2 * S], F32, kind="ExternalInput").ap()
    w_dt = nc.dram_tensor("w_dt", [R, DI], F32, kind="ExternalInput").ap()
    b_dt = nc.dram_tensor("b_dt", [DI], F32, kind="ExternalInput").ap()
    a_log = nc.dram_tensor("a_log", [DI, S], F32, kind="ExternalInput").ap()
    d_vec = nc.dram_tensor("d_vec", [DI], F32, kind="ExternalInput").ap()
    w_out = nc.dram_tensor("w_out", [DI, D], F32, kind="ExternalInput").ap()
    f_w1 = nc.dram_tensor("f_w1", [DH, D], F32, kind="ExternalInput").ap()
    f_cw = nc.dram_tensor("f_cw", [DH, 3], F32, kind="ExternalInput").ap()
    f_cb = nc.dram_tensor("f_cb", [DH], F32, kind="ExternalInput").ap()
    f_w2 = nc.dram_tensor("f_w2", [D, DH], F32, kind="ExternalInput").ap()
    y_out = nc.dram_tensor("y_out", [B, L, D], F32, kind="ExternalOutput").ap()

    _scr_kind0 = "ExternalOutput" if DEBUG_SCRATCH else "Internal"
    wih_s = nc.dram_tensor("wih_s", [2 * DI // 128, 8, 128, 128], BF16, kind=_scr_kind0).ap()
    wil_s = nc.dram_tensor("wil_s", [2 * DI // 128, 8, 128, 128], BF16, kind=_scr_kind0).ap()
    woh_s = nc.dram_tensor("woh_s", [DI, D], BF16, kind="Internal").ap()
    wol_s = nc.dram_tensor("wol_s", [DI, D], BF16, kind="Internal").ap()
    w1qt_s = nc.dram_tensor("w1qt_s", [DMCH, 128, DH], BF16, kind=_scr_kind2).ap()
    w2qt_s = nc.dram_tensor("w2qt_s", [DH, D], BF16, kind=_scr_kind2).ap()
    bc_s = nc.dram_tensor("bc_s", [2, 2 * S, T], F32, kind="Internal").ap()
    _scr_kind = "ExternalOutput" if DEBUG_SCRATCH else "Internal"
    x2_s = nc.dram_tensor("x2_s", [B, L, D], F32, kind=_scr_kind).ap()
    h_scr = nc.dram_tensor("h_scr", [DH, L], F32, kind=_scr_kind).ap()
    dbg_c = nc.dram_tensor("dbg_c", [128, 4096], F32, kind=_scr_kind).ap() if DEBUG_SCRATCH else None
    dbg_b = nc.dram_tensor("dbg_b", [128, 8192], BF16, kind=_scr_kind).ap() if DEBUG_SCRATCH else None
    s2r_s = nc.dram_tensor("s2r_s", [B, L], F32, kind="Internal").ap()
    i1r_s = nc.dram_tensor("i1r_s", [B, L], F32, kind="Internal").ap()

    es = ExitStack()
    with es:
        def sb(name, shape, dt=F32):
            return es.enter_context(nc.sbuf_tensor(name, shape, dt))

        wxt = sb("wxt", [128, DCH, 96])
        wdtt = sb("wdtt", [64, DI])
        cwt = sb("cwt", [128, DCH, 4])
        cbt = sb("cbt", [128, DCH])
        bdtt = sb("bdtt", [128, DCH])
        dvt = sb("dvt", [128, DCH])
        ang = sb("ang", [128, DCH, S])
        fcwt = sb("fcwt", [128, HCH, 3])
        fcbt = sb("fcbt", [128, HCH])
        idn = sb("idn", [128, 128])
        idnb = sb("idnb", [128, 128], BF16)
        onescol = sb("onescol", [1, 128])
        ones128 = sb("ones128", [128, 1])
        k1c = sb("k1c", [128, 1])
        k2c = sb("k2c", [128, 1])
        sw1b = sb("sw1b", [128, 1])
        sw2b = sb("sw2b", [128, 1])
        hstate = sb("hstate", [128, DCH, S])
        xtail = sb("xtail", [128, DCH, 3])
        scr1 = sb("scr1", [128, 1024])
        scrb = sb("scrb", [128, 2048], BF16)
        scrs = sb("scrs", [128, 16])
        cst = sb("cst", [128, 2])
        projt = sb("projt", [96, T])
        inv2t = sb("inv2t", [128, 16])

        block = es.enter_context(nc.Block())
        ctx = Ctx(nc, block, es)
        ctx.mksem("dmw", "dma", "dmo", "pe", "dve", "sce", "gp",
                  "dwx0", "dwx1", "dwo0", "dwo1", "dh0", "dh1", "dw20", "dw21",
                  "dho", "dxo", "dyo", "tileb")

        def on(name):
            dec = getattr(block, name)
            if not DEV_DRAINS or name in ("sync", "tensor"):
                return dec

            def wrap(fn):
                return dec(lambda e: fn(_DrainProxy(e)))
            return wrap

        SET_LNEXP = 6
        SET_SILU = 18

        def act_set(e, set_id):
            e.drain()
            e.add_instruction(mybir.InstLoadActFuncSet(
                name=nc.get_next_instruction_name(), act_func_set_id=set_id, ins=[], outs=[]))
            e.drain()

        def tpose(e, out, in_, ident, start, stop):
            return e.matmul(out, in_, ident, is_transpose=True, start=start, stop=stop)

        def emit_ln(sqbuf, lnfbuf):
            @on("vector")
            def _(e):
                e.reduce_sum(out=scrs[:, 0:1], in_=scr1[:, :], axis=AX.X); ctx.fence_inc(e, "dve")
            dve_s = ctx.val("dve")

            @on("scalar")
            def _(e, dve_s=dve_s):
                ctx.wait(e, "dve", dve_s)
                e.activation(out=sqbuf, in_=scr1[:, :], func=ACT.Square).then_inc(*ctx.bump("sce", 1))
            sce_sq = ctx.val("sce")

            @on("vector")
            def _(e, sce_sq=sce_sq):
                ctx.wait(e, "sce", sce_sq)
                e.reduce_sum(out=scrs[:, 1:2], in_=sqbuf, axis=AX.X)
                e.drain()
                e.tensor_single_scalar(out=scrs[:, 2:3], in_=scrs[:, 0:1], scalar=1.0 / D, op=ALU.mult)
                e.tensor_single_scalar(out=scrs[:, 3:4], in_=scrs[:, 1:2], scalar=1.0 / D, op=ALU.mult)
                e.drain()
                e.tensor_tensor(out=scrs[:, 4:5], in0=scrs[:, 2:3], in1=scrs[:, 2:3], op=ALU.mult)
                e.drain()
                e.tensor_tensor(out=scrs[:, 4:5], in0=scrs[:, 3:4], in1=scrs[:, 4:5], op=ALU.subtract)
                e.drain(); ctx.fence_inc(e, "dve")
            dve_v = ctx.val("dve")

            @on("scalar")
            def _(e, dve_v=dve_v):
                ctx.wait(e, "dve", dve_v)
                e.activation(out=scrs[:, 5:6], in_=scrs[:, 4:5], func=ACT.Ln, bias=cst[:, 0:1])
                e.drain()
                e.activation(out=scrs[:, 6:7], in_=scrs[:, 5:6], func=ACT.Exp, scale=-0.5)
                e.drain()
                e.activation(out=scrs[:, 6:7], in_=scrs[:, 6:7], func=ACT.Identity).then_inc(*ctx.bump("sce", 1))
            sce_r = ctx.val("sce")

            @on("vector")
            def _(e, sce_r=sce_r):
                ctx.wait(e, "sce", sce_r)
                e.tensor_tensor(out=scrs[:, 7:8], in0=scrs[:, 2:3], in1=scrs[:, 6:7], op=ALU.mult)
                e.drain()
                e.tensor_single_scalar(out=scrs[:, 7:8], in_=scrs[:, 7:8], scalar=-1.0, op=ALU.mult)
                e.drain(); ctx.fence_inc(e, "dve")
            dve_mb = ctx.val("dve")

            @on("scalar")
            def _(e, dve_mb=dve_mb):
                ctx.wait(e, "dve", dve_mb)
                e.activation(out=lnfbuf, in_=scr1[:, :], func=ACT.Identity,
                             scale=scrs[:, 6:7], bias=scrs[:, 7:8]).then_inc(*ctx.bump("sce", 1))
            return ctx.val("sce")

        with nc.sbuf_tensor("pscrA", [128, DH], F32) as pscrA, \
             nc.sbuf_tensor("pscrB", [128, DH], BF16) as pscrB, \
             nc.sbuf_tensor("pscrC", [128, DH], BF16) as pscrC, \
             nc.sbuf_tensor("stag", [128, HCH, 128], BF16) as stag, \
             nc.sbuf_tensor("cw4", [4, DI], F32) as cw4, \
             nc.sbuf_tensor("vec3", [1, 3 * DI + DH], F32) as vec3, \
             nc.sbuf_tensor("msum", [128, 2], F32) as msum, \
             nc.sbuf_tensor("smv", [1, 8], F32) as smv, \
             nc.psum_tensor("pps0", [128, 512], F32) as pps0, \
             nc.psum_tensor("ppsb0", [128, 512], BF16) as ppsb0, \
             nc.psum_tensor("ppsb1", [128, 512], BF16) as ppsb1:

            @on("sync")
            def _(e):
                e.dma_start(out=wxt[:, :, :], in_=w_x.rearrange("(c p) m -> p c m", p=128)).then_inc(*ctx.bump("dmw", 16))
                e.dma_start(out=wdtt[:, :], in_=w_dt).then_inc(*ctx.bump("dmw", 16))
                e.dma_start(out=ang[:, :, :], in_=a_log.rearrange("(c p) s -> p c s", p=128)).then_inc(*ctx.bump("dmw", 16))
                e.dma_start(out=fcwt[:, :, :], in_=f_cw.rearrange("(c p) k -> p c k", p=128)).then_inc(*ctx.bump("dmw", 16))
                e.dma_start(out=cw4[:, :], in_=conv_w).then_inc(*ctx.bump("dmw", 16))
                e.dma_start(out=vec3[:, 0:DI], in_=conv_b.unsqueeze(0)).then_inc(*ctx.bump("dmw", 16))
                e.dma_start(out=vec3[:, DI:2 * DI], in_=b_dt.unsqueeze(0)).then_inc(*ctx.bump("dmw", 16))
                e.dma_start(out=vec3[:, 2 * DI:3 * DI], in_=d_vec.unsqueeze(0)).then_inc(*ctx.bump("dmw", 16))
                e.dma_start(out=vec3[:, 3 * DI:3 * DI + DH], in_=f_cb.unsqueeze(0)).then_inc(*ctx.bump("dmw", 16))
            dmw0 = ctx.val("dmw")

            @on("gpsimd")
            def _(e):
                e.iota(out=scr1[:, 0:128], pattern=[[-1, 128]], base=0,
                       channel_multiplier=1, allow_small_or_imprecise_dtypes=True); ctx.fence_inc(e, "gp")
            gp_i = ctx.val("gp")

            @on("vector")
            def _(e, gp_i=gp_i, dmw0=dmw0):
                ctx.wait(e, "gp", gp_i)
                e.tensor_single_scalar(out=idn[:], in_=scr1[:, 0:128], scalar=0.0, op=ALU.is_equal)
                e.tensor_copy(idnb[:], idn[:])
                e.memset(onescol[:], 1.0)
                e.memset(ones128[:], 1.0)
                e.memset(scrs[:, 15:16], 0.0)
                e.memset(cst[:, 0:1], EPS)
                e.memset(cst[:, 1:2], 1.0)
                ctx.wait(e, "dmw", dmw0)
                e.engine_nop().then_inc(*ctx.bump("dve", 1))
            dve0 = ctx.val("dve")

            @on("scalar")
            def _(e, dve0=dve0):
                ctx.wait(e, "dve", dve0)
                e.activation(out=ang[:, :, :], in_=ang[:, :, :], func=ACT.Exp, scale=1.0).then_inc(*ctx.bump("sce", 1))
            sce_a0 = ctx.val("sce")

            @on("vector")
            def _(e, sce_a0=sce_a0):
                ctx.wait(e, "sce", sce_a0)
                e.tensor_single_scalar(out=ang[:, :, :], in_=ang[:, :, :], scalar=-1.0, op=ALU.mult); ctx.fence_inc(e, "dve")
            dve_ang = ctx.val("dve")

            @on("tensor")
            def _(e, dve_ang=dve_ang):
                ctx.wait(e, "dve", dve_ang)
                n_tot = DCH + DCH + DCH + DCH + HCH
                k = 0
                for c in range(DCH):
                    tpose(e, pps0[:, c * 4:(c + 1) * 4], cw4[:, c * 128:(c + 1) * 128], idn[0:4, 0:4], k == 0, k == n_tot - 1); k += 1
                for c in range(DCH):
                    tpose(e, pps0[:, 64 + c:65 + c], vec3[:, c * 128:(c + 1) * 128], idn[0:1, 0:1], k == 0, k == n_tot - 1); k += 1
                for c in range(DCH):
                    tpose(e, pps0[:, 80 + c:81 + c], vec3[:, DI + c * 128:DI + (c + 1) * 128], idn[0:1, 0:1], k == 0, k == n_tot - 1); k += 1
                for c in range(DCH):
                    tpose(e, pps0[:, 96 + c:97 + c], vec3[:, 2 * DI + c * 128:2 * DI + (c + 1) * 128], idn[0:1, 0:1], k == 0, k == n_tot - 1); k += 1
                for c in range(HCH):
                    mm = tpose(e, pps0[:, 112 + c:113 + c], vec3[:, 3 * DI + c * 128:3 * DI + (c + 1) * 128], idn[0:1, 0:1], k == 0, k == n_tot - 1); k += 1
                ctx.inc(mm, "pe", 1)
            pe_tp = ctx.val("pe")

            @on("scalar")
            def _(e, pe_tp=pe_tp):
                ctx.wait(e, "pe", pe_tp)
                e.copy(out=cwt[:, :, :], in_=pps0[:, 0:64].rearrange("p (c k) -> p c k", k=4))
                e.copy(out=cbt[:, :], in_=pps0[:, 64:80])
                e.copy(out=bdtt[:, :], in_=pps0[:, 80:96])
                e.copy(out=dvt[:, :], in_=pps0[:, 96:112])
                e.copy(out=fcbt[:, :], in_=pps0[:, 112:144]).then_inc(*ctx.bump("sce", 1))

            for (src, dsth, dstl, rows, cols) in ((w_in, wih_s, wil_s, D, 2 * DI),
                                                  (w_out, woh_s, wol_s, DI, D)):
                for j in range(rows // 128):
                    @on("sync")
                    def _(e, src=src, j=j, cols=cols):
                        ctx.wait(e, "dmo", ctx.val("dmo"))
                        e.dma_start(out=pscrA[:, 0:cols], in_=src[j * 128:(j + 1) * 128, :]).then_inc(*ctx.bump("dma", 16))
                    dma_t = ctx.val("dma")

                    @on("scalar")
                    def _(e, dma_t=dma_t, cols=cols):
                        ctx.wait(e, "dma", dma_t)
                        e.copy(out=pscrB[:, 0:cols], in_=pscrA[:, 0:cols]).then_inc(*ctx.bump("sce", 1))
                    sce_t = ctx.val("sce")

                    @on("vector")
                    def _(e, sce_t=sce_t, cols=cols):
                        ctx.wait(e, "sce", sce_t)
                        e.tensor_tensor(out=pscrC[:, 0:cols], in0=pscrA[:, 0:cols], in1=pscrB[:, 0:cols], op=ALU.subtract); ctx.fence_inc(e, "dve")
                    dve_t = ctx.val("dve")

                    @on("sync")
                    def _(e, dve_t=dve_t, dsth=dsth, dstl=dstl, j=j, cols=cols, rows=rows):
                        ctx.wait(e, "dve", dve_t)
                        if rows == D:
                            e.dma_start(out=dsth[:, j, :, :].transpose([1, 0, 2]),
                                        in_=pscrB[:, 0:cols].rearrange("p (cc m) -> p cc m", m=128)).then_inc(*ctx.bump("dmo", 16))
                            e.dma_start(out=dstl[:, j, :, :].transpose([1, 0, 2]),
                                        in_=pscrC[:, 0:cols].rearrange("p (cc m) -> p cc m", m=128)).then_inc(*ctx.bump("dmo", 16))
                        else:
                            e.dma_start(out=dsth[j * 128:(j + 1) * 128, :], in_=pscrB[:, 0:cols]).then_inc(*ctx.bump("dmo", 16))
                            e.dma_start(out=dstl[j * 128:(j + 1) * 128, :], in_=pscrC[:, 0:cols]).then_inc(*ctx.bump("dmo", 16))

            @on("vector")
            def _(e):
                ctx.wait(e, "dve", ctx.val("dve"))
                e.memset(msum[:], 0.0)
                e.engine_nop().then_inc(*ctx.bump("dve", 1))

            for col_ix, (src, nch, cols) in enumerate(((f_w1, HCH, D), (f_w2, DMCH, DH))):
                for j in range(nch):
                    @on("sync")
                    def _(e, src=src, j=j, cols=cols):
                        ctx.wait(e, "dve", ctx.val("dve"))
                        e.dma_start(out=pscrA[:, 0:cols], in_=src[j * 128:(j + 1) * 128, :]).then_inc(*ctx.bump("dma", 16))
                    dma_t = ctx.val("dma")

                    @on("vector")
                    def _(e, dma_t=dma_t, cols=cols, col_ix=col_ix):
                        ctx.wait(e, "dma", dma_t)
                        e.reduce_sum(out=scrs[:, 0:1], in_=pscrA[:, 0:cols], axis=AX.X, apply_absolute_value=True)
                        e.drain()
                        e.tensor_tensor(out=msum[:, col_ix:col_ix + 1], in0=msum[:, col_ix:col_ix + 1],
                                        in1=scrs[:, 0:1], op=ALU.add)
                        e.drain(); ctx.fence_inc(e, "dve")
            dve_ms = ctx.val("dve")

            @on("tensor")
            def _(e, dve_ms=dve_ms):
                ctx.wait(e, "dve", dve_ms)
                e.matmul(pps0[0:1, 0:1], msum[:, 0:1], ones128[:, :], start=True, stop=False)
                mm = e.matmul(pps0[0:1, 1:2], msum[:, 1:2], ones128[:, :], start=False, stop=True)
                ctx.inc(mm, "pe", 1)
            pe_ms = ctx.val("pe")

            @on("vector")
            def _(e, pe_ms=pe_ms):
                ctx.wait(e, "pe", pe_ms)
                e.tensor_copy(smv[:, 0:2], pps0[0:1, 0:2])
                e.drain()
                e.tensor_single_scalar(out=smv[:, 2:3], in_=smv[:, 0:1], scalar=1.0 / (DH * D), op=ALU.mult)
                e.tensor_single_scalar(out=smv[:, 3:4], in_=smv[:, 1:2], scalar=1.0 / (DH * D), op=ALU.mult)
                e.drain()
                e.tensor_single_scalar(out=smv[:, 2:3], in_=smv[:, 2:3], scalar=1e-5, op=ALU.max)
                e.tensor_single_scalar(out=smv[:, 3:4], in_=smv[:, 3:4], scalar=1e-5, op=ALU.max)
                e.drain()
                e.reciprocal(out=smv[:, 4:5], in_=smv[:, 2:3])
                e.reciprocal(out=smv[:, 5:6], in_=smv[:, 3:4])
                e.tensor_single_scalar(out=smv[:, 6:7], in_=smv[:, 2:3], scalar=1.0 / 127.0, op=ALU.mult)
                e.tensor_single_scalar(out=smv[:, 7:8], in_=smv[:, 3:4], scalar=1.0 / 127.0, op=ALU.mult)
                e.drain(); ctx.fence_inc(e, "dve")
            dve_sw = ctx.val("dve")

            @on("tensor")
            def _(e, dve_sw=dve_sw):
                ctx.wait(e, "dve", dve_sw)
                mm = e.matmul(pps0[:, 0:8], onescol[:, :], smv[:, 0:8], start=True, stop=True)
                ctx.inc(mm, "pe", 1)
            pe_bc = ctx.val("pe")

            @on("scalar")
            def _(e, pe_bc=pe_bc):
                ctx.wait(e, "pe", pe_bc)
                e.copy(out=sw1b[:, :], in_=pps0[:, 4:5])
                e.copy(out=sw2b[:, :], in_=pps0[:, 5:6])
                e.copy(out=k1c[:, :], in_=pps0[:, 6:7])
                e.copy(out=k2c[:, :], in_=pps0[:, 7:8]).then_inc(*ctx.bump("sce", 1))
            sce_sw = ctx.val("sce")

            for wi, (src, swb, nch, cols, dst) in enumerate(
                    ((f_w1, sw1b, HCH, D, w1qt_s), (f_w2, sw2b, DMCH, DH, w2qt_s))):
                njt = cols // 128
                for j in range(nch):
                    @on("sync")
                    def _(e, src=src, j=j, cols=cols):
                        ctx.wait(e, "dmo", ctx.val("dmo"))
                        ctx.wait(e, "dve", ctx.val("dve"))
                        e.dma_start(out=pscrA[:, 0:cols], in_=src[j * 128:(j + 1) * 128, :]).then_inc(*ctx.bump("dma", 16))
                    dma_t = ctx.val("dma")

                    @on("vector")
                    def _(e, dma_t=dma_t, cols=cols, swb=swb, first=(wi == 0 and j == 0), sce_sw=sce_sw):
                        ctx.wait(e, "dma", dma_t)
                        if first:
                            ctx.wait(e, "sce", sce_sw)
                        ctx.wait(e, "pe", ctx.val("pe"))
                        e.tensor_scalar(out=pscrB[:, 0:cols], in0=pscrA[:, 0:cols], scalar1=swb[:, :],
                                        scalar2=0.5, op0=ALU.mult, op1=ALU.is_gt)
                        e.tensor_scalar(out=scr1[:, 0:cols] if False else pscrC[:, 0:cols],
                                        in0=pscrA[:, 0:cols], scalar1=swb[:, :],
                                        scalar2=-0.5, op0=ALU.mult, op1=ALU.is_lt)
                        e.tensor_tensor(out=pscrC[:, 0:cols], in0=pscrB[:, 0:cols], in1=pscrC[:, 0:cols],
                                        op=ALU.subtract); ctx.fence_inc(e, "dve")
                    dve_t = ctx.val("dve")

                    nblk = (njt + 3) // 4
                    for blk in range(nblk):
                        n_in = min(4, njt - blk * 4)
                        pb = ppsb0 if blk % 2 == 0 else ppsb1

                        @on("tensor")
                        def _(e, dve_t=dve_t, blk=blk, n_in=n_in, pb=pb):
                            if blk == 0:
                                ctx.wait(e, "dve", dve_t)
                            if blk >= 2:
                                ctx.wait(e, "sce", ctx.val("sce"))
                            for q in range(n_in):
                                jj = blk * 4 + q
                                mm = tpose(e, pb[:, q * 128:(q + 1) * 128], pscrC[:, jj * 128:(jj + 1) * 128],
                                           idnb[:, :], q == 0, q == n_in - 1)
                            ctx.inc(mm, "pe", 1)
                        pe_b = ctx.val("pe")

                        @on("scalar")
                        def _(e, pe_b=pe_b, blk=blk, n_in=n_in, pb=pb):
                            ctx.wait(e, "pe", pe_b)
                            e.copy(out=stag[:, blk * 4:blk * 4 + n_in, :], in_=pb[:, 0:n_in * 128].rearrange("p (q x) -> p q x", x=128)).then_inc(*ctx.bump("sce", 1))
                    sce_st = ctx.val("sce")

                    @on("sync")
                    def _(e, sce_st=sce_st, wi=wi, j=j, njt=njt, dst=dst):
                        ctx.wait(e, "sce", sce_st)
                        if wi == 0:
                            e.dma_start(out=dst[:, :, j * 128:(j + 1) * 128].transpose([1, 0, 2]),
                                        in_=stag[:, 0:njt, :]).then_inc(*ctx.bump("dmo", 16))
                        else:
                            e.dma_start(out=dst[:, j * 128:(j + 1) * 128].rearrange("(a p) m -> a p m", p=128).transpose([1, 0, 2]),
                                        in_=stag[:, 0:njt, :]).then_inc(*ctx.bump("dmo", 16))

            @on("sync")
            def _(e):
                ctx.wait(e, "dmo", ctx.val("dmo"))

        ctx.barrier(on, ("dmo", "dma", "dmw"))
        if DEBUG_SCRATCH:
            @on("sync")
            def _(e):
                ctx.wait(e, "sce", ctx.val("sce"))
                ctx.wait(e, "dve", ctx.val("dve"))
                e.dma_start(out=dbg_c[:, 0:256], in_=ang[:, :, :].rearrange("p c s -> p (c s)")).then_inc(*ctx.bump("dmo", 16))
                e.dma_start(out=dbg_c[:, 256:320], in_=cwt[:, :, :].rearrange("p c k -> p (c k)")).then_inc(*ctx.bump("dmo", 16))
                e.dma_start(out=dbg_c[:, 320:336], in_=cbt[:, :]).then_inc(*ctx.bump("dmo", 16))
                e.dma_start(out=dbg_c[:, 336:352], in_=bdtt[:, :]).then_inc(*ctx.bump("dmo", 16))
                e.dma_start(out=dbg_c[:, 352:368], in_=dvt[:, :]).then_inc(*ctx.bump("dmo", 16))
                e.dma_start(out=dbg_c[:, 368:400], in_=fcbt[:, :]).then_inc(*ctx.bump("dmo", 16))
                e.dma_start(out=dbg_c[:, 400:496], in_=fcwt[:, :, :].rearrange("p c k -> p (c k)")).then_inc(*ctx.bump("dmo", 16))
                e.dma_start(out=dbg_c[:, 512:2048], in_=wxt[:, :, :].rearrange("p c m -> p (c m)")).then_inc(*ctx.bump("dmo", 16))
        with nc.sbuf_tensor("xcv", [128, DCH, 3 + T], F32) as xcv, \
             nc.sbuf_tensor("zsl", [128, DCH, T], F32) as zsl, \
             nc.sbuf_tensor("dta", [128, DCH, T], F32) as dta, \
             nc.sbuf_tensor("bcgB", [128, 2, SGS, T], F32) as bcgB, \
             nc.sbuf_tensor("agrp", [128, SGS, T], F32) as agrp, \
             nc.sbuf_tensor("ugrp", [128, 2, SGS, T], F32) as ugrp, \
             nc.sbuf_tensor("wxzb", [128, 2, 2, 8, 128], BF16) as wxzb:

            for b in range(B):
                @on("vector")
                def _(e):
                    ctx.wait(e, "gp", ctx.val("gp"))
                    e.memset(hstate[:], 0.0)
                    e.memset(xtail[:], 0.0)
                    e.engine_nop().then_inc(*ctx.bump("dve", 1))

                pe_wxz_prev = {}
                pe_wo_prev = {}
                for tile in range(NT):
                    t0 = tile * T
                    ctx.barrier(on, ("dxo", "dyo", "dho"))

                    with nc.sbuf_tensor(f"ln1T_{b}_{tile}", [128, 2, DMCH, T], BF16) as ln1T, \
                         nc.psum_tensor(f"psbH_{b}_{tile}", [128, 512], BF16) as psbH, \
                         nc.psum_tensor(f"psbL_{b}_{tile}", [128, 512], BF16) as psbL:
                        for i in range(NSUB):
                            @on("sync")
                            def _(e, b=b, t0=t0, i=i):
                                ctx.wait(e, "dve", ctx.val("dve"))
                                e.dma_start(out=scr1[:, :], in_=x_in[b, t0 + i * 128:t0 + (i + 1) * 128, :]).then_inc(*ctx.bump("dma", 16))
                            dma_x = ctx.val("dma")

                            @on("vector")
                            def _(e, dma_x=dma_x):
                                ctx.wait(e, "dma", dma_x)
                                e.engine_nop().then_inc(*ctx.bump("dve", 1))

                            sq = ugrp[:].rearrange("p a s t -> p (a s t)")[:, 0:1024]
                            lnf = ugrp[:].rearrange("p a s t -> p (a s t)")[:, 1024:2048]
                            sce_ln = emit_ln(sq, lnf)

                            @on("vector")
                            def _(e, sce_ln=sce_ln, lnf=lnf):
                                ctx.wait(e, "sce", sce_ln)
                                ctx.wait(e, "pe", ctx.val("pe"))
                                e.tensor_copy(scrb[:, 0:1024], lnf)
                                e.tensor_tensor(out=scrb[:, 1024:2048], in0=lnf, in1=scrb[:, 0:1024], op=ALU.subtract); ctx.fence_inc(e, "dve")
                            dve_c = ctx.val("dve")

                            for hl in range(2):
                                for half in range(2):
                                    pb = psbH if half == 0 else psbL

                                    @on("tensor")
                                    def _(e, dve_c=dve_c, hl=hl, half=half, pb=pb):
                                        if hl == 0 and half == 0:
                                            ctx.wait(e, "dve", dve_c)
                                        ctx.wait(e, "sce", ctx.val("sce"))
                                        for q in range(4):
                                            j = half * 4 + q
                                            mm = tpose(e, pb[:, q * 128:(q + 1) * 128],
                                                       scrb[:, hl * 1024 + j * 128:hl * 1024 + (j + 1) * 128],
                                                       idnb[:, :], q == 0, q == 3)
                                        ctx.inc(mm, "pe", 1)
                                    pe_h = ctx.val("pe")

                                    @on("scalar")
                                    def _(e, pe_h=pe_h, hl=hl, half=half, i=i, pb=pb):
                                        ctx.wait(e, "pe", pe_h)
                                        e.copy(out=ln1T[:, hl, half * 4:(half + 1) * 4, i * 128:(i + 1) * 128],
                                               in_=pb[:, :].rearrange("p (q x) -> p q x", x=128)).then_inc(*ctx.bump("sce", 1))

                        sce_ln1T = ctx.val("sce")

                        with nc.psum_tensor(f"psP_{b}_{tile}", [96, T], F32) as psP, \
                             nc.psum_tensor(f"psX0_{b}_{tile}", [128, T], F32) as psX0, \
                             nc.psum_tensor(f"psX1_{b}_{tile}", [128, T], F32) as psX1, \
                             nc.psum_tensor(f"psZ0_{b}_{tile}", [128, T], F32) as psZ0, \
                             nc.psum_tensor(f"psZ1_{b}_{tile}", [128, T], F32) as psZ1:
                            sce_evac_prev = {}
                            for c in range(DCH):
                                par = c % 2

                                wsem = "dwx0" if par == 0 else "dwx1"

                                @on("sync")
                                def _(e, c=c, wsem=wsem):
                                    ctx.wait(e, wsem, ctx.val(wsem))
                                    pv = pe_wxz_prev.get("both")
                                    if pv is not None:
                                        ctx.wait(e, "pe", pv)
                                    for hix in range(2):
                                        cc = c + hix * (DI // 128)
                                        e.dma_start(out=wxzb[:, hix, 0, :, :],
                                                    in_=wih_s[cc, :, :, :].transpose([1, 0, 2])).then_inc(*ctx.bump(wsem, 16))
                                        e.dma_start(out=wxzb[:, hix, 1, :, :],
                                                    in_=wil_s[cc, :, :, :].transpose([1, 0, 2])).then_inc(*ctx.bump(wsem, 16))
                                dmw_c = ctx.val(wsem)

                                for hix in range(2):
                                    ps = (psX0 if par == 0 else psX1) if hix == 0 else (psZ0 if par == 0 else psZ1)

                                    @on("tensor")
                                    def _(e, dmw_c=dmw_c, ps=ps, par=par, hix=hix, sce_ln1T=sce_ln1T, wsem=wsem):
                                        ctx.wait(e, wsem, dmw_c)
                                        ctx.wait(e, "sce", max(sce_ln1T, sce_evac_prev.get(par) or 0))
                                        n = 0
                                        for (wl, al) in ((0, 0), (0, 1), (1, 0)):
                                            for k in range(8):
                                                n += 1
                                                mm = e.matmul(ps[:, :], wxzb[:, hix, wl, k, :], ln1T[:, al, k, :],
                                                              start=(n == 1), stop=(n == 24))
                                        ctx.inc(mm, "pe", 1)
                                pe_wxz_prev["both"] = ctx.val("pe")
                                pe_xz = ctx.val("pe")

                                psx = psX0 if par == 0 else psX1
                                psz = psZ0 if par == 0 else psZ1

                                @on("scalar")
                                def _(e, pe_xz=pe_xz, c=c, psx=psx, psz=psz):
                                    ctx.wait(e, "pe", pe_xz)
                                    ctx.wait(e, "dve", ctx.val("dve"))
                                    e.copy(out=xcv[:, c, 3:3 + T], in_=psx[:, :])
                                    e.activation(out=zsl[:, c, :], in_=psz[:, :], func=ACT.Silu).then_inc(*ctx.bump("sce", 1))
                                sce_ev = ctx.val("sce")
                                sce_evac_prev[par] = sce_ev

                                @on("vector")
                                def _(e, sce_ev=sce_ev, c=c):
                                    ctx.wait(e, "sce", sce_ev)
                                    e.tensor_copy(xcv[:, c, 0:3], xtail[:, c, :])
                                    cacc = scr1[:, 0:T]
                                    z1 = scrs[:, 15:16]
                                    e.scalar_tensor_tensor(out=cacc, in0=xcv[:, c, 0:T], scalar=cwt[:, c, 0:1],
                                                           in1=z1.broadcast_to([128, T]), op0=ALU.mult, op1=ALU.add)
                                    for k in range(1, 4):
                                        e.scalar_tensor_tensor(out=cacc, in0=xcv[:, c, k:k + T], scalar=cwt[:, c, k:k + 1],
                                                               in1=cacc, op0=ALU.mult, op1=ALU.add)
                                    e.tensor_copy(xtail[:, c, :], xcv[:, c, T:T + 3]); ctx.fence_inc(e, "dve")
                                dve_cv = ctx.val("dve")

                                @on("scalar")
                                def _(e, dve_cv=dve_cv, c=c):
                                    ctx.wait(e, "dve", dve_cv)
                                    e.activation(out=xcv[:, c, 3:3 + T], in_=scr1[:, 0:T], func=ACT.Silu,
                                                 bias=cbt[:, c:c + 1]).then_inc(*ctx.bump("sce", 1))
                                sce_sl = ctx.val("sce")

                            sce_all_silu = ctx.val("sce")

                            @on("tensor")
                            def _(e, sce_all_silu=sce_all_silu):
                                ctx.wait(e, "sce", sce_all_silu)
                                for c in range(DCH):
                                    mm = e.matmul(psP[:, :], wxt[:, c, :], xcv[:, c, 3:3 + T],
                                                  start=(c == 0), stop=(c == DCH - 1))
                                ctx.inc(mm, "pe", 1)
                            pe_pj = ctx.val("pe")

                            @on("scalar")
                            def _(e, pe_pj=pe_pj):
                                ctx.wait(e, "pe", pe_pj)
                                e.copy(out=projt[:, :], in_=psP[:, :]).then_inc(*ctx.bump("sce", 1))
                            sce_pj = ctx.val("sce")

                            @on("sync")
                            def _(e, sce_pj=sce_pj, tile=tile):
                                ctx.wait(e, "sce", sce_pj)
                                e.dma_start(out=bc_s[tile % 2, :, :], in_=projt[64:96, :]).then_inc(*ctx.bump("dma", 16))
                            dma_bc = ctx.val("dma")

                            if DEBUG_SCRATCH and b == 0 and tile == 0:
                                @on("sync")
                                def _(e):
                                    e.dma_start(out=dbg_b[:, 0:8192], in_=ln1T[:, :, :, :].rearrange("p a j t -> p (a j t)")).then_inc(*ctx.bump("dmo", 16))
                                    e.dma_start(out=dbg_c[:, 2048:2560], in_=xcv[:, 0, 3:3 + T]).then_inc(*ctx.bump("dmo", 16))
                                    e.dma_start(out=dbg_c[:, 2560:3072], in_=zsl[:, 0, :]).then_inc(*ctx.bump("dmo", 16))
                                    e.dma_start(out=dbg_c[0:96, 3072:3584], in_=projt[:, :]).then_inc(*ctx.bump("dmo", 16))

                    with nc.sbuf_tensor(f"yscan_{b}_{tile}", [128, DCH, T], F32) as yscan, \
                         nc.psum_tensor(f"psD0_{b}_{tile}", [128, T], F32) as psD0, \
                         nc.psum_tensor(f"psD1_{b}_{tile}", [128, T], F32) as psD1:

                        sce_sp_prev = {}
                        for c in range(DCH):
                            par = c % 2
                            psD = psD0 if par == 0 else psD1

                            @on("tensor")
                            def _(e, c=c, psD=psD, spw=sce_sp_prev.get(c % 2)):
                                ctx.wait(e, "sce", spw if spw is not None else ctx.val("sce"))
                                mm = e.matmul(psD[:, :], wdtt[:, c * 128:(c + 1) * 128], projt[0:64, :],
                                              start=True, stop=True)
                                ctx.inc(mm, "pe", 1)
                            pe_dt = ctx.val("pe")

                            @on("scalar")
                            def _(e, pe_dt=pe_dt, c=c, psD=psD):
                                ctx.wait(e, "pe", pe_dt)
                                ctx.wait(e, "dve", ctx.val("dve"))
                                e.activation(out=scr1[:, 0:T], in_=psD[:, :], func=ACT.Relu, bias=bdtt[:, c:c + 1])
                                e.activation(out=scr1[:, 512:1024], in_=psD[:, :], func=ACT.Abs, bias=bdtt[:, c:c + 1])
                                e.activation(out=scr1[:, 512:1024], in_=scr1[:, 512:1024], func=ACT.Exp, scale=-1.0)
                                e.activation(out=scr1[:, 512:1024], in_=scr1[:, 512:1024], func=ACT.Ln, bias=cst[:, 1:2]).then_inc(*ctx.bump("sce", 1))
                            sce_sp = ctx.val("sce")
                            sce_sp_prev[c % 2] = sce_sp

                            @on("vector")
                            def _(e, sce_sp=sce_sp, c=c):
                                ctx.wait(e, "sce", sce_sp)
                                e.tensor_tensor(out=dta[:, c, :], in0=scr1[:, 0:T], in1=scr1[:, 512:1024], op=ALU.add); ctx.fence_inc(e, "dve")

                        for sg in range(SG):
                            @on("sync")
                            def _(e, sg=sg, tile=tile, dma_bc=dma_bc):
                                ctx.wait(e, "dma", dma_bc)
                                if sg > 0:
                                    ctx.wait(e, "gp", ctx.val("gp"))
                                    ctx.wait(e, "dve", ctx.val("dve"))
                                e.dma_start(out=bcgB[:, 0, :, :],
                                            in_=bc_s[tile % 2, sg * SGS:(sg + 1) * SGS, :].partition_broadcast(128)).then_inc(*ctx.bump("dma", 16))
                                e.dma_start(out=bcgB[:, 1, :, :],
                                            in_=bc_s[tile % 2, S + sg * SGS:S + (sg + 1) * SGS, :].partition_broadcast(128)).then_inc(*ctx.bump("dma", 16))
                            dma_g = ctx.val("dma")

                            @on("vector")
                            def _(e):
                                ctx.wait(e, "gp", ctx.val("gp"))
                                e.tensor_tensor(out=scr1[:, 0:T], in0=dta[:, 0, :], in1=xcv[:, 0, 3:3 + T], op=ALU.mult); ctx.fence_inc(e, "dve")
                            dve_dtx = {0: ctx.val("dve")}

                            @on("scalar")
                            def _(e, sg=sg):
                                ctx.wait(e, "dve", ctx.val("dve"))
                                for si in range(SGS):
                                    a_ = e.activation(out=agrp[:, si, :], in_=dta[:, 0, :], func=ACT.Exp,
                                                      scale=ang[:, 0, sg * SGS + si:sg * SGS + si + 1])
                                ctx.inc(a_, "sce", 1)
                            sce_a = {0: ctx.val("sce")}
                            gp_u = {}
                            dve_scan = {}
                            dve_tree = {}

                            def emit_ctree(cc, sg=sg):
                                pr = cc % 2
                                tmp = scr1[:, ((cc + 1) % 2) * 512:((cc + 1) % 2) * 512 + T]

                                @on("vector")
                                def _(e, cc=cc, pr=pr, tmp=tmp, sg=sg):
                                    if sg == 0:
                                        e.tensor_tensor(out=yscan[:, cc, :], in0=ugrp[:, pr, 0, :], in1=bcgB[:, 1, 0, :], op=ALU.mult)
                                    else:
                                        e.tensor_tensor(out=tmp, in0=ugrp[:, pr, 0, :], in1=bcgB[:, 1, 0, :], op=ALU.mult)
                                        e.tensor_tensor(out=yscan[:, cc, :], in0=yscan[:, cc, :], in1=tmp, op=ALU.add)
                                    for si in range(1, SGS):
                                        e.tensor_tensor(out=tmp, in0=ugrp[:, pr, si, :], in1=bcgB[:, 1, si, :], op=ALU.mult)
                                        e.tensor_tensor(out=yscan[:, cc, :], in0=yscan[:, cc, :], in1=tmp, op=ALU.add)
                                    ctx.fence_inc(e, "dve")
                                dve_tree[cc] = ctx.val("dve")

                            for c in range(DCH):
                                par = c % 2

                                @on("gpsimd")
                                def _(e, c=c, par=par, dma_g=dma_g, dtx_c=dve_dtx[c],
                                      war=dve_tree.get(c - 2)):
                                    ctx.wait(e, "dve", dtx_c)
                                    ctx.wait(e, "dma", dma_g)
                                    if war is not None:
                                        ctx.wait(e, "dve", war)
                                    e.tensor_tensor(out=ugrp[:, par, :, :],
                                                    in0=scr1[:, par * 512:par * 512 + T].unsqueeze(1).broadcast_to([128, SGS, T]),
                                                    in1=bcgB[:, 0, :, :], op=ALU.mult); ctx.fence_inc(e, "gp")
                                gp_u[c] = ctx.val("gp")

                                if c + 1 < DCH:
                                    @on("vector")
                                    def _(e, c=c):
                                        e.tensor_tensor(out=scr1[:, ((c + 1) % 2) * 512:((c + 1) % 2) * 512 + T],
                                                        in0=dta[:, c + 1, :], in1=xcv[:, c + 1, 3:3 + T], op=ALU.mult); ctx.fence_inc(e, "dve")
                                    dve_dtx[c + 1] = ctx.val("dve")

                                if c >= 1:
                                    emit_ctree(c - 1)

                                @on("vector")
                                def _(e, c=c, sg=sg, par=par, gp_u_c=gp_u[c], sce_a_c=sce_a[c]):
                                    ctx.wait(e, "gp", gp_u_c)
                                    ctx.wait(e, "sce", sce_a_c)
                                    for si in range(SGS):
                                        e.tensor_tensor_scan(out=ugrp[:, par, si, :], data0=agrp[:, si, :],
                                                             data1=ugrp[:, par, si, :],
                                                             initial=hstate[:, c, sg * SGS + si:sg * SGS + si + 1],
                                                             op0=ALU.mult, op1=ALU.add)
                                    e.drain()
                                    e.tensor_copy(hstate[:, c, sg * SGS:(sg + 1) * SGS],
                                                  ugrp[:, par, :, T - 1:T].rearrange("p s o -> p (s o)"))
                                    e.drain(); ctx.fence_inc(e, "dve")
                                dve_scan[c] = ctx.val("dve")

                                if c + 1 < DCH:
                                    @on("scalar")
                                    def _(e, c=c, sg=sg, scan_war=dve_scan[c]):
                                        ctx.wait(e, "dve", scan_war)
                                        for si in range(SGS):
                                            a_ = e.activation(out=agrp[:, si, :], in_=dta[:, c + 1, :], func=ACT.Exp,
                                                              scale=ang[:, c + 1, sg * SGS + si:sg * SGS + si + 1])
                                        ctx.inc(a_, "sce", 1)
                                    sce_a[c + 1] = ctx.val("sce")

                            emit_ctree(DCH - 1)

                        for c in range(DCH):
                            @on("vector")
                            def _(e, c=c):
                                f1 = scr1[:, 0:T]
                                f2 = scr1[:, 512:1024]
                                e.scalar_tensor_tensor(out=f1, in0=xcv[:, c, 3:3 + T], scalar=dvt[:, c:c + 1],
                                                       in1=yscan[:, c, :], op0=ALU.mult, op1=ALU.add)
                                e.tensor_tensor(out=f2, in0=f1, in1=zsl[:, c, :], op=ALU.mult)
                                yb = xcv[:, c, :].bitcast(BF16)
                                e.tensor_copy(yb[:, 0:T], f2)
                                e.tensor_tensor(out=yb[:, T:2 * T], in0=f2, in1=yb[:, 0:T], op=ALU.subtract); ctx.fence_inc(e, "dve")
                        dve_fin = ctx.val("dve")

                        if DEBUG_SCRATCH and b == 0 and tile == 0:
                            @on("sync")
                            def _(e, dve_fin=dve_fin):
                                ctx.wait(e, "dve", dve_fin)
                                e.dma_start(out=dbg_c[:, 3584:4096], in_=dta[:, 0, :]).then_inc(*ctx.bump("dmo", 16))
                                e.dma_start(out=dbg_c[:, 1536:2048], in_=yscan[:, 0, :]).then_inc(*ctx.bump("dmo", 16))

                    with nc.sbuf_tensor(f"wob_{b}_{tile}", [128, 2, 2, D], BF16) as wob, \
                         nc.sbuf_tensor(f"obufE_{b}_{tile}", [128, 2, D], F32) as obuf, \
                         nc.psum_tensor(f"psE0_{b}_{tile}", [128, 512], F32) as psE0, \
                         nc.psum_tensor(f"psE1_{b}_{tile}", [128, 512], F32) as psE1, \
                         nc.psum_tensor(f"psE2_{b}_{tile}", [128, 512], F32) as psE2, \
                         nc.psum_tensor(f"psE3_{b}_{tile}", [128, 512], F32) as psE3, \
                         nc.psum_tensor(f"psE4_{b}_{tile}", [128, 512], F32) as psE4, \
                         nc.psum_tensor(f"psE5_{b}_{tile}", [128, 512], F32) as psE5, \
                         nc.psum_tensor(f"psE6_{b}_{tile}", [128, 512], F32) as psE6, \
                         nc.psum_tensor(f"psE7_{b}_{tile}", [128, 512], F32) as psE7:
                        psE = [psE0, psE1, psE2, psE3, psE4, psE5, psE6, psE7]
                        for c in range(DCH):
                            par = c % 2

                            osem = "dwo0" if par == 0 else "dwo1"

                            @on("sync")
                            def _(e, c=c, par=par, dve_fin=dve_fin, osem=osem):
                                ctx.wait(e, osem, ctx.val(osem))
                                if c == 0:
                                    ctx.wait(e, "dve", dve_fin)
                                pv = pe_wo_prev.get(par)
                                if pv is not None:
                                    ctx.wait(e, "pe", pv)
                                e.dma_start(out=wob[:, par, 0, :], in_=woh_s[c * 128:(c + 1) * 128, :]).then_inc(*ctx.bump(osem, 16))
                                e.dma_start(out=wob[:, par, 1, :], in_=wol_s[c * 128:(c + 1) * 128, :]).then_inc(*ctx.bump(osem, 16))
                            dmw_o = ctx.val(osem)

                            @on("tensor")
                            def _(e, c=c, par=par, dmw_o=dmw_o, dve_fin=dve_fin, osem=osem):
                                ctx.wait(e, osem, dmw_o)
                                if c == 0:
                                    ctx.wait(e, "dve", dve_fin)
                                yb = xcv[:, c, :].bitcast(BF16)
                                for mb in range(4):
                                    for (yl, wl) in ((0, 0), (0, 1), (1, 0)):
                                        lhs = yb[:, yl * T + mb * 128:yl * T + (mb + 1) * 128]
                                        for nb in range(2):
                                            mm = e.matmul(psE[mb * 2 + nb][:, :], lhs, wob[:, par, wl, nb * 512:(nb + 1) * 512],
                                                          start=(c == 0 and yl == 0 and wl == 0),
                                                          stop=(c == DCH - 1 and yl == 1))
                                ctx.inc(mm, "pe", 1)
                            pe_wo_prev[par] = ctx.val("pe")
                        pe_wo = ctx.val("pe")

                        dxo_prev = {}
                        for mb in range(4):
                            @on("sync")
                            def _(e, b=b, t0=t0, mb=mb):
                                ctx.wait(e, "dve", ctx.val("dve"))
                                e.dma_start(out=scr1[:, :], in_=x_in[b, t0 + mb * 128:t0 + (mb + 1) * 128, :]).then_inc(*ctx.bump("dma", 16))
                            dma_r = ctx.val("dma")

                            @on("vector")
                            def _(e, pe_wo=pe_wo, dma_r=dma_r, mb=mb, dxo_safe=dxo_prev.get(mb % 2)):
                                ctx.wait(e, "pe", pe_wo)
                                ctx.wait(e, "dma", dma_r)
                                if dxo_safe is not None:
                                    ctx.wait(e, "dxo", dxo_safe)
                                for nb in range(2):
                                    e.tensor_tensor(out=obuf[:, mb % 2, nb * 512:(nb + 1) * 512], in0=psE[mb * 2 + nb][:, :],
                                                    in1=scr1[:, nb * 512:(nb + 1) * 512], op=ALU.add)
                                e.engine_nop().then_inc(*ctx.bump("dve", 1))
                            dve_x2 = ctx.val("dve")

                            @on("sync")
                            def _(e, dve_x2=dve_x2, b=b, t0=t0, mb=mb):
                                ctx.wait(e, "dxo", ctx.val("dxo"))
                                ctx.wait(e, "dve", dve_x2)
                                e.dma_start(out=x2_s[b, t0 + mb * 128:t0 + (mb + 1) * 128, :], in_=obuf[:, mb % 2, :]).then_inc(*ctx.bump("dxo", 16))
                            dxo_prev[mb % 2] = ctx.val("dxo")

        with ExitStack() as ffn_es:
            wqr = ffn_es.enter_context(nc.sbuf_tensor("wqr", [128, DMCH, DH], BF16))
            k1T = ffn_es.enter_context(nc.sbuf_tensor("k1T", [128, DMCH, L], BF16))
            hlin = ffn_es.enter_context(nc.sbuf_tensor("hlin", [128, 2 + L], F32))
            hbuf = ffn_es.enter_context(nc.sbuf_tensor("hbuf", [128, L], F32))
            macc = ffn_es.enter_context(nc.sbuf_tensor("macc", [128, L], F32))
            i1bc = ffn_es.enter_context(nc.sbuf_tensor("i1bc", [128, L], F32))
            s2bc = ffn_es.enter_context(nc.sbuf_tensor("s2bc", [128, L], F32))
            k2b = ffn_es.enter_context(nc.sbuf_tensor("k2b", [128, 2, T], BF16))
            obufF = ffn_es.enter_context(nc.sbuf_tensor("obufF", [128, 2, D], F32))
            w2b = ffn_es.enter_context(nc.sbuf_tensor("w2b", [128, 2, D], BF16))

            for b in range(B):
                ctx.barrier(on, ("dxo", "dyo", "dho", "dma", "dmo"))

                with nc.psum_tensor(f"fpsbH_{b}", [128, 512], BF16) as psbH, \
                     nc.psum_tensor(f"fpsbL_{b}", [128, 512], BF16) as psbL:
                    for tt in range(L // 128):
                        @on("sync")
                        def _(e, b=b, tt=tt):
                            ctx.wait(e, "dve", ctx.val("dve"))
                            ctx.wait(e, "dma", ctx.val("dma"))
                            e.dma_start(out=scr1[:, :], in_=x2_s[b, tt * 128:(tt + 1) * 128, :]).then_inc(*ctx.bump("dma", 16))
                        dma_x = ctx.val("dma")

                        @on("vector")
                        def _(e, dma_x=dma_x):
                            ctx.wait(e, "dma", dma_x)
                            e.engine_nop().then_inc(*ctx.bump("dve", 1))

                        sce_ln = emit_ln(hlin[:, 0:1024], hlin[:, 1024:2048])

                        @on("vector")
                        def _(e, sce_ln=sce_ln, b=b, tt=tt):
                            ctx.wait(e, "sce", sce_ln)
                            ctx.wait(e, "pe", ctx.val("pe"))
                            ln2 = hlin[:, 1024:2048]
                            e.reduce_max(out=scrs[:, 8:9], in_=ln2, axis=AX.X, apply_absolute_value=True)
                            e.drain()
                            e.tensor_single_scalar(out=scrs[:, 8:9], in_=scrs[:, 8:9], scalar=1e-5, op=ALU.max)
                            e.drain()
                            e.reciprocal(out=scrs[:, 9:10], in_=scrs[:, 8:9])
                            e.drain()
                            e.tensor_single_scalar(out=scrs[:, 9:10], in_=scrs[:, 9:10], scalar=127.0, op=ALU.mult)
                            e.drain()
                            e.tensor_scalar(out=hlin[:, 0:1024], in0=ln2, scalar1=scrs[:, 9:10], scalar2=RND,
                                            op0=ALU.mult, op1=ALU.add)
                            e.tensor_single_scalar(out=scrb[:, 0:1024], in_=hlin[:, 0:1024], scalar=RND, op=ALU.subtract)
                            e.tensor_tensor(out=scrs[:, 10:11], in0=scrs[:, 8:9], in1=k1c[:, :], op=ALU.mult)
                            e.drain(); ctx.fence_inc(e, "dve")
                        dve_k = ctx.val("dve")

                        @on("sync")
                        def _(e, dve_k=dve_k, b=b, tt=tt):
                            ctx.wait(e, "dve", dve_k)
                            e.dma_start(out=i1r_s[b, tt * 128:(tt + 1) * 128].unsqueeze(1), in_=scrs[:, 10:11]).then_inc(*ctx.bump("dma", 16))

                        for half in range(2):
                            pb = psbH if half == 0 else psbL

                            @on("tensor")
                            def _(e, dve_k=dve_k, half=half, pb=pb):
                                if half == 0:
                                    ctx.wait(e, "dve", dve_k)
                                ctx.wait(e, "sce", ctx.val("sce"))
                                for q in range(4):
                                    j = half * 4 + q
                                    mm = tpose(e, pb[:, q * 128:(q + 1) * 128], scrb[:, j * 128:(j + 1) * 128],
                                               idnb[:, :], q == 0, q == 3)
                                ctx.inc(mm, "pe", 1)
                            pe_t = ctx.val("pe")

                            @on("scalar")
                            def _(e, pe_t=pe_t, tt=tt, half=half, pb=pb):
                                ctx.wait(e, "pe", pe_t)
                                e.copy(out=k1T[:, half * 4:(half + 1) * 4, tt * 128:(tt + 1) * 128],
                                       in_=pb[:, :].rearrange("p (q x) -> p q x", x=128)).then_inc(*ctx.bump("sce", 1))

                @on("sync")
                def _(e, b=b):
                    ctx.wait(e, "pe", ctx.val("pe"))
                    ctx.wait(e, "dma", ctx.val("dma"))
                    ctx.wait(e, "dmw", ctx.val("dmw"))
                    for j in range(DMCH):
                        e.dma_start(out=wqr[:, j, :], in_=w1qt_s[j, :, :]).then_inc(*ctx.bump("dmw", 16))
                    e.dma_start(out=i1bc[:, :], in_=i1r_s[b, :].unsqueeze(0).partition_broadcast(128).squeeze(1)).then_inc(*ctx.bump("dma", 16))
                dmw_w1 = ctx.val("dmw")
                dma_i1 = ctx.val("dma")

                @on("vector")
                def _(e):
                    ctx.wait(e, "gp", ctx.val("gp"))
                    e.memset(hlin[:, 0:1], 0.0)
                    e.memset(hlin[:, 1 + L:2 + L], 0.0)
                    e.memset(macc[:], 0.0)
                    e.engine_nop().then_inc(*ctx.bump("dve", 1))

                with nc.psum_tensor(f"psH_{b}", [128, 4 * 512], F32) as psH:
                    sce_h_prev = None
                    for hc in range(HCH):
                        @on("tensor")
                        def _(e, hc=hc, dmw_w1=dmw_w1):
                            ctx.wait(e, "dmw", dmw_w1)
                            if hc == 0:
                                ctx.wait(e, "sce", ctx.val("sce"))
                            if hc > 0:
                                ctx.wait(e, "dve", ctx.val("dve"))
                            for nb in range(4):
                                for k in range(8):
                                    mm = e.matmul(psH[:, nb * 512:(nb + 1) * 512], wqr[:, k, hc * 128:(hc + 1) * 128],
                                                  k1T[:, k, nb * 512:(nb + 1) * 512], start=(k == 0), stop=(k == 7))
                            ctx.inc(mm, "pe", 1)
                        pe_h = ctx.val("pe")

                        @on("vector")
                        def _(e, pe_h=pe_h, dma_i1=dma_i1, sce_h_prev=sce_h_prev):
                            ctx.wait(e, "pe", pe_h)
                            ctx.wait(e, "dma", dma_i1)
                            ctx.wait(e, "dho", ctx.val("dho"))
                            if sce_h_prev is not None:
                                ctx.wait(e, "sce", sce_h_prev)
                            for nb in range(4):
                                e.tensor_tensor(out=hlin[:, 1 + nb * 512:1 + (nb + 1) * 512],
                                                in0=psH[:, nb * 512:(nb + 1) * 512],
                                                in1=i1bc[:, nb * 512:(nb + 1) * 512], op=ALU.mult)
                            z1 = scrs[:, 15:16]
                            e.scalar_tensor_tensor(out=hbuf[:, :], in0=hlin[:, 0:L], scalar=fcwt[:, hc, 0:1],
                                                   in1=z1.broadcast_to([128, L]), op0=ALU.mult, op1=ALU.add)
                            e.scalar_tensor_tensor(out=hbuf[:, :], in0=hlin[:, 1:1 + L], scalar=fcwt[:, hc, 1:2],
                                                   in1=hbuf[:, :], op0=ALU.mult, op1=ALU.add)
                            e.scalar_tensor_tensor(out=hbuf[:, :], in0=hlin[:, 2:2 + L], scalar=fcwt[:, hc, 2:3],
                                                   in1=hbuf[:, :], op0=ALU.mult, op1=ALU.add); ctx.fence_inc(e, "dve")
                        dve_cv = ctx.val("dve")

                        @on("scalar")
                        def _(e, dve_cv=dve_cv, hc=hc):
                            ctx.wait(e, "dve", dve_cv)
                            e.activation(out=hbuf[:, :], in_=hbuf[:, :], func=ACT.Silu, bias=fcbt[:, hc:hc + 1])
                            e.activation(out=hlin[:, 1:1 + L], in_=hbuf[:, :], func=ACT.Abs).then_inc(*ctx.bump("sce", 1))
                        sce_h = ctx.val("sce")
                        sce_h_prev = sce_h

                        @on("vector")
                        def _(e, sce_h=sce_h):
                            ctx.wait(e, "sce", sce_h)
                            e.tensor_tensor(out=macc[:, :], in0=macc[:, :], in1=hlin[:, 1:1 + L], op=ALU.max); ctx.fence_inc(e, "dve")

                        @on("sync")
                        def _(e, sce_h=sce_h, hc=hc):
                            ctx.wait(e, "dho", ctx.val("dho"))
                            ctx.wait(e, "sce", sce_h)
                            e.dma_start(out=h_scr[hc * 128:(hc + 1) * 128, :], in_=hbuf[:, :]).then_inc(*ctx.bump("dho", 16))

                with nc.psum_tensor(f"psM_{b}", [128, 512], F32) as psM:
                    dve_red = ctx.val("dve")
                    for blk in range(4):
                        @on("tensor")
                        def _(e, blk=blk, dve_red=dve_red):
                            if blk == 0:
                                ctx.wait(e, "dve", ctx.val("dve"))
                            else:
                                ctx.wait(e, "dve", ctx.val("dve"))
                            for q in range(4):
                                it = blk * 4 + q
                                mm = tpose(e, psM[:, q * 128:(q + 1) * 128], macc[:, it * 128:(it + 1) * 128],
                                           idn[:, :], q == 0, q == 3)
                            ctx.inc(mm, "pe", 1)
                        pe_m = ctx.val("pe")

                        @on("vector")
                        def _(e, pe_m=pe_m, blk=blk):
                            ctx.wait(e, "pe", pe_m)
                            for q in range(4):
                                it = blk * 4 + q
                                e.reduce_max(out=inv2t[:, it:it + 1], in_=psM[:, q * 128:(q + 1) * 128], axis=AX.X)
                            e.drain()
                            e.engine_nop().then_inc(*ctx.bump("dve", 1))

                    @on("vector")
                    def _(e, b=b):
                        e.drain()
                        e.tensor_single_scalar(out=inv2t[:, :], in_=inv2t[:, :], scalar=1e-5, op=ALU.max)
                        e.drain()
                        e.reciprocal(out=scr1[:, 0:16], in_=inv2t[:, :])
                        e.drain()
                        e.tensor_single_scalar(out=scr1[:, 0:16], in_=scr1[:, 0:16], scalar=127.0, op=ALU.mult)
                        e.tensor_tensor(out=inv2t[:, :], in0=inv2t[:, :],
                                        in1=k2c[:, :].broadcast_to([128, 16]), op=ALU.mult)
                        e.drain(); ctx.fence_inc(e, "dve")
                    dve_s2 = ctx.val("dve")

                    @on("tensor")
                    def _(e, dve_s2=dve_s2):
                        ctx.wait(e, "dve", dve_s2)
                        mm = tpose(e, psM[0:16, 0:128], scr1[:, 0:16], idn[:, :], True, True)
                        ctx.inc(mm, "pe", 1)
                    pe_s2 = ctx.val("pe")

                    @on("scalar")
                    def _(e, pe_s2=pe_s2):
                        ctx.wait(e, "pe", pe_s2)
                        e.copy(out=scr1[0:16, 512:640], in_=psM[0:16, 0:128]).then_inc(*ctx.bump("sce", 1))
                    sce_s2 = ctx.val("sce")

                    @on("sync")
                    def _(e, sce_s2=sce_s2, b=b):
                        ctx.wait(e, "sce", sce_s2)
                        e.dma_start(out=s2r_s[b, :].rearrange("(it x) -> it x", x=128), in_=scr1[0:16, 512:640]).then_inc(*ctx.bump("dma", 16))
                    dma_s2w = ctx.val("dma")

                    @on("sync")
                    def _(e, dma_s2w=dma_s2w, b=b):
                        ctx.wait(e, "dma", dma_s2w)
                        e.dma_start(out=s2bc[:, :], in_=s2r_s[b, :].unsqueeze(0).partition_broadcast(128).squeeze(1)).then_inc(*ctx.bump("dma", 16))
                    dma_s2b = ctx.val("dma")

                with nc.psum_tensor(f"psO0_{b}", [128, 512], F32) as psO0, \
                     nc.psum_tensor(f"psO1_{b}", [128, 512], F32) as psO1, \
                     nc.psum_tensor(f"psO2_{b}", [128, 512], F32) as psO2, \
                     nc.psum_tensor(f"psO3_{b}", [128, 512], F32) as psO3, \
                     nc.psum_tensor(f"psO4_{b}", [128, 512], F32) as psO4, \
                     nc.psum_tensor(f"psO5_{b}", [128, 512], F32) as psO5, \
                     nc.psum_tensor(f"psO6_{b}", [128, 512], F32) as psO6, \
                     nc.psum_tensor(f"psO7_{b}", [128, 512], F32) as psO7:
                    psO = [psO0, psO1, psO2, psO3, psO4, psO5, psO6, psO7]
                    dve_evac_last = None
                    for tile in range(NT):
                        t0 = tile * T
                        pe_mm_prev = {}
                        for hc in range(HCH):
                            par = hc % 2

                            hsem = "dh0" if par == 0 else "dh1"
                            w2sem = "dw20" if par == 0 else "dw21"

                            @on("sync")
                            def _(e, hc=hc, par=par, t0=t0, dma_s2b=dma_s2b, hsem=hsem, w2sem=w2sem):
                                ctx.wait(e, hsem, ctx.val(hsem))
                                ctx.wait(e, w2sem, ctx.val(w2sem))
                                ctx.wait(e, "dho", ctx.val("dho"))
                                ctx.wait(e, "dma", dma_s2b)
                                pv = pe_mm_prev.get(par)
                                if pv is not None:
                                    ctx.wait(e, "pe", pv)
                                ctx.wait(e, "dve", ctx.val("dve"))
                                e.dma_start(out=hbuf[:, par * T:(par + 1) * T],
                                            in_=h_scr[hc * 128:(hc + 1) * 128, t0:t0 + T]).then_inc(*ctx.bump(hsem, 16))
                                e.dma_start(out=w2b[:, par, :], in_=w2qt_s[hc * 128:(hc + 1) * 128, :]).then_inc(*ctx.bump(w2sem, 16))
                            dma_h = ctx.val(hsem)
                            dmw_2 = ctx.val(w2sem)

                            @on("vector")
                            def _(e, dma_h=dma_h, par=par, t0=t0, pv=pe_mm_prev.get(par), hsem=hsem):
                                ctx.wait(e, hsem, dma_h)
                                if pv is not None:
                                    ctx.wait(e, "pe", pv)
                                e.tensor_tensor(out=scr1[:, 0:T], in0=hbuf[:, par * T:(par + 1) * T],
                                                in1=s2bc[:, t0:t0 + T], op=ALU.mult)
                                e.tensor_scalar(out=k2b[:, par, :], in0=scr1[:, 0:T], scalar1=RND, scalar2=RND,
                                                op0=ALU.add, op1=ALU.subtract); ctx.fence_inc(e, "dve")
                            dve_k2 = ctx.val("dve")

                            @on("tensor")
                            def _(e, dve_k2=dve_k2, dmw_2=dmw_2, hc=hc, par=par, tile=tile, dve_evac_last=dve_evac_last, w2sem=w2sem):
                                ctx.wait(e, "dve", dve_k2)
                                ctx.wait(e, w2sem, dmw_2)
                                if hc == 0 and tile > 0 and dve_evac_last is not None:
                                    ctx.wait(e, "dve", dve_evac_last)
                                for mb in range(4):
                                    for nb in range(2):
                                        mm = e.matmul(psO[mb * 2 + nb][:, :], k2b[:, par, mb * 128:(mb + 1) * 128],
                                                      w2b[:, par, nb * 512:(nb + 1) * 512],
                                                      start=(hc == 0), stop=(hc == HCH - 1))
                                ctx.inc(mm, "pe", 1)
                            pe_mm_prev[par] = ctx.val("pe")
                        pe_o = ctx.val("pe")

                        dyo_prev = {}
                        for mb in range(4):
                            @on("sync")
                            def _(e, b=b, t0=t0, mb=mb):
                                ctx.wait(e, "dve", ctx.val("dve"))
                                e.dma_start(out=scr1[:, :], in_=x2_s[b, t0 + mb * 128:t0 + (mb + 1) * 128, :]).then_inc(*ctx.bump("dma", 16))
                            dma_r = ctx.val("dma")

                            @on("vector")
                            def _(e, pe_o=pe_o, dma_r=dma_r, mb=mb, tile=tile, dyo_safe=dyo_prev.get(mb % 2)):
                                ctx.wait(e, "pe", pe_o)
                                ctx.wait(e, "dma", dma_r)
                                if dyo_safe is not None:
                                    ctx.wait(e, "dyo", dyo_safe)
                                for nb in range(2):
                                    e.scalar_tensor_tensor(out=obufF[:, mb % 2, nb * 512:(nb + 1) * 512], in0=psO[mb * 2 + nb][:, :],
                                                           scalar=inv2t[:, tile * 4 + mb:tile * 4 + mb + 1],
                                                           in1=scr1[:, nb * 512:(nb + 1) * 512], op0=ALU.mult, op1=ALU.add)
                                e.engine_nop().then_inc(*ctx.bump("dve", 1))
                            dve_o = ctx.val("dve")
                            dve_evac_last = dve_o

                            @on("sync")
                            def _(e, dve_o=dve_o, b=b, t0=t0, mb=mb):
                                ctx.wait(e, "dyo", ctx.val("dyo"))
                                ctx.wait(e, "dve", dve_o)
                                e.dma_start(out=y_out[b, t0 + mb * 128:t0 + (mb + 1) * 128, :], in_=obufF[:, mb % 2, :]).then_inc(*ctx.bump("dyo", 16))
                            dyo_prev[mb % 2] = ctx.val("dyo")

        @on("sync")
        def _(e):
            for s_ in ("dyo", "dxo", "dho", "dmo", "dma", "dmw",
                       "dwx0", "dwx1", "dwo0", "dwo1", "dh0", "dh1", "dw20", "dw21"):
                ctx.wait(e, s_, ctx.val(s_))

    return nc


def kernel(L0_attn_hidden_states, L0_attn_W_in, L0_attn_conv_w, L0_attn_conv_b,
           L0_attn_W_x, L0_attn_W_dt, L0_attn_b_dt, L0_attn_A_log, L0_attn_D,
           L0_attn_W_out, L0_ffn_w1, L0_ffn_conv_w, L0_ffn_conv_b, L0_ffn_w2):
    x = np.ascontiguousarray(np.asarray(L0_attn_hidden_states, dtype=np.float32))
    weights = dict(
        w_in=np.ascontiguousarray(np.asarray(L0_attn_W_in, dtype=np.float32)),
        conv_w=np.ascontiguousarray(np.asarray(L0_attn_conv_w, dtype=np.float32)),
        conv_b=np.ascontiguousarray(np.asarray(L0_attn_conv_b, dtype=np.float32)),
        w_x=np.ascontiguousarray(np.asarray(L0_attn_W_x, dtype=np.float32)),
        w_dt=np.ascontiguousarray(np.asarray(L0_attn_W_dt, dtype=np.float32)),
        b_dt=np.ascontiguousarray(np.asarray(L0_attn_b_dt, dtype=np.float32)),
        a_log=np.ascontiguousarray(np.asarray(L0_attn_A_log, dtype=np.float32)),
        d_vec=np.ascontiguousarray(np.asarray(L0_attn_D, dtype=np.float32)),
        w_out=np.ascontiguousarray(np.asarray(L0_attn_W_out, dtype=np.float32)),
        f_w1=np.ascontiguousarray(np.asarray(L0_ffn_w1, dtype=np.float32)),
        f_cw=np.ascontiguousarray(np.asarray(L0_ffn_conv_w, dtype=np.float32).reshape(DH, 3)),
        f_cb=np.ascontiguousarray(np.asarray(L0_ffn_conv_b, dtype=np.float32)),
        f_w2=np.ascontiguousarray(np.asarray(L0_ffn_w2, dtype=np.float32)),
    )
    nc = build()
    in_maps = []
    for i in range(8):
        m = dict(weights)
        m["x_in"] = x[2 * i:2 * i + 2]
        in_maps.append(m)
    res = run_bass_kernel_spmd(nc, in_maps, core_ids=list(range(8)))
    return np.concatenate([r["y_out"] for r in res.results], axis=0)


# version 09
# speedup vs baseline: 1.0705x; 1.0045x over previous
import numpy as np
from contextlib import ExitStack

import concourse.bass as bass
import concourse.mybir as mybir
from concourse.bass_utils import run_bass_kernel_spmd

F32 = mybir.dt.float32
BF16 = mybir.dt.bfloat16
ALU = mybir.AluOpType
ACT = mybir.ActivationFunctionType
AX = mybir.AxisListType

B = 2
L = 2048
D = 1024
DI = 2048
S = 16
R = 64
DH = 4096
T = 512
NT = L // T
DCH = DI // 128
DMCH = D // 128
HCH = DH // 128
NSUB = T // 128
SG = 4
SGS = S // SG
RND = 12582912.0
EPS = 1e-5

DEV_DRAINS = False
DEBUG_SCRATCH = False

_DRAIN_OPS = frozenset((
    "tensor_tensor", "tensor_copy", "tensor_scalar", "tensor_single_scalar",
    "scalar_tensor_tensor", "tensor_tensor_scan", "tensor_reduce",
    "reduce_sum", "reduce_max", "memset", "activation", "copy",
    "reciprocal", "iota", "tensor_relu", "tensor_add", "tensor_mul",
    "tensor_sub", "tensor_max", "tensor_scalar_max",
))


class _DrainProxy:
    def __init__(self, eng):
        self._e = eng

    def __getattr__(self, name):
        f = getattr(self._e, name)
        if name in _DRAIN_OPS:
            def g(*a, **k):
                r = f(*a, **k)
                self._e.drain()
                return r
            return g
        return f


class Ctx:
    def __init__(self, nc, block, es):
        self.nc = nc
        self.block = block
        self.es = es
        self.sems = {}

    def mksem(self, *names):
        for n in names:
            self.sems[n] = [self.es.enter_context(self.nc.semaphore(n)), 0]

    def sem(self, name):
        return self.sems[name][0]

    def fence_inc(self, e, name, n=1):
        s = self.sems[name]
        s[1] += n
        e.engine_nop().then_inc(s[0], n)

    def bump(self, name, n=1):
        s = self.sems[name]
        s[1] += n
        return (s[0], n)

    def inc(self, instr, name, n=1):
        s = self.sems[name]
        s[1] += n
        instr.then_inc(s[0], n)
        return s[1]

    def val(self, name):
        return self.sems[name][1]

    def barrier(self, on, store_sems):
        @on("sync")
        def _(e):
            for s_ in store_sems:
                self.wait(e, s_, self.val(s_))
            self.sems["tileb"][1] += 1
            e.sem_inc(self.sems["tileb"][0], 1)
        tb = self.val("tileb")
        for eng in ("vector", "scalar", "tensor", "gpsimd"):
            @on(eng)
            def _(e, tb=tb):
                self.wait(e, "tileb", tb)

    def wait(self, eng, name, v=None):
        s = self.sems[name]
        eng.wait_ge(s[0], s[1] if v is None else v)


def build():
    nc = bass.Bass("TRN2", target_bir_lowering=False,
                   detect_race_conditions=DEV_DRAINS)

    _scr_kind2 = "ExternalOutput" if DEBUG_SCRATCH else "Internal"
    x_in = nc.dram_tensor("x_in", [B, L, D], F32, kind="ExternalInput").ap()
    w_in = nc.dram_tensor("w_in", [D, 2 * DI], F32, kind="ExternalInput").ap()
    conv_w = nc.dram_tensor("conv_w", [4, DI], F32, kind="ExternalInput").ap()
    conv_b = nc.dram_tensor("conv_b", [DI], F32, kind="ExternalInput").ap()
    w_x = nc.dram_tensor("w_x", [DI, R + 2 * S], F32, kind="ExternalInput").ap()
    w_dt = nc.dram_tensor("w_dt", [R, DI], F32, kind="ExternalInput").ap()
    b_dt = nc.dram_tensor("b_dt", [DI], F32, kind="ExternalInput").ap()
    a_log = nc.dram_tensor("a_log", [DI, S], F32, kind="ExternalInput").ap()
    d_vec = nc.dram_tensor("d_vec", [DI], F32, kind="ExternalInput").ap()
    w_out = nc.dram_tensor("w_out", [DI, D], F32, kind="ExternalInput").ap()
    f_w1 = nc.dram_tensor("f_w1", [DH, D], F32, kind="ExternalInput").ap()
    f_cw = nc.dram_tensor("f_cw", [DH, 3], F32, kind="ExternalInput").ap()
    f_cb = nc.dram_tensor("f_cb", [DH], F32, kind="ExternalInput").ap()
    f_w2 = nc.dram_tensor("f_w2", [D, DH], F32, kind="ExternalInput").ap()
    y_out = nc.dram_tensor("y_out", [B, L, D], F32, kind="ExternalOutput").ap()

    _scr_kind0 = "ExternalOutput" if DEBUG_SCRATCH else "Internal"
    wih_s = nc.dram_tensor("wih_s", [2 * DI // 128, 8, 128, 128], BF16, kind=_scr_kind0).ap()
    wil_s = nc.dram_tensor("wil_s", [2 * DI // 128, 8, 128, 128], BF16, kind=_scr_kind0).ap()
    woh_s = nc.dram_tensor("woh_s", [DI, D], BF16, kind="Internal").ap()
    wol_s = nc.dram_tensor("wol_s", [DI, D], BF16, kind="Internal").ap()
    w1qt_s = nc.dram_tensor("w1qt_s", [DMCH, 128, DH], BF16, kind=_scr_kind2).ap()
    w2qt_s = nc.dram_tensor("w2qt_s", [DH, D], BF16, kind=_scr_kind2).ap()
    bc_s = nc.dram_tensor("bc_s", [2, 2 * S, T], F32, kind="Internal").ap()
    _scr_kind = "ExternalOutput" if DEBUG_SCRATCH else "Internal"
    x2_s = nc.dram_tensor("x2_s", [B, L, D], F32, kind=_scr_kind).ap()
    h_scr = nc.dram_tensor("h_scr", [DH, L], F32, kind=_scr_kind).ap()
    dbg_c = nc.dram_tensor("dbg_c", [128, 4096], F32, kind=_scr_kind).ap() if DEBUG_SCRATCH else None
    dbg_b = nc.dram_tensor("dbg_b", [128, 8192], BF16, kind=_scr_kind).ap() if DEBUG_SCRATCH else None
    dbg_k = nc.dram_tensor("dbg_k", [128, DMCH, L], BF16, kind=_scr_kind).ap() if DEBUG_SCRATCH else None
    s2r_s = nc.dram_tensor("s2r_s", [B, L], F32, kind="Internal").ap()
    i1r_s = nc.dram_tensor("i1r_s", [B, L], F32, kind="Internal").ap()

    es = ExitStack()
    with es:
        def sb(name, shape, dt=F32):
            return es.enter_context(nc.sbuf_tensor(name, shape, dt))

        wxt = sb("wxt", [128, DCH, 96])
        wdtt = sb("wdtt", [64, DI])
        cwt = sb("cwt", [128, DCH, 4])
        cbt = sb("cbt", [128, DCH])
        bdtt = sb("bdtt", [128, DCH])
        dvt = sb("dvt", [128, DCH])
        ang = sb("ang", [128, DCH, S])
        fcwt = sb("fcwt", [128, HCH, 3])
        fcbt = sb("fcbt", [128, HCH])
        idn = sb("idn", [128, 128])
        idnb = sb("idnb", [128, 128], BF16)
        onescol = sb("onescol", [1, 128])
        ones128 = sb("ones128", [128, 1])
        k1c = sb("k1c", [128, 1])
        k2c = sb("k2c", [128, 1])
        sw1b = sb("sw1b", [128, 1])
        sw2b = sb("sw2b", [128, 1])
        hstate = sb("hstate", [128, DCH, S])
        xtail = sb("xtail", [128, DCH, 3])
        scr1 = sb("scr1", [128, 1024])
        scrb = sb("scrb", [128, 2048], BF16)
        scrs = sb("scrs", [128, 16])
        cst = sb("cst", [128, 2])
        projt = sb("projt", [96, T])
        inv2t = sb("inv2t", [128, 16])

        block = es.enter_context(nc.Block())
        ctx = Ctx(nc, block, es)
        ctx.mksem("dmw", "dma", "dmo", "pe", "dve", "sce", "gp",
                  "dwx0", "dwx1", "dwo0", "dwo1", "dh0", "dh1", "dw20", "dw21",
                  "dho", "dxo", "dyo", "tileb")

        def on(name):
            dec = getattr(block, name)
            if not DEV_DRAINS or name in ("sync", "tensor"):
                return dec

            def wrap(fn):
                return dec(lambda e: fn(_DrainProxy(e)))
            return wrap

        SET_LNEXP = 6
        SET_SILU = 18

        def act_set(e, set_id):
            e.drain()
            e.add_instruction(mybir.InstLoadActFuncSet(
                name=nc.get_next_instruction_name(), act_func_set_id=set_id, ins=[], outs=[]))
            e.drain()

        def tpose(e, out, in_, ident, start, stop):
            return e.matmul(out, in_, ident, is_transpose=True, start=start, stop=stop)

        def emit_ln(sqbuf, lnfbuf):
            @on("vector")
            def _(e):
                e.reduce_sum(out=scrs[:, 0:1], in_=scr1[:, :], axis=AX.X); ctx.fence_inc(e, "dve")
            dve_s = ctx.val("dve")

            @on("scalar")
            def _(e, dve_s=dve_s):
                ctx.wait(e, "dve", dve_s)
                e.activation(out=sqbuf, in_=scr1[:, :], func=ACT.Square).then_inc(*ctx.bump("sce", 1))
            sce_sq = ctx.val("sce")

            @on("vector")
            def _(e, sce_sq=sce_sq):
                ctx.wait(e, "sce", sce_sq)
                e.reduce_sum(out=scrs[:, 1:2], in_=sqbuf, axis=AX.X)
                e.drain()
                e.tensor_single_scalar(out=scrs[:, 2:3], in_=scrs[:, 0:1], scalar=1.0 / D, op=ALU.mult)
                e.tensor_single_scalar(out=scrs[:, 3:4], in_=scrs[:, 1:2], scalar=1.0 / D, op=ALU.mult)
                e.drain()
                e.tensor_tensor(out=scrs[:, 4:5], in0=scrs[:, 2:3], in1=scrs[:, 2:3], op=ALU.mult)
                e.drain()
                e.tensor_tensor(out=scrs[:, 4:5], in0=scrs[:, 3:4], in1=scrs[:, 4:5], op=ALU.subtract)
                e.drain(); ctx.fence_inc(e, "dve")
            dve_v = ctx.val("dve")

            @on("scalar")
            def _(e, dve_v=dve_v):
                ctx.wait(e, "dve", dve_v)
                e.activation(out=scrs[:, 5:6], in_=scrs[:, 4:5], func=ACT.Ln, bias=cst[:, 0:1])
                e.drain()
                e.activation(out=scrs[:, 6:7], in_=scrs[:, 5:6], func=ACT.Exp, scale=-0.5)
                e.drain()
                e.activation(out=scrs[:, 6:7], in_=scrs[:, 6:7], func=ACT.Identity).then_inc(*ctx.bump("sce", 1))
            sce_r = ctx.val("sce")

            @on("vector")
            def _(e, sce_r=sce_r):
                ctx.wait(e, "sce", sce_r)
                e.tensor_tensor(out=scrs[:, 7:8], in0=scrs[:, 2:3], in1=scrs[:, 6:7], op=ALU.mult)
                e.drain()
                e.tensor_single_scalar(out=scrs[:, 7:8], in_=scrs[:, 7:8], scalar=-1.0, op=ALU.mult)
                e.drain(); ctx.fence_inc(e, "dve")
            dve_mb = ctx.val("dve")

            @on("scalar")
            def _(e, dve_mb=dve_mb):
                ctx.wait(e, "dve", dve_mb)
                e.activation(out=lnfbuf, in_=scr1[:, :], func=ACT.Identity,
                             scale=scrs[:, 6:7], bias=scrs[:, 7:8]).then_inc(*ctx.bump("sce", 1))
            return ctx.val("sce")

        with nc.sbuf_tensor("pscrA", [128, DH], F32) as pscrA, \
             nc.sbuf_tensor("pscrB", [128, DH], BF16) as pscrB, \
             nc.sbuf_tensor("pscrC", [128, DH], BF16) as pscrC, \
             nc.sbuf_tensor("stag", [128, HCH, 128], BF16) as stag, \
             nc.sbuf_tensor("cw4", [4, DI], F32) as cw4, \
             nc.sbuf_tensor("vec3", [1, 3 * DI + DH], F32) as vec3, \
             nc.sbuf_tensor("msum", [128, 2], F32) as msum, \
             nc.sbuf_tensor("smv", [1, 8], F32) as smv, \
             nc.psum_tensor("pps0", [128, 512], F32) as pps0, \
             nc.psum_tensor("ppsb0", [128, 512], BF16) as ppsb0, \
             nc.psum_tensor("ppsb1", [128, 512], BF16) as ppsb1:

            @on("sync")
            def _(e):
                e.dma_start(out=wxt[:, :, :], in_=w_x.rearrange("(c p) m -> p c m", p=128)).then_inc(*ctx.bump("dmw", 16))
                e.dma_start(out=wdtt[:, :], in_=w_dt).then_inc(*ctx.bump("dmw", 16))
                e.dma_start(out=ang[:, :, :], in_=a_log.rearrange("(c p) s -> p c s", p=128)).then_inc(*ctx.bump("dmw", 16))
                e.dma_start(out=fcwt[:, :, :], in_=f_cw.rearrange("(c p) k -> p c k", p=128)).then_inc(*ctx.bump("dmw", 16))
                e.dma_start(out=cw4[:, :], in_=conv_w).then_inc(*ctx.bump("dmw", 16))
                e.dma_start(out=vec3[:, 0:DI], in_=conv_b.unsqueeze(0)).then_inc(*ctx.bump("dmw", 16))
                e.dma_start(out=vec3[:, DI:2 * DI], in_=b_dt.unsqueeze(0)).then_inc(*ctx.bump("dmw", 16))
                e.dma_start(out=vec3[:, 2 * DI:3 * DI], in_=d_vec.unsqueeze(0)).then_inc(*ctx.bump("dmw", 16))
                e.dma_start(out=vec3[:, 3 * DI:3 * DI + DH], in_=f_cb.unsqueeze(0)).then_inc(*ctx.bump("dmw", 16))
            dmw0 = ctx.val("dmw")

            @on("gpsimd")
            def _(e):
                e.iota(out=scr1[:, 0:128], pattern=[[-1, 128]], base=0,
                       channel_multiplier=1, allow_small_or_imprecise_dtypes=True); ctx.fence_inc(e, "gp")
            gp_i = ctx.val("gp")

            @on("vector")
            def _(e, gp_i=gp_i, dmw0=dmw0):
                ctx.wait(e, "gp", gp_i)
                e.tensor_single_scalar(out=idn[:], in_=scr1[:, 0:128], scalar=0.0, op=ALU.is_equal)
                e.tensor_copy(idnb[:], idn[:])
                e.memset(onescol[:], 1.0)
                e.memset(ones128[:], 1.0)
                e.memset(scrs[:, 15:16], 0.0)
                e.memset(cst[:, 0:1], EPS)
                e.memset(cst[:, 1:2], 1.0)
                ctx.wait(e, "dmw", dmw0)
                e.engine_nop().then_inc(*ctx.bump("dve", 1))
            dve0 = ctx.val("dve")

            @on("scalar")
            def _(e, dve0=dve0):
                ctx.wait(e, "dve", dve0)
                e.activation(out=ang[:, :, :], in_=ang[:, :, :], func=ACT.Exp, scale=1.0).then_inc(*ctx.bump("sce", 1))
            sce_a0 = ctx.val("sce")

            @on("vector")
            def _(e, sce_a0=sce_a0):
                ctx.wait(e, "sce", sce_a0)
                e.tensor_single_scalar(out=ang[:, :, :], in_=ang[:, :, :], scalar=-1.0, op=ALU.mult); ctx.fence_inc(e, "dve")
            dve_ang = ctx.val("dve")

            @on("tensor")
            def _(e, dve_ang=dve_ang):
                ctx.wait(e, "dve", dve_ang)
                n_tot = DCH + DCH + DCH + DCH + HCH
                k = 0
                for c in range(DCH):
                    tpose(e, pps0[:, c * 4:(c + 1) * 4], cw4[:, c * 128:(c + 1) * 128], idn[0:4, 0:4], k == 0, k == n_tot - 1); k += 1
                for c in range(DCH):
                    tpose(e, pps0[:, 64 + c:65 + c], vec3[:, c * 128:(c + 1) * 128], idn[0:1, 0:1], k == 0, k == n_tot - 1); k += 1
                for c in range(DCH):
                    tpose(e, pps0[:, 80 + c:81 + c], vec3[:, DI + c * 128:DI + (c + 1) * 128], idn[0:1, 0:1], k == 0, k == n_tot - 1); k += 1
                for c in range(DCH):
                    tpose(e, pps0[:, 96 + c:97 + c], vec3[:, 2 * DI + c * 128:2 * DI + (c + 1) * 128], idn[0:1, 0:1], k == 0, k == n_tot - 1); k += 1
                for c in range(HCH):
                    mm = tpose(e, pps0[:, 112 + c:113 + c], vec3[:, 3 * DI + c * 128:3 * DI + (c + 1) * 128], idn[0:1, 0:1], k == 0, k == n_tot - 1); k += 1
                ctx.inc(mm, "pe", 1)
            pe_tp = ctx.val("pe")

            @on("scalar")
            def _(e, pe_tp=pe_tp):
                ctx.wait(e, "pe", pe_tp)
                e.copy(out=cwt[:, :, :], in_=pps0[:, 0:64].rearrange("p (c k) -> p c k", k=4))
                e.copy(out=cbt[:, :], in_=pps0[:, 64:80])
                e.copy(out=bdtt[:, :], in_=pps0[:, 80:96])
                e.copy(out=dvt[:, :], in_=pps0[:, 96:112])
                e.copy(out=fcbt[:, :], in_=pps0[:, 112:144]).then_inc(*ctx.bump("sce", 1))

            for (src, dsth, dstl, rows, cols) in ((w_in, wih_s, wil_s, D, 2 * DI),
                                                  (w_out, woh_s, wol_s, DI, D)):
                for j in range(rows // 128):
                    @on("sync")
                    def _(e, src=src, j=j, cols=cols):
                        ctx.wait(e, "dmo", ctx.val("dmo"))
                        e.dma_start(out=pscrA[:, 0:cols], in_=src[j * 128:(j + 1) * 128, :]).then_inc(*ctx.bump("dma", 16))
                    dma_t = ctx.val("dma")

                    @on("scalar")
                    def _(e, dma_t=dma_t, cols=cols):
                        ctx.wait(e, "dma", dma_t)
                        e.copy(out=pscrB[:, 0:cols], in_=pscrA[:, 0:cols]).then_inc(*ctx.bump("sce", 1))
                    sce_t = ctx.val("sce")

                    @on("vector")
                    def _(e, sce_t=sce_t, cols=cols):
                        ctx.wait(e, "sce", sce_t)
                        e.tensor_tensor(out=pscrC[:, 0:cols], in0=pscrA[:, 0:cols], in1=pscrB[:, 0:cols], op=ALU.subtract); ctx.fence_inc(e, "dve")
                    dve_t = ctx.val("dve")

                    @on("sync")
                    def _(e, dve_t=dve_t, dsth=dsth, dstl=dstl, j=j, cols=cols, rows=rows):
                        ctx.wait(e, "dve", dve_t)
                        if rows == D:
                            e.dma_start(out=dsth[:, j, :, :].transpose([1, 0, 2]),
                                        in_=pscrB[:, 0:cols].rearrange("p (cc m) -> p cc m", m=128)).then_inc(*ctx.bump("dmo", 16))
                            e.dma_start(out=dstl[:, j, :, :].transpose([1, 0, 2]),
                                        in_=pscrC[:, 0:cols].rearrange("p (cc m) -> p cc m", m=128)).then_inc(*ctx.bump("dmo", 16))
                        else:
                            e.dma_start(out=dsth[j * 128:(j + 1) * 128, :], in_=pscrB[:, 0:cols]).then_inc(*ctx.bump("dmo", 16))
                            e.dma_start(out=dstl[j * 128:(j + 1) * 128, :], in_=pscrC[:, 0:cols]).then_inc(*ctx.bump("dmo", 16))

            @on("vector")
            def _(e):
                ctx.wait(e, "dve", ctx.val("dve"))
                e.memset(msum[:], 0.0)
                e.engine_nop().then_inc(*ctx.bump("dve", 1))

            for col_ix, (src, nch, cols) in enumerate(((f_w1, HCH, D), (f_w2, DMCH, DH))):
                for j in range(nch):
                    @on("sync")
                    def _(e, src=src, j=j, cols=cols):
                        ctx.wait(e, "dve", ctx.val("dve"))
                        e.dma_start(out=pscrA[:, 0:cols], in_=src[j * 128:(j + 1) * 128, :]).then_inc(*ctx.bump("dma", 16))
                    dma_t = ctx.val("dma")

                    @on("vector")
                    def _(e, dma_t=dma_t, cols=cols, col_ix=col_ix):
                        ctx.wait(e, "dma", dma_t)
                        e.reduce_sum(out=scrs[:, 0:1], in_=pscrA[:, 0:cols], axis=AX.X, apply_absolute_value=True)
                        e.drain()
                        e.tensor_tensor(out=msum[:, col_ix:col_ix + 1], in0=msum[:, col_ix:col_ix + 1],
                                        in1=scrs[:, 0:1], op=ALU.add)
                        e.drain(); ctx.fence_inc(e, "dve")
            dve_ms = ctx.val("dve")

            @on("tensor")
            def _(e, dve_ms=dve_ms):
                ctx.wait(e, "dve", dve_ms)
                e.matmul(pps0[0:1, 0:1], msum[:, 0:1], ones128[:, :], start=True, stop=False)
                mm = e.matmul(pps0[0:1, 1:2], msum[:, 1:2], ones128[:, :], start=False, stop=True)
                ctx.inc(mm, "pe", 1)
            pe_ms = ctx.val("pe")

            @on("vector")
            def _(e, pe_ms=pe_ms):
                ctx.wait(e, "pe", pe_ms)
                e.tensor_copy(smv[:, 0:2], pps0[0:1, 0:2])
                e.drain()
                e.tensor_single_scalar(out=smv[:, 2:3], in_=smv[:, 0:1], scalar=1.0 / (DH * D), op=ALU.mult)
                e.tensor_single_scalar(out=smv[:, 3:4], in_=smv[:, 1:2], scalar=1.0 / (DH * D), op=ALU.mult)
                e.drain()
                e.tensor_single_scalar(out=smv[:, 2:3], in_=smv[:, 2:3], scalar=1e-5, op=ALU.max)
                e.tensor_single_scalar(out=smv[:, 3:4], in_=smv[:, 3:4], scalar=1e-5, op=ALU.max)
                e.drain()
                e.reciprocal(out=smv[:, 4:5], in_=smv[:, 2:3])
                e.reciprocal(out=smv[:, 5:6], in_=smv[:, 3:4])
                e.tensor_single_scalar(out=smv[:, 6:7], in_=smv[:, 2:3], scalar=1.0 / 127.0, op=ALU.mult)
                e.tensor_single_scalar(out=smv[:, 7:8], in_=smv[:, 3:4], scalar=1.0 / 127.0, op=ALU.mult)
                e.drain(); ctx.fence_inc(e, "dve")
            dve_sw = ctx.val("dve")

            @on("tensor")
            def _(e, dve_sw=dve_sw):
                ctx.wait(e, "dve", dve_sw)
                mm = e.matmul(pps0[:, 0:8], onescol[:, :], smv[:, 0:8], start=True, stop=True)
                ctx.inc(mm, "pe", 1)
            pe_bc = ctx.val("pe")

            @on("scalar")
            def _(e, pe_bc=pe_bc):
                ctx.wait(e, "pe", pe_bc)
                e.copy(out=sw1b[:, :], in_=pps0[:, 4:5])
                e.copy(out=sw2b[:, :], in_=pps0[:, 5:6])
                e.copy(out=k1c[:, :], in_=pps0[:, 6:7])
                e.copy(out=k2c[:, :], in_=pps0[:, 7:8]).then_inc(*ctx.bump("sce", 1))
            sce_sw = ctx.val("sce")

            for wi, (src, swb, nch, cols, dst) in enumerate(
                    ((f_w1, sw1b, HCH, D, w1qt_s), (f_w2, sw2b, DMCH, DH, w2qt_s))):
                njt = cols // 128
                for j in range(nch):
                    @on("sync")
                    def _(e, src=src, j=j, cols=cols):
                        ctx.wait(e, "dmo", ctx.val("dmo"))
                        ctx.wait(e, "dve", ctx.val("dve"))
                        e.dma_start(out=pscrA[:, 0:cols], in_=src[j * 128:(j + 1) * 128, :]).then_inc(*ctx.bump("dma", 16))
                    dma_t = ctx.val("dma")

                    @on("vector")
                    def _(e, dma_t=dma_t, cols=cols, swb=swb, first=(wi == 0 and j == 0), sce_sw=sce_sw):
                        ctx.wait(e, "dma", dma_t)
                        if first:
                            ctx.wait(e, "sce", sce_sw)
                        ctx.wait(e, "pe", ctx.val("pe"))
                        e.tensor_scalar(out=pscrB[:, 0:cols], in0=pscrA[:, 0:cols], scalar1=swb[:, :],
                                        scalar2=0.5, op0=ALU.mult, op1=ALU.is_gt)
                        e.tensor_scalar(out=scr1[:, 0:cols] if False else pscrC[:, 0:cols],
                                        in0=pscrA[:, 0:cols], scalar1=swb[:, :],
                                        scalar2=-0.5, op0=ALU.mult, op1=ALU.is_lt)
                        e.tensor_tensor(out=pscrC[:, 0:cols], in0=pscrB[:, 0:cols], in1=pscrC[:, 0:cols],
                                        op=ALU.subtract); ctx.fence_inc(e, "dve")
                    dve_t = ctx.val("dve")

                    nblk = (njt + 3) // 4
                    for blk in range(nblk):
                        n_in = min(4, njt - blk * 4)
                        pb = ppsb0 if blk % 2 == 0 else ppsb1

                        @on("tensor")
                        def _(e, dve_t=dve_t, blk=blk, n_in=n_in, pb=pb):
                            if blk == 0:
                                ctx.wait(e, "dve", dve_t)
                            if blk >= 2:
                                ctx.wait(e, "sce", ctx.val("sce"))
                            for q in range(n_in):
                                jj = blk * 4 + q
                                mm = tpose(e, pb[:, q * 128:(q + 1) * 128], pscrC[:, jj * 128:(jj + 1) * 128],
                                           idnb[:, :], q == 0, q == n_in - 1)
                            ctx.inc(mm, "pe", 1)
                        pe_b = ctx.val("pe")

                        @on("scalar")
                        def _(e, pe_b=pe_b, blk=blk, n_in=n_in, pb=pb):
                            ctx.wait(e, "pe", pe_b)
                            e.copy(out=stag[:, blk * 4:blk * 4 + n_in, :], in_=pb[:, 0:n_in * 128].rearrange("p (q x) -> p q x", x=128)).then_inc(*ctx.bump("sce", 1))
                    sce_st = ctx.val("sce")

                    @on("sync")
                    def _(e, sce_st=sce_st, wi=wi, j=j, njt=njt, dst=dst):
                        ctx.wait(e, "sce", sce_st)
                        if wi == 0:
                            e.dma_start(out=dst[:, :, j * 128:(j + 1) * 128].transpose([1, 0, 2]),
                                        in_=stag[:, 0:njt, :]).then_inc(*ctx.bump("dmo", 16))
                        else:
                            e.dma_start(out=dst[:, j * 128:(j + 1) * 128].rearrange("(a p) m -> a p m", p=128).transpose([1, 0, 2]),
                                        in_=stag[:, 0:njt, :]).then_inc(*ctx.bump("dmo", 16))

            @on("sync")
            def _(e):
                ctx.wait(e, "dmo", ctx.val("dmo"))

        ctx.barrier(on, ("dmo", "dma", "dmw"))
        if DEBUG_SCRATCH:
            @on("sync")
            def _(e):
                ctx.wait(e, "sce", ctx.val("sce"))
                ctx.wait(e, "dve", ctx.val("dve"))
                e.dma_start(out=dbg_c[:, 0:256], in_=ang[:, :, :].rearrange("p c s -> p (c s)")).then_inc(*ctx.bump("dmo", 16))
                e.dma_start(out=dbg_c[:, 256:320], in_=cwt[:, :, :].rearrange("p c k -> p (c k)")).then_inc(*ctx.bump("dmo", 16))
                e.dma_start(out=dbg_c[:, 320:336], in_=cbt[:, :]).then_inc(*ctx.bump("dmo", 16))
                e.dma_start(out=dbg_c[:, 336:352], in_=bdtt[:, :]).then_inc(*ctx.bump("dmo", 16))
                e.dma_start(out=dbg_c[:, 352:368], in_=dvt[:, :]).then_inc(*ctx.bump("dmo", 16))
                e.dma_start(out=dbg_c[:, 368:400], in_=fcbt[:, :]).then_inc(*ctx.bump("dmo", 16))
                e.dma_start(out=dbg_c[:, 400:496], in_=fcwt[:, :, :].rearrange("p c k -> p (c k)")).then_inc(*ctx.bump("dmo", 16))
                e.dma_start(out=dbg_c[:, 512:2048], in_=wxt[:, :, :].rearrange("p c m -> p (c m)")).then_inc(*ctx.bump("dmo", 16))
        with nc.sbuf_tensor("xcv", [128, DCH, 3 + T], F32) as xcv, \
             nc.sbuf_tensor("zsl", [128, DCH, T], F32) as zsl, \
             nc.sbuf_tensor("dta", [128, DCH, T], F32) as dta, \
             nc.sbuf_tensor("bcgB", [128, 2, SGS, T], F32) as bcgB, \
             nc.sbuf_tensor("agrp", [128, SGS, T], F32) as agrp, \
             nc.sbuf_tensor("ugrp", [128, 2, SGS, T], F32) as ugrp, \
             nc.sbuf_tensor("wxzb", [128, 2, 2, 8, 128], BF16) as wxzb:

            for b in range(B):
                @on("vector")
                def _(e):
                    ctx.wait(e, "gp", ctx.val("gp"))
                    e.memset(hstate[:], 0.0)
                    e.memset(xtail[:], 0.0)
                    e.engine_nop().then_inc(*ctx.bump("dve", 1))

                pe_wxz_prev = {}
                pe_wo_prev = {}
                for tile in range(NT):
                    t0 = tile * T
                    ctx.barrier(on, ("dxo", "dyo", "dho"))

                    with nc.sbuf_tensor(f"ln1T_{b}_{tile}", [128, 2, DMCH, T], BF16) as ln1T, \
                         nc.psum_tensor(f"psbH_{b}_{tile}", [128, 512], BF16) as psbH, \
                         nc.psum_tensor(f"psbL_{b}_{tile}", [128, 512], BF16) as psbL:
                        for i in range(NSUB):
                            @on("sync")
                            def _(e, b=b, t0=t0, i=i):
                                ctx.wait(e, "dve", ctx.val("dve"))
                                e.dma_start(out=scr1[:, :], in_=x_in[b, t0 + i * 128:t0 + (i + 1) * 128, :]).then_inc(*ctx.bump("dma", 16))
                            dma_x = ctx.val("dma")

                            @on("vector")
                            def _(e, dma_x=dma_x):
                                ctx.wait(e, "dma", dma_x)
                                e.engine_nop().then_inc(*ctx.bump("dve", 1))

                            sq = ugrp[:].rearrange("p a s t -> p (a s t)")[:, 0:1024]
                            lnf = ugrp[:].rearrange("p a s t -> p (a s t)")[:, 1024:2048]
                            sce_ln = emit_ln(sq, lnf)

                            @on("vector")
                            def _(e, sce_ln=sce_ln, lnf=lnf):
                                ctx.wait(e, "sce", sce_ln)
                                ctx.wait(e, "pe", ctx.val("pe"))
                                e.tensor_copy(scrb[:, 0:1024], lnf)
                                e.tensor_tensor(out=scrb[:, 1024:2048], in0=lnf, in1=scrb[:, 0:1024], op=ALU.subtract); ctx.fence_inc(e, "dve")
                            dve_c = ctx.val("dve")

                            for hl in range(2):
                                for half in range(2):
                                    pb = psbH if half == 0 else psbL

                                    @on("tensor")
                                    def _(e, dve_c=dve_c, hl=hl, half=half, pb=pb):
                                        if hl == 0 and half == 0:
                                            ctx.wait(e, "dve", dve_c)
                                        ctx.wait(e, "sce", ctx.val("sce"))
                                        for q in range(4):
                                            j = half * 4 + q
                                            mm = tpose(e, pb[:, q * 128:(q + 1) * 128],
                                                       scrb[:, hl * 1024 + j * 128:hl * 1024 + (j + 1) * 128],
                                                       idnb[:, :], q == 0, q == 3)
                                        ctx.inc(mm, "pe", 1)
                                    pe_h = ctx.val("pe")

                                    @on("scalar")
                                    def _(e, pe_h=pe_h, hl=hl, half=half, i=i, pb=pb):
                                        ctx.wait(e, "pe", pe_h)
                                        e.copy(out=ln1T[:, hl, half * 4:(half + 1) * 4, i * 128:(i + 1) * 128],
                                               in_=pb[:, :].rearrange("p (q x) -> p q x", x=128)).then_inc(*ctx.bump("sce", 1))

                        sce_ln1T = ctx.val("sce")

                        with nc.psum_tensor(f"psP_{b}_{tile}", [96, T], F32) as psP, \
                             nc.psum_tensor(f"psX0_{b}_{tile}", [128, T], F32) as psX0, \
                             nc.psum_tensor(f"psX1_{b}_{tile}", [128, T], F32) as psX1, \
                             nc.psum_tensor(f"psZ0_{b}_{tile}", [128, T], F32) as psZ0, \
                             nc.psum_tensor(f"psZ1_{b}_{tile}", [128, T], F32) as psZ1:
                            sce_evac_prev = {}
                            for c in range(DCH):
                                par = c % 2

                                wsem = "dwx0" if par == 0 else "dwx1"

                                @on("sync")
                                def _(e, c=c, wsem=wsem):
                                    ctx.wait(e, wsem, ctx.val(wsem))
                                    pv = pe_wxz_prev.get("both")
                                    if pv is not None:
                                        ctx.wait(e, "pe", pv)
                                    for hix in range(2):
                                        cc = c + hix * (DI // 128)
                                        e.dma_start(out=wxzb[:, hix, 0, :, :],
                                                    in_=wih_s[cc, :, :, :].transpose([1, 0, 2])).then_inc(*ctx.bump(wsem, 16))
                                        e.dma_start(out=wxzb[:, hix, 1, :, :],
                                                    in_=wil_s[cc, :, :, :].transpose([1, 0, 2])).then_inc(*ctx.bump(wsem, 16))
                                dmw_c = ctx.val(wsem)

                                for hix in range(2):
                                    ps = (psX0 if par == 0 else psX1) if hix == 0 else (psZ0 if par == 0 else psZ1)

                                    @on("tensor")
                                    def _(e, dmw_c=dmw_c, ps=ps, par=par, hix=hix, sce_ln1T=sce_ln1T, wsem=wsem):
                                        ctx.wait(e, wsem, dmw_c)
                                        ctx.wait(e, "sce", max(sce_ln1T, sce_evac_prev.get(par) or 0))
                                        n = 0
                                        for (wl, al) in ((0, 0), (0, 1), (1, 0)):
                                            for k in range(8):
                                                n += 1
                                                mm = e.matmul(ps[:, :], wxzb[:, hix, wl, k, :], ln1T[:, al, k, :],
                                                              start=(n == 1), stop=(n == 24))
                                        ctx.inc(mm, "pe", 1)
                                pe_wxz_prev["both"] = ctx.val("pe")
                                pe_xz = ctx.val("pe")

                                psx = psX0 if par == 0 else psX1
                                psz = psZ0 if par == 0 else psZ1

                                @on("scalar")
                                def _(e, pe_xz=pe_xz, c=c, psx=psx, psz=psz):
                                    ctx.wait(e, "pe", pe_xz)
                                    ctx.wait(e, "dve", ctx.val("dve"))
                                    e.copy(out=xcv[:, c, 3:3 + T], in_=psx[:, :])
                                    e.activation(out=zsl[:, c, :], in_=psz[:, :], func=ACT.Silu).then_inc(*ctx.bump("sce", 1))
                                sce_ev = ctx.val("sce")
                                sce_evac_prev[par] = sce_ev

                                @on("vector")
                                def _(e, sce_ev=sce_ev, c=c):
                                    ctx.wait(e, "sce", sce_ev)
                                    e.tensor_copy(xcv[:, c, 0:3], xtail[:, c, :])
                                    cacc = scr1[:, 0:T]
                                    z1 = scrs[:, 15:16]
                                    e.scalar_tensor_tensor(out=cacc, in0=xcv[:, c, 0:T], scalar=cwt[:, c, 0:1],
                                                           in1=z1.broadcast_to([128, T]), op0=ALU.mult, op1=ALU.add)
                                    for k in range(1, 4):
                                        e.scalar_tensor_tensor(out=cacc, in0=xcv[:, c, k:k + T], scalar=cwt[:, c, k:k + 1],
                                                               in1=cacc, op0=ALU.mult, op1=ALU.add)
                                    e.tensor_copy(xtail[:, c, :], xcv[:, c, T:T + 3]); ctx.fence_inc(e, "dve")
                                dve_cv = ctx.val("dve")

                                @on("scalar")
                                def _(e, dve_cv=dve_cv, c=c):
                                    ctx.wait(e, "dve", dve_cv)
                                    e.activation(out=xcv[:, c, 3:3 + T], in_=scr1[:, 0:T], func=ACT.Silu,
                                                 bias=cbt[:, c:c + 1]).then_inc(*ctx.bump("sce", 1))
                                sce_sl = ctx.val("sce")

                            sce_all_silu = ctx.val("sce")

                            @on("tensor")
                            def _(e, sce_all_silu=sce_all_silu):
                                ctx.wait(e, "sce", sce_all_silu)
                                for c in range(DCH):
                                    mm = e.matmul(psP[:, :], wxt[:, c, :], xcv[:, c, 3:3 + T],
                                                  start=(c == 0), stop=(c == DCH - 1))
                                ctx.inc(mm, "pe", 1)
                            pe_pj = ctx.val("pe")

                            @on("scalar")
                            def _(e, pe_pj=pe_pj):
                                ctx.wait(e, "pe", pe_pj)
                                e.copy(out=projt[:, :], in_=psP[:, :]).then_inc(*ctx.bump("sce", 1))
                            sce_pj = ctx.val("sce")

                            @on("sync")
                            def _(e, sce_pj=sce_pj, tile=tile):
                                ctx.wait(e, "sce", sce_pj)
                                e.dma_start(out=bc_s[tile % 2, :, :], in_=projt[64:96, :]).then_inc(*ctx.bump("dma", 16))
                            dma_bc = ctx.val("dma")

                            if DEBUG_SCRATCH and b == 0 and tile == 0:
                                @on("sync")
                                def _(e):
                                    e.dma_start(out=dbg_b[:, 0:8192], in_=ln1T[:, :, :, :].rearrange("p a j t -> p (a j t)")).then_inc(*ctx.bump("dmo", 16))
                                    e.dma_start(out=dbg_c[:, 2048:2560], in_=xcv[:, 0, 3:3 + T]).then_inc(*ctx.bump("dmo", 16))
                                    e.dma_start(out=dbg_c[:, 2560:3072], in_=zsl[:, 0, :]).then_inc(*ctx.bump("dmo", 16))
                                    e.dma_start(out=dbg_c[0:96, 3072:3584], in_=projt[:, :]).then_inc(*ctx.bump("dmo", 16))

                    with nc.sbuf_tensor(f"yscan_{b}_{tile}", [128, DCH, T], F32) as yscan, \
                         nc.psum_tensor(f"psD0_{b}_{tile}", [128, T], F32) as psD0, \
                         nc.psum_tensor(f"psD1_{b}_{tile}", [128, T], F32) as psD1:

                        sce_sp_prev = {}
                        for c in range(DCH):
                            par = c % 2
                            psD = psD0 if par == 0 else psD1

                            @on("tensor")
                            def _(e, c=c, psD=psD, spw=sce_sp_prev.get(c % 2)):
                                ctx.wait(e, "sce", spw if spw is not None else ctx.val("sce"))
                                mm = e.matmul(psD[:, :], wdtt[:, c * 128:(c + 1) * 128], projt[0:64, :],
                                              start=True, stop=True)
                                ctx.inc(mm, "pe", 1)
                            pe_dt = ctx.val("pe")

                            @on("scalar")
                            def _(e, pe_dt=pe_dt, c=c, psD=psD):
                                ctx.wait(e, "pe", pe_dt)
                                ctx.wait(e, "dve", ctx.val("dve"))
                                e.activation(out=scr1[:, 0:T], in_=psD[:, :], func=ACT.Relu, bias=bdtt[:, c:c + 1])
                                e.activation(out=scr1[:, 512:1024], in_=psD[:, :], func=ACT.Abs, bias=bdtt[:, c:c + 1])
                                e.activation(out=scr1[:, 512:1024], in_=scr1[:, 512:1024], func=ACT.Exp, scale=-1.0)
                                e.activation(out=scr1[:, 512:1024], in_=scr1[:, 512:1024], func=ACT.Ln, bias=cst[:, 1:2]).then_inc(*ctx.bump("sce", 1))
                            sce_sp = ctx.val("sce")
                            sce_sp_prev[c % 2] = sce_sp

                            @on("vector")
                            def _(e, sce_sp=sce_sp, c=c):
                                ctx.wait(e, "sce", sce_sp)
                                e.tensor_tensor(out=dta[:, c, :], in0=scr1[:, 0:T], in1=scr1[:, 512:1024], op=ALU.add); ctx.fence_inc(e, "dve")

                        for sg in range(SG):
                            @on("sync")
                            def _(e, sg=sg, tile=tile, dma_bc=dma_bc):
                                ctx.wait(e, "dma", dma_bc)
                                if sg > 0:
                                    ctx.wait(e, "gp", ctx.val("gp"))
                                    ctx.wait(e, "dve", ctx.val("dve"))
                                e.dma_start(out=bcgB[:, 0, :, :],
                                            in_=bc_s[tile % 2, sg * SGS:(sg + 1) * SGS, :].partition_broadcast(128)).then_inc(*ctx.bump("dma", 16))
                                e.dma_start(out=bcgB[:, 1, :, :],
                                            in_=bc_s[tile % 2, S + sg * SGS:S + (sg + 1) * SGS, :].partition_broadcast(128)).then_inc(*ctx.bump("dma", 16))
                            dma_g = ctx.val("dma")

                            @on("vector")
                            def _(e):
                                ctx.wait(e, "gp", ctx.val("gp"))
                                e.tensor_tensor(out=scr1[:, 0:T], in0=dta[:, 0, :], in1=xcv[:, 0, 3:3 + T], op=ALU.mult); ctx.fence_inc(e, "dve")
                            dve_dtx = {0: ctx.val("dve")}

                            @on("scalar")
                            def _(e, sg=sg):
                                ctx.wait(e, "dve", ctx.val("dve"))
                                for si in range(SGS):
                                    a_ = e.activation(out=agrp[:, si, :], in_=dta[:, 0, :], func=ACT.Exp,
                                                      scale=ang[:, 0, sg * SGS + si:sg * SGS + si + 1])
                                ctx.inc(a_, "sce", 1)
                            sce_a = {0: ctx.val("sce")}
                            gp_u = {}
                            dve_scan = {}
                            dve_tree = {}

                            def emit_ctree(cc, sg=sg):
                                pr = cc % 2
                                tmp = scr1[:, ((cc + 1) % 2) * 512:((cc + 1) % 2) * 512 + T]

                                @on("vector")
                                def _(e, cc=cc, pr=pr, tmp=tmp, sg=sg):
                                    if sg == 0:
                                        e.tensor_tensor(out=yscan[:, cc, :], in0=ugrp[:, pr, 0, :], in1=bcgB[:, 1, 0, :], op=ALU.mult)
                                    else:
                                        e.tensor_tensor(out=tmp, in0=ugrp[:, pr, 0, :], in1=bcgB[:, 1, 0, :], op=ALU.mult)
                                        e.tensor_tensor(out=yscan[:, cc, :], in0=yscan[:, cc, :], in1=tmp, op=ALU.add)
                                    for si in range(1, SGS):
                                        e.tensor_tensor(out=tmp, in0=ugrp[:, pr, si, :], in1=bcgB[:, 1, si, :], op=ALU.mult)
                                        e.tensor_tensor(out=yscan[:, cc, :], in0=yscan[:, cc, :], in1=tmp, op=ALU.add)
                                    ctx.fence_inc(e, "dve")
                                dve_tree[cc] = ctx.val("dve")

                            for c in range(DCH):
                                par = c % 2

                                @on("gpsimd")
                                def _(e, c=c, par=par, dma_g=dma_g, dtx_c=dve_dtx[c],
                                      war=dve_tree.get(c - 2)):
                                    ctx.wait(e, "dve", dtx_c)
                                    ctx.wait(e, "dma", dma_g)
                                    if war is not None:
                                        ctx.wait(e, "dve", war)
                                    e.tensor_tensor(out=ugrp[:, par, :, :],
                                                    in0=scr1[:, par * 512:par * 512 + T].unsqueeze(1).broadcast_to([128, SGS, T]),
                                                    in1=bcgB[:, 0, :, :], op=ALU.mult); ctx.fence_inc(e, "gp")
                                gp_u[c] = ctx.val("gp")

                                if c + 1 < DCH:
                                    @on("vector")
                                    def _(e, c=c):
                                        e.tensor_tensor(out=scr1[:, ((c + 1) % 2) * 512:((c + 1) % 2) * 512 + T],
                                                        in0=dta[:, c + 1, :], in1=xcv[:, c + 1, 3:3 + T], op=ALU.mult); ctx.fence_inc(e, "dve")
                                    dve_dtx[c + 1] = ctx.val("dve")

                                if c >= 1:
                                    emit_ctree(c - 1)

                                @on("vector")
                                def _(e, c=c, sg=sg, par=par, gp_u_c=gp_u[c], sce_a_c=sce_a[c]):
                                    ctx.wait(e, "gp", gp_u_c)
                                    ctx.wait(e, "sce", sce_a_c)
                                    for si in range(SGS):
                                        e.tensor_tensor_scan(out=ugrp[:, par, si, :], data0=agrp[:, si, :],
                                                             data1=ugrp[:, par, si, :],
                                                             initial=hstate[:, c, sg * SGS + si:sg * SGS + si + 1],
                                                             op0=ALU.mult, op1=ALU.add)
                                    e.drain()
                                    e.tensor_copy(hstate[:, c, sg * SGS:(sg + 1) * SGS],
                                                  ugrp[:, par, :, T - 1:T].rearrange("p s o -> p (s o)"))
                                    e.drain(); ctx.fence_inc(e, "dve")
                                dve_scan[c] = ctx.val("dve")

                                if c + 1 < DCH:
                                    @on("scalar")
                                    def _(e, c=c, sg=sg, scan_war=dve_scan[c]):
                                        ctx.wait(e, "dve", scan_war)
                                        for si in range(SGS):
                                            a_ = e.activation(out=agrp[:, si, :], in_=dta[:, c + 1, :], func=ACT.Exp,
                                                              scale=ang[:, c + 1, sg * SGS + si:sg * SGS + si + 1])
                                        ctx.inc(a_, "sce", 1)
                                    sce_a[c + 1] = ctx.val("sce")

                            emit_ctree(DCH - 1)

                        for c in range(DCH):
                            @on("vector")
                            def _(e, c=c):
                                f1 = scr1[:, 0:T]
                                f2 = scr1[:, 512:1024]
                                e.scalar_tensor_tensor(out=f1, in0=xcv[:, c, 3:3 + T], scalar=dvt[:, c:c + 1],
                                                       in1=yscan[:, c, :], op0=ALU.mult, op1=ALU.add)
                                e.tensor_tensor(out=f2, in0=f1, in1=zsl[:, c, :], op=ALU.mult)
                                yb = xcv[:, c, :].bitcast(BF16)
                                e.tensor_copy(yb[:, 0:T], f2)
                                e.tensor_tensor(out=yb[:, T:2 * T], in0=f2, in1=yb[:, 0:T], op=ALU.subtract); ctx.fence_inc(e, "dve")
                        dve_fin = ctx.val("dve")

                        if DEBUG_SCRATCH and b == 0 and tile == 0:
                            @on("sync")
                            def _(e, dve_fin=dve_fin):
                                ctx.wait(e, "dve", dve_fin)
                                e.dma_start(out=dbg_c[:, 3584:4096], in_=dta[:, 0, :]).then_inc(*ctx.bump("dmo", 16))
                                e.dma_start(out=dbg_c[:, 1536:2048], in_=yscan[:, 0, :]).then_inc(*ctx.bump("dmo", 16))

                    with nc.sbuf_tensor(f"wob_{b}_{tile}", [128, 2, 2, D], BF16) as wob, \
                         nc.sbuf_tensor(f"obufE_{b}_{tile}", [128, 2, D], F32) as obuf, \
                         nc.psum_tensor(f"psE0_{b}_{tile}", [128, 512], F32) as psE0, \
                         nc.psum_tensor(f"psE1_{b}_{tile}", [128, 512], F32) as psE1, \
                         nc.psum_tensor(f"psE2_{b}_{tile}", [128, 512], F32) as psE2, \
                         nc.psum_tensor(f"psE3_{b}_{tile}", [128, 512], F32) as psE3, \
                         nc.psum_tensor(f"psE4_{b}_{tile}", [128, 512], F32) as psE4, \
                         nc.psum_tensor(f"psE5_{b}_{tile}", [128, 512], F32) as psE5, \
                         nc.psum_tensor(f"psE6_{b}_{tile}", [128, 512], F32) as psE6, \
                         nc.psum_tensor(f"psE7_{b}_{tile}", [128, 512], F32) as psE7:
                        psE = [psE0, psE1, psE2, psE3, psE4, psE5, psE6, psE7]
                        for c in range(DCH):
                            par = c % 2

                            osem = "dwo0" if par == 0 else "dwo1"

                            @on("sync")
                            def _(e, c=c, par=par, dve_fin=dve_fin, osem=osem):
                                ctx.wait(e, osem, ctx.val(osem))
                                if c == 0:
                                    ctx.wait(e, "dve", dve_fin)
                                pv = pe_wo_prev.get(par)
                                if pv is not None:
                                    ctx.wait(e, "pe", pv)
                                e.dma_start(out=wob[:, par, 0, :], in_=woh_s[c * 128:(c + 1) * 128, :]).then_inc(*ctx.bump(osem, 16))
                                e.dma_start(out=wob[:, par, 1, :], in_=wol_s[c * 128:(c + 1) * 128, :]).then_inc(*ctx.bump(osem, 16))
                            dmw_o = ctx.val(osem)

                            @on("tensor")
                            def _(e, c=c, par=par, dmw_o=dmw_o, dve_fin=dve_fin, osem=osem):
                                ctx.wait(e, osem, dmw_o)
                                if c == 0:
                                    ctx.wait(e, "dve", dve_fin)
                                yb = xcv[:, c, :].bitcast(BF16)
                                for mb in range(4):
                                    for (yl, wl) in ((0, 0), (0, 1), (1, 0)):
                                        lhs = yb[:, yl * T + mb * 128:yl * T + (mb + 1) * 128]
                                        for nb in range(2):
                                            mm = e.matmul(psE[mb * 2 + nb][:, :], lhs, wob[:, par, wl, nb * 512:(nb + 1) * 512],
                                                          start=(c == 0 and yl == 0 and wl == 0),
                                                          stop=(c == DCH - 1 and yl == 1))
                                ctx.inc(mm, "pe", 1)
                            pe_wo_prev[par] = ctx.val("pe")
                        pe_wo = ctx.val("pe")

                        dxo_prev = {}
                        for mb in range(4):
                            @on("sync")
                            def _(e, b=b, t0=t0, mb=mb):
                                ctx.wait(e, "dve", ctx.val("dve"))
                                e.dma_start(out=scr1[:, :], in_=x_in[b, t0 + mb * 128:t0 + (mb + 1) * 128, :]).then_inc(*ctx.bump("dma", 16))
                            dma_r = ctx.val("dma")

                            @on("vector")
                            def _(e, pe_wo=pe_wo, dma_r=dma_r, mb=mb, dxo_safe=dxo_prev.get(mb % 2)):
                                ctx.wait(e, "pe", pe_wo)
                                ctx.wait(e, "dma", dma_r)
                                if dxo_safe is not None:
                                    ctx.wait(e, "dxo", dxo_safe)
                                for nb in range(2):
                                    e.tensor_tensor(out=obuf[:, mb % 2, nb * 512:(nb + 1) * 512], in0=psE[mb * 2 + nb][:, :],
                                                    in1=scr1[:, nb * 512:(nb + 1) * 512], op=ALU.add)
                                e.engine_nop().then_inc(*ctx.bump("dve", 1))
                            dve_x2 = ctx.val("dve")

                            @on("sync")
                            def _(e, dve_x2=dve_x2, b=b, t0=t0, mb=mb):
                                ctx.wait(e, "dxo", ctx.val("dxo"))
                                ctx.wait(e, "dve", dve_x2)
                                e.dma_start(out=x2_s[b, t0 + mb * 128:t0 + (mb + 1) * 128, :], in_=obuf[:, mb % 2, :]).then_inc(*ctx.bump("dxo", 16))
                            dxo_prev[mb % 2] = ctx.val("dxo")

        with ExitStack() as ffn_es:
            wqr = ffn_es.enter_context(nc.sbuf_tensor("wqr", [128, DMCH, DH], BF16))
            k1T = ffn_es.enter_context(nc.sbuf_tensor("k1T", [128, DMCH, L], BF16))
            hlin = ffn_es.enter_context(nc.sbuf_tensor("hlin", [128, 2 + L], F32))
            hbuf = ffn_es.enter_context(nc.sbuf_tensor("hbuf", [128, L], F32))
            macc = ffn_es.enter_context(nc.sbuf_tensor("macc", [128, L], F32))
            i1bc = ffn_es.enter_context(nc.sbuf_tensor("i1bc", [128, L], F32))
            s2bc = ffn_es.enter_context(nc.sbuf_tensor("s2bc", [128, L], F32))
            k2b = ffn_es.enter_context(nc.sbuf_tensor("k2b", [128, 2, T], BF16))
            obufF = ffn_es.enter_context(nc.sbuf_tensor("obufF", [128, 2, D], F32))
            w2b = ffn_es.enter_context(nc.sbuf_tensor("w2b", [128, 2, D], BF16))

            for b in range(B):
                ctx.barrier(on, ("dxo", "dyo", "dho", "dma", "dmo"))

                with nc.psum_tensor(f"fpsbH_{b}", [128, 512], BF16) as psbH, \
                     nc.psum_tensor(f"fpsbL_{b}", [128, 512], BF16) as psbL:
                    for tt in range(L // 128):
                        @on("sync")
                        def _(e, b=b, tt=tt):
                            ctx.wait(e, "dve", ctx.val("dve"))
                            ctx.wait(e, "dma", ctx.val("dma"))
                            e.dma_start(out=scr1[:, :], in_=x2_s[b, tt * 128:(tt + 1) * 128, :]).then_inc(*ctx.bump("dma", 16))
                        dma_x = ctx.val("dma")

                        @on("vector")
                        def _(e, dma_x=dma_x):
                            ctx.wait(e, "dma", dma_x)
                            e.engine_nop().then_inc(*ctx.bump("dve", 1))

                        sce_ln2 = emit_ln(hlin[:, 0:1024], hlin[:, 1024:2048])

                        @on("vector")
                        def _(e, b=b, tt=tt, sce_ln2=sce_ln2):
                            ctx.wait(e, "sce", sce_ln2)
                            ctx.wait(e, "pe", ctx.val("pe"))
                            ln2 = hlin[:, 1024:2048]
                            e.reduce_max(out=scrs[:, 8:9], in_=ln2, axis=AX.X, apply_absolute_value=True)
                            e.drain()
                            e.tensor_single_scalar(out=scrs[:, 8:9], in_=scrs[:, 8:9], scalar=1e-5, op=ALU.max)
                            e.drain()
                            e.reciprocal(out=scrs[:, 9:10], in_=scrs[:, 8:9])
                            e.drain()
                            e.tensor_single_scalar(out=scrs[:, 9:10], in_=scrs[:, 9:10], scalar=127.0, op=ALU.mult)
                            e.drain()
                            e.tensor_scalar(out=hlin[:, 0:1024], in0=ln2, scalar1=scrs[:, 9:10], scalar2=RND,
                                            op0=ALU.mult, op1=ALU.add)
                            e.tensor_single_scalar(out=scrb[:, 0:1024], in_=hlin[:, 0:1024], scalar=RND, op=ALU.subtract)
                            e.tensor_tensor(out=scrs[:, 10:11], in0=scrs[:, 8:9], in1=k1c[:, :], op=ALU.mult)
                            e.drain(); ctx.fence_inc(e, "dve")
                        dve_k = ctx.val("dve")

                        @on("sync")
                        def _(e, dve_k=dve_k, b=b, tt=tt):
                            ctx.wait(e, "dve", dve_k)
                            e.dma_start(out=i1r_s[b, tt * 128:(tt + 1) * 128].unsqueeze(1), in_=scrs[:, 10:11]).then_inc(*ctx.bump("dma", 16))

                        for half in range(2):
                            pb = psbH if half == 0 else psbL

                            @on("tensor")
                            def _(e, dve_k=dve_k, half=half, pb=pb):
                                if half == 0:
                                    ctx.wait(e, "dve", dve_k)
                                ctx.wait(e, "sce", ctx.val("sce"))
                                for q in range(4):
                                    j = half * 4 + q
                                    mm = tpose(e, pb[:, q * 128:(q + 1) * 128], scrb[:, j * 128:(j + 1) * 128],
                                               idnb[:, :], q == 0, q == 3)
                                ctx.inc(mm, "pe", 1)
                            pe_t = ctx.val("pe")

                            @on("scalar")
                            def _(e, pe_t=pe_t, tt=tt, half=half, pb=pb):
                                ctx.wait(e, "pe", pe_t)
                                e.copy(out=k1T[:, half * 4:(half + 1) * 4, tt * 128:(tt + 1) * 128],
                                       in_=pb[:, :].rearrange("p (q x) -> p q x", x=128)).then_inc(*ctx.bump("sce", 1))

                if DEBUG_SCRATCH and b == 0:
                    @on("sync")
                    def _(e):
                        ctx.wait(e, "sce", ctx.val("sce"))
                        e.dma_start(out=dbg_k[:, :, :], in_=k1T[:, :, :]).then_inc(*ctx.bump("dmo", 16))

                @on("sync")
                def _(e, b=b):
                    ctx.wait(e, "pe", ctx.val("pe"))
                    ctx.wait(e, "dma", ctx.val("dma"))
                    ctx.wait(e, "dmw", ctx.val("dmw"))
                    for j in range(DMCH):
                        e.dma_start(out=wqr[:, j, :], in_=w1qt_s[j, :, :]).then_inc(*ctx.bump("dmw", 16))
                    e.dma_start(out=i1bc[:, :], in_=i1r_s[b, :].unsqueeze(0).partition_broadcast(128).squeeze(1)).then_inc(*ctx.bump("dma", 16))
                dmw_w1 = ctx.val("dmw")
                dma_i1 = ctx.val("dma")

                @on("vector")
                def _(e):
                    ctx.wait(e, "gp", ctx.val("gp"))
                    e.memset(hlin[:, 0:1], 0.0)
                    e.memset(hlin[:, 1 + L:2 + L], 0.0)
                    e.memset(macc[:], 0.0)
                    e.engine_nop().then_inc(*ctx.bump("dve", 1))

                with nc.psum_tensor(f"psH_{b}", [128, 4 * 512], F32) as psH:
                    sce_h_prev = None
                    for hc in range(HCH):
                        @on("tensor")
                        def _(e, hc=hc, dmw_w1=dmw_w1):
                            ctx.wait(e, "dmw", dmw_w1)
                            if hc == 0:
                                ctx.wait(e, "sce", ctx.val("sce"))
                            if hc > 0:
                                ctx.wait(e, "dve", ctx.val("dve"))
                            for nb in range(4):
                                for k in range(8):
                                    mm = e.matmul(psH[:, nb * 512:(nb + 1) * 512], wqr[:, k, hc * 128:(hc + 1) * 128],
                                                  k1T[:, k, nb * 512:(nb + 1) * 512], start=(k == 0), stop=(k == 7))
                            ctx.inc(mm, "pe", 1)
                        pe_h = ctx.val("pe")

                        @on("vector")
                        def _(e, pe_h=pe_h, dma_i1=dma_i1, sce_h_prev=sce_h_prev):
                            ctx.wait(e, "pe", pe_h)
                            ctx.wait(e, "dma", dma_i1)
                            ctx.wait(e, "dho", ctx.val("dho"))
                            if sce_h_prev is not None:
                                ctx.wait(e, "sce", sce_h_prev)
                            for nb in range(4):
                                e.tensor_tensor(out=hlin[:, 1 + nb * 512:1 + (nb + 1) * 512],
                                                in0=psH[:, nb * 512:(nb + 1) * 512],
                                                in1=i1bc[:, nb * 512:(nb + 1) * 512], op=ALU.mult)
                            z1 = scrs[:, 15:16]
                            e.scalar_tensor_tensor(out=hbuf[:, :], in0=hlin[:, 0:L], scalar=fcwt[:, hc, 0:1],
                                                   in1=z1.broadcast_to([128, L]), op0=ALU.mult, op1=ALU.add)
                            e.scalar_tensor_tensor(out=hbuf[:, :], in0=hlin[:, 1:1 + L], scalar=fcwt[:, hc, 1:2],
                                                   in1=hbuf[:, :], op0=ALU.mult, op1=ALU.add)
                            e.scalar_tensor_tensor(out=hbuf[:, :], in0=hlin[:, 2:2 + L], scalar=fcwt[:, hc, 2:3],
                                                   in1=hbuf[:, :], op0=ALU.mult, op1=ALU.add); ctx.fence_inc(e, "dve")
                        dve_cv = ctx.val("dve")

                        @on("scalar")
                        def _(e, dve_cv=dve_cv, hc=hc):
                            ctx.wait(e, "dve", dve_cv)
                            e.activation(out=hbuf[:, :], in_=hbuf[:, :], func=ACT.Silu, bias=fcbt[:, hc:hc + 1])
                            e.activation(out=hlin[:, 1:1 + L], in_=hbuf[:, :], func=ACT.Abs).then_inc(*ctx.bump("sce", 1))
                        sce_h = ctx.val("sce")
                        sce_h_prev = sce_h

                        @on("vector")
                        def _(e, sce_h=sce_h):
                            ctx.wait(e, "sce", sce_h)
                            e.tensor_tensor(out=macc[:, :], in0=macc[:, :], in1=hlin[:, 1:1 + L], op=ALU.max); ctx.fence_inc(e, "dve")

                        @on("sync")
                        def _(e, sce_h=sce_h, hc=hc):
                            ctx.wait(e, "dho", ctx.val("dho"))
                            ctx.wait(e, "sce", sce_h)
                            e.dma_start(out=h_scr[hc * 128:(hc + 1) * 128, :], in_=hbuf[:, :]).then_inc(*ctx.bump("dho", 16))

                with nc.psum_tensor(f"psM_{b}", [128, 512], F32) as psM:
                    dve_red = ctx.val("dve")
                    for blk in range(4):
                        @on("tensor")
                        def _(e, blk=blk, dve_red=dve_red):
                            if blk == 0:
                                ctx.wait(e, "dve", ctx.val("dve"))
                            else:
                                ctx.wait(e, "dve", ctx.val("dve"))
                            for q in range(4):
                                it = blk * 4 + q
                                mm = tpose(e, psM[:, q * 128:(q + 1) * 128], macc[:, it * 128:(it + 1) * 128],
                                           idn[:, :], q == 0, q == 3)
                            ctx.inc(mm, "pe", 1)
                        pe_m = ctx.val("pe")

                        @on("vector")
                        def _(e, pe_m=pe_m, blk=blk):
                            ctx.wait(e, "pe", pe_m)
                            for q in range(4):
                                it = blk * 4 + q
                                e.reduce_max(out=inv2t[:, it:it + 1], in_=psM[:, q * 128:(q + 1) * 128], axis=AX.X)
                            e.drain()
                            e.engine_nop().then_inc(*ctx.bump("dve", 1))

                    @on("vector")
                    def _(e, b=b):
                        e.drain()
                        e.tensor_single_scalar(out=inv2t[:, :], in_=inv2t[:, :], scalar=1e-5, op=ALU.max)
                        e.drain()
                        e.reciprocal(out=scr1[:, 0:16], in_=inv2t[:, :])
                        e.drain()
                        e.tensor_single_scalar(out=scr1[:, 0:16], in_=scr1[:, 0:16], scalar=127.0, op=ALU.mult)
                        e.tensor_tensor(out=inv2t[:, :], in0=inv2t[:, :],
                                        in1=k2c[:, :].broadcast_to([128, 16]), op=ALU.mult)
                        e.drain(); ctx.fence_inc(e, "dve")
                    dve_s2 = ctx.val("dve")

                    @on("tensor")
                    def _(e, dve_s2=dve_s2):
                        ctx.wait(e, "dve", dve_s2)
                        mm = tpose(e, psM[0:16, 0:128], scr1[:, 0:16], idn[:, :], True, True)
                        ctx.inc(mm, "pe", 1)
                    pe_s2 = ctx.val("pe")

                    @on("scalar")
                    def _(e, pe_s2=pe_s2):
                        ctx.wait(e, "pe", pe_s2)
                        e.copy(out=scr1[0:16, 512:640], in_=psM[0:16, 0:128]).then_inc(*ctx.bump("sce", 1))
                    sce_s2 = ctx.val("sce")

                    @on("sync")
                    def _(e, sce_s2=sce_s2, b=b):
                        ctx.wait(e, "sce", sce_s2)
                        e.dma_start(out=s2r_s[b, :].rearrange("(it x) -> it x", x=128), in_=scr1[0:16, 512:640]).then_inc(*ctx.bump("dma", 16))
                    dma_s2w = ctx.val("dma")

                    @on("sync")
                    def _(e, dma_s2w=dma_s2w, b=b):
                        ctx.wait(e, "dma", dma_s2w)
                        e.dma_start(out=s2bc[:, :], in_=s2r_s[b, :].unsqueeze(0).partition_broadcast(128).squeeze(1)).then_inc(*ctx.bump("dma", 16))
                    dma_s2b = ctx.val("dma")

                with nc.psum_tensor(f"psO0_{b}", [128, 512], F32) as psO0, \
                     nc.psum_tensor(f"psO1_{b}", [128, 512], F32) as psO1, \
                     nc.psum_tensor(f"psO2_{b}", [128, 512], F32) as psO2, \
                     nc.psum_tensor(f"psO3_{b}", [128, 512], F32) as psO3, \
                     nc.psum_tensor(f"psO4_{b}", [128, 512], F32) as psO4, \
                     nc.psum_tensor(f"psO5_{b}", [128, 512], F32) as psO5, \
                     nc.psum_tensor(f"psO6_{b}", [128, 512], F32) as psO6, \
                     nc.psum_tensor(f"psO7_{b}", [128, 512], F32) as psO7:
                    psO = [psO0, psO1, psO2, psO3, psO4, psO5, psO6, psO7]
                    dve_evac_last = None
                    for tile in range(NT):
                        t0 = tile * T
                        pe_mm_prev = {}
                        for hc in range(HCH):
                            par = hc % 2

                            hsem = "dh0" if par == 0 else "dh1"
                            w2sem = "dw20" if par == 0 else "dw21"

                            @on("sync")
                            def _(e, hc=hc, par=par, t0=t0, dma_s2b=dma_s2b, hsem=hsem, w2sem=w2sem):
                                ctx.wait(e, hsem, ctx.val(hsem))
                                ctx.wait(e, w2sem, ctx.val(w2sem))
                                ctx.wait(e, "dho", ctx.val("dho"))
                                ctx.wait(e, "dma", dma_s2b)
                                pv = pe_mm_prev.get(par)
                                if pv is not None:
                                    ctx.wait(e, "pe", pv)
                                ctx.wait(e, "dve", ctx.val("dve"))
                                e.dma_start(out=hbuf[:, par * T:(par + 1) * T],
                                            in_=h_scr[hc * 128:(hc + 1) * 128, t0:t0 + T]).then_inc(*ctx.bump(hsem, 16))
                                e.dma_start(out=w2b[:, par, :], in_=w2qt_s[hc * 128:(hc + 1) * 128, :]).then_inc(*ctx.bump(w2sem, 16))
                            dma_h = ctx.val(hsem)
                            dmw_2 = ctx.val(w2sem)

                            @on("vector")
                            def _(e, dma_h=dma_h, par=par, t0=t0, pv=pe_mm_prev.get(par), hsem=hsem):
                                ctx.wait(e, hsem, dma_h)
                                if pv is not None:
                                    ctx.wait(e, "pe", pv)
                                e.tensor_tensor(out=scr1[:, 0:T], in0=hbuf[:, par * T:(par + 1) * T],
                                                in1=s2bc[:, t0:t0 + T], op=ALU.mult)
                                e.tensor_scalar(out=k2b[:, par, :], in0=scr1[:, 0:T], scalar1=RND, scalar2=RND,
                                                op0=ALU.add, op1=ALU.subtract); ctx.fence_inc(e, "dve")
                            dve_k2 = ctx.val("dve")

                            @on("tensor")
                            def _(e, dve_k2=dve_k2, dmw_2=dmw_2, hc=hc, par=par, tile=tile, dve_evac_last=dve_evac_last, w2sem=w2sem):
                                ctx.wait(e, "dve", dve_k2)
                                ctx.wait(e, w2sem, dmw_2)
                                if hc == 0 and tile > 0 and dve_evac_last is not None:
                                    ctx.wait(e, "dve", dve_evac_last)
                                for mb in range(4):
                                    for nb in range(2):
                                        mm = e.matmul(psO[mb * 2 + nb][:, :], k2b[:, par, mb * 128:(mb + 1) * 128],
                                                      w2b[:, par, nb * 512:(nb + 1) * 512],
                                                      start=(hc == 0), stop=(hc == HCH - 1))
                                ctx.inc(mm, "pe", 1)
                            pe_mm_prev[par] = ctx.val("pe")
                        pe_o = ctx.val("pe")

                        dyo_prev = {}
                        for mb in range(4):
                            @on("sync")
                            def _(e, b=b, t0=t0, mb=mb):
                                ctx.wait(e, "dve", ctx.val("dve"))
                                e.dma_start(out=scr1[:, :], in_=x2_s[b, t0 + mb * 128:t0 + (mb + 1) * 128, :]).then_inc(*ctx.bump("dma", 16))
                            dma_r = ctx.val("dma")

                            @on("vector")
                            def _(e, pe_o=pe_o, dma_r=dma_r, mb=mb, tile=tile, dyo_safe=dyo_prev.get(mb % 2)):
                                ctx.wait(e, "pe", pe_o)
                                ctx.wait(e, "dma", dma_r)
                                if dyo_safe is not None:
                                    ctx.wait(e, "dyo", dyo_safe)
                                for nb in range(2):
                                    e.scalar_tensor_tensor(out=obufF[:, mb % 2, nb * 512:(nb + 1) * 512], in0=psO[mb * 2 + nb][:, :],
                                                           scalar=inv2t[:, tile * 4 + mb:tile * 4 + mb + 1],
                                                           in1=scr1[:, nb * 512:(nb + 1) * 512], op0=ALU.mult, op1=ALU.add)
                                e.engine_nop().then_inc(*ctx.bump("dve", 1))
                            dve_o = ctx.val("dve")
                            dve_evac_last = dve_o

                            @on("sync")
                            def _(e, dve_o=dve_o, b=b, t0=t0, mb=mb):
                                ctx.wait(e, "dyo", ctx.val("dyo"))
                                ctx.wait(e, "dve", dve_o)
                                e.dma_start(out=y_out[b, t0 + mb * 128:t0 + (mb + 1) * 128, :], in_=obufF[:, mb % 2, :]).then_inc(*ctx.bump("dyo", 16))
                            dyo_prev[mb % 2] = ctx.val("dyo")

        @on("sync")
        def _(e):
            for s_ in ("dyo", "dxo", "dho", "dmo", "dma", "dmw",
                       "dwx0", "dwx1", "dwo0", "dwo1", "dh0", "dh1", "dw20", "dw21"):
                ctx.wait(e, s_, ctx.val(s_))

    return nc


def kernel(L0_attn_hidden_states, L0_attn_W_in, L0_attn_conv_w, L0_attn_conv_b,
           L0_attn_W_x, L0_attn_W_dt, L0_attn_b_dt, L0_attn_A_log, L0_attn_D,
           L0_attn_W_out, L0_ffn_w1, L0_ffn_conv_w, L0_ffn_conv_b, L0_ffn_w2):
    x = np.ascontiguousarray(np.asarray(L0_attn_hidden_states, dtype=np.float32))
    weights = dict(
        w_in=np.ascontiguousarray(np.asarray(L0_attn_W_in, dtype=np.float32)),
        conv_w=np.ascontiguousarray(np.asarray(L0_attn_conv_w, dtype=np.float32)),
        conv_b=np.ascontiguousarray(np.asarray(L0_attn_conv_b, dtype=np.float32)),
        w_x=np.ascontiguousarray(np.asarray(L0_attn_W_x, dtype=np.float32)),
        w_dt=np.ascontiguousarray(np.asarray(L0_attn_W_dt, dtype=np.float32)),
        b_dt=np.ascontiguousarray(np.asarray(L0_attn_b_dt, dtype=np.float32)),
        a_log=np.ascontiguousarray(np.asarray(L0_attn_A_log, dtype=np.float32)),
        d_vec=np.ascontiguousarray(np.asarray(L0_attn_D, dtype=np.float32)),
        w_out=np.ascontiguousarray(np.asarray(L0_attn_W_out, dtype=np.float32)),
        f_w1=np.ascontiguousarray(np.asarray(L0_ffn_w1, dtype=np.float32)),
        f_cw=np.ascontiguousarray(np.asarray(L0_ffn_conv_w, dtype=np.float32).reshape(DH, 3)),
        f_cb=np.ascontiguousarray(np.asarray(L0_ffn_conv_b, dtype=np.float32)),
        f_w2=np.ascontiguousarray(np.asarray(L0_ffn_w2, dtype=np.float32)),
    )
    nc = build()
    in_maps = []
    for i in range(8):
        m = dict(weights)
        m["x_in"] = x[2 * i:2 * i + 2]
        in_maps.append(m)
    res = run_bass_kernel_spmd(nc, in_maps, core_ids=list(range(8)))
    return np.concatenate([r["y_out"] for r in res.results], axis=0)


# version 10
# speedup vs baseline: 1.1139x; 1.0405x over previous
import numpy as np
from contextlib import ExitStack

import concourse.bass as bass
import concourse.mybir as mybir
from concourse.bass_utils import run_bass_kernel_spmd

F32 = mybir.dt.float32
BF16 = mybir.dt.bfloat16
ALU = mybir.AluOpType
ACT = mybir.ActivationFunctionType
AX = mybir.AxisListType

B = 2
L = 2048
D = 1024
DI = 2048
S = 16
R = 64
DH = 4096
T = 512
NT = L // T
DCH = DI // 128
DMCH = D // 128
HCH = DH // 128
NSUB = T // 128
SG = 4
SGS = S // SG
RND = 12582912.0
EPS = 1e-5

DEV_DRAINS = False
DEBUG_SCRATCH = False

_DRAIN_OPS = frozenset((
    "tensor_tensor", "tensor_copy", "tensor_scalar", "tensor_single_scalar",
    "scalar_tensor_tensor", "tensor_tensor_scan", "tensor_reduce",
    "reduce_sum", "reduce_max", "memset", "activation", "copy",
    "reciprocal", "iota", "tensor_relu", "tensor_add", "tensor_mul",
    "tensor_sub", "tensor_max", "tensor_scalar_max",
))


class _DrainProxy:
    def __init__(self, eng):
        self._e = eng

    def __getattr__(self, name):
        f = getattr(self._e, name)
        if name in _DRAIN_OPS:
            def g(*a, **k):
                r = f(*a, **k)
                self._e.drain()
                return r
            return g
        return f


class Ctx:
    def __init__(self, nc, block, es):
        self.nc = nc
        self.block = block
        self.es = es
        self.sems = {}

    def mksem(self, *names):
        for n in names:
            self.sems[n] = [self.es.enter_context(self.nc.semaphore(n)), 0]

    def sem(self, name):
        return self.sems[name][0]

    def fence_inc(self, e, name, n=1):
        s = self.sems[name]
        s[1] += n
        e.engine_nop().then_inc(s[0], n)

    def bump(self, name, n=1):
        s = self.sems[name]
        s[1] += n
        return (s[0], n)

    def inc(self, instr, name, n=1):
        s = self.sems[name]
        s[1] += n
        instr.then_inc(s[0], n)
        return s[1]

    def val(self, name):
        return self.sems[name][1]

    def barrier(self, on, store_sems):
        @on("sync")
        def _(e):
            for s_ in store_sems:
                self.wait(e, s_, self.val(s_))
            self.sems["tileb"][1] += 1
            e.sem_inc(self.sems["tileb"][0], 1)
        tb = self.val("tileb")
        for eng in ("vector", "scalar", "tensor", "gpsimd"):
            @on(eng)
            def _(e, tb=tb):
                self.wait(e, "tileb", tb)

    def wait(self, eng, name, v=None):
        s = self.sems[name]
        eng.wait_ge(s[0], s[1] if v is None else v)


def build():
    nc = bass.Bass("TRN2", target_bir_lowering=False,
                   detect_race_conditions=DEV_DRAINS)

    _scr_kind2 = "ExternalOutput" if DEBUG_SCRATCH else "Internal"
    x_in = nc.dram_tensor("x_in", [B, L, D], F32, kind="ExternalInput").ap()
    w_in = nc.dram_tensor("w_in", [D, 2 * DI], F32, kind="ExternalInput").ap()
    conv_w = nc.dram_tensor("conv_w", [4, DI], F32, kind="ExternalInput").ap()
    conv_b = nc.dram_tensor("conv_b", [DI], F32, kind="ExternalInput").ap()
    w_x = nc.dram_tensor("w_x", [DI, R + 2 * S], F32, kind="ExternalInput").ap()
    w_dt = nc.dram_tensor("w_dt", [R, DI], F32, kind="ExternalInput").ap()
    b_dt = nc.dram_tensor("b_dt", [DI], F32, kind="ExternalInput").ap()
    a_log = nc.dram_tensor("a_log", [DI, S], F32, kind="ExternalInput").ap()
    d_vec = nc.dram_tensor("d_vec", [DI], F32, kind="ExternalInput").ap()
    w_out = nc.dram_tensor("w_out", [DI, D], F32, kind="ExternalInput").ap()
    f_w1 = nc.dram_tensor("f_w1", [DH, D], F32, kind="ExternalInput").ap()
    f_cw = nc.dram_tensor("f_cw", [DH, 3], F32, kind="ExternalInput").ap()
    f_cb = nc.dram_tensor("f_cb", [DH], F32, kind="ExternalInput").ap()
    f_w2 = nc.dram_tensor("f_w2", [D, DH], F32, kind="ExternalInput").ap()
    y_out = nc.dram_tensor("y_out", [B, L, D], F32, kind="ExternalOutput").ap()

    _scr_kind0 = "ExternalOutput" if DEBUG_SCRATCH else "Internal"
    wih_s = nc.dram_tensor("wih_s", [2 * DI // 128, 8, 128, 128], BF16, kind=_scr_kind0).ap()
    wil_s = nc.dram_tensor("wil_s", [2 * DI // 128, 8, 128, 128], BF16, kind=_scr_kind0).ap()
    woh_s = nc.dram_tensor("woh_s", [DI, D], BF16, kind="Internal").ap()
    wol_s = nc.dram_tensor("wol_s", [DI, D], BF16, kind="Internal").ap()
    w1qt_s = nc.dram_tensor("w1qt_s", [DMCH, 128, DH], BF16, kind=_scr_kind2).ap()
    w2qt_s = nc.dram_tensor("w2qt_s", [DH, D], BF16, kind=_scr_kind2).ap()
    bc_s = nc.dram_tensor("bc_s", [2, 2 * S, T], F32, kind="Internal").ap()
    _scr_kind = "ExternalOutput" if DEBUG_SCRATCH else "Internal"
    x2_s = nc.dram_tensor("x2_s", [B, L, D], F32, kind=_scr_kind).ap()
    h_scr = nc.dram_tensor("h_scr", [DH, L], F32, kind=_scr_kind).ap()
    dbg_c = nc.dram_tensor("dbg_c", [128, 4096], F32, kind=_scr_kind).ap() if DEBUG_SCRATCH else None
    dbg_b = nc.dram_tensor("dbg_b", [128, 8192], BF16, kind=_scr_kind).ap() if DEBUG_SCRATCH else None
    dbg_k = nc.dram_tensor("dbg_k", [128, DMCH, L], BF16, kind=_scr_kind).ap() if DEBUG_SCRATCH else None
    s2r_s = nc.dram_tensor("s2r_s", [B, L], F32, kind="Internal").ap()
    i1r_s = nc.dram_tensor("i1r_s", [B, L], F32, kind="Internal").ap()

    es = ExitStack()
    with es:
        def sb(name, shape, dt=F32):
            return es.enter_context(nc.sbuf_tensor(name, shape, dt))

        wxt = sb("wxt", [128, DCH, 96])
        wdtt = sb("wdtt", [64, DI])
        cwt = sb("cwt", [128, DCH, 4])
        cbt = sb("cbt", [128, DCH])
        bdtt = sb("bdtt", [128, DCH])
        dvt = sb("dvt", [128, DCH])
        ang = sb("ang", [128, DCH, S])
        fcwt = sb("fcwt", [128, HCH, 3])
        fcbt = sb("fcbt", [128, HCH])
        idn = sb("idn", [128, 128])
        idnb = sb("idnb", [128, 128], BF16)
        onescol = sb("onescol", [1, 128])
        ones128 = sb("ones128", [128, 1])
        k1c = sb("k1c", [128, 1])
        k2c = sb("k2c", [128, 1])
        sw1b = sb("sw1b", [128, 1])
        sw2b = sb("sw2b", [128, 1])
        hstate = sb("hstate", [128, DCH, S])
        xtail = sb("xtail", [128, DCH, 3])
        scr1 = sb("scr1", [128, 1024])
        scrb = sb("scrb", [128, 2048], BF16)
        scrs = sb("scrs", [128, 16])
        cst = sb("cst", [128, 2])
        projt = sb("projt", [96, T])
        inv2t = sb("inv2t", [128, 16])

        block = es.enter_context(nc.Block())
        ctx = Ctx(nc, block, es)
        ctx.mksem("dmw", "dma", "dmo", "pe", "dve", "sce", "gp",
                  "dwx0", "dwx1", "dwo0", "dwo1", "dh0", "dh1", "dw20", "dw21",
                  "dho", "dxo", "dyo", "tileb")

        def on(name):
            dec = getattr(block, name)
            if not DEV_DRAINS or name in ("sync", "tensor"):
                return dec

            def wrap(fn):
                return dec(lambda e: fn(_DrainProxy(e)))
            return wrap

        SET_LNEXP = 6
        SET_SILU = 18

        def act_set(e, set_id):
            e.drain()
            e.add_instruction(mybir.InstLoadActFuncSet(
                name=nc.get_next_instruction_name(), act_func_set_id=set_id, ins=[], outs=[]))
            e.drain()

        def tpose(e, out, in_, ident, start, stop):
            return e.matmul(out, in_, ident, is_transpose=True, start=start, stop=stop)

        def emit_ln(sqbuf, lnfbuf):
            @on("vector")
            def _(e):
                e.reduce_sum(out=scrs[:, 0:1], in_=scr1[:, :], axis=AX.X); ctx.fence_inc(e, "dve")
            dve_s = ctx.val("dve")

            @on("scalar")
            def _(e, dve_s=dve_s):
                ctx.wait(e, "dve", dve_s)
                e.activation(out=sqbuf, in_=scr1[:, :], func=ACT.Square).then_inc(*ctx.bump("sce", 1))
            sce_sq = ctx.val("sce")

            @on("vector")
            def _(e, sce_sq=sce_sq):
                ctx.wait(e, "sce", sce_sq)
                e.reduce_sum(out=scrs[:, 1:2], in_=sqbuf, axis=AX.X)
                e.drain()
                e.tensor_single_scalar(out=scrs[:, 2:3], in_=scrs[:, 0:1], scalar=1.0 / D, op=ALU.mult)
                e.tensor_single_scalar(out=scrs[:, 3:4], in_=scrs[:, 1:2], scalar=1.0 / D, op=ALU.mult)
                e.drain()
                e.tensor_tensor(out=scrs[:, 4:5], in0=scrs[:, 2:3], in1=scrs[:, 2:3], op=ALU.mult)
                e.drain()
                e.tensor_tensor(out=scrs[:, 4:5], in0=scrs[:, 3:4], in1=scrs[:, 4:5], op=ALU.subtract)
                e.drain(); ctx.fence_inc(e, "dve")
            dve_v = ctx.val("dve")

            @on("scalar")
            def _(e, dve_v=dve_v):
                ctx.wait(e, "dve", dve_v)
                e.activation(out=scrs[:, 5:6], in_=scrs[:, 4:5], func=ACT.Ln, bias=cst[:, 0:1])
                e.drain()
                e.activation(out=scrs[:, 6:7], in_=scrs[:, 5:6], func=ACT.Exp, scale=-0.5)
                e.drain()
                e.activation(out=scrs[:, 6:7], in_=scrs[:, 6:7], func=ACT.Identity).then_inc(*ctx.bump("sce", 1))
            sce_r = ctx.val("sce")

            @on("vector")
            def _(e, sce_r=sce_r):
                ctx.wait(e, "sce", sce_r)
                e.tensor_tensor(out=scrs[:, 7:8], in0=scrs[:, 2:3], in1=scrs[:, 6:7], op=ALU.mult)
                e.drain()
                e.tensor_single_scalar(out=scrs[:, 7:8], in_=scrs[:, 7:8], scalar=-1.0, op=ALU.mult)
                e.drain(); ctx.fence_inc(e, "dve")
            dve_mb = ctx.val("dve")

            @on("scalar")
            def _(e, dve_mb=dve_mb):
                ctx.wait(e, "dve", dve_mb)
                e.activation(out=lnfbuf, in_=scr1[:, :], func=ACT.Identity,
                             scale=scrs[:, 6:7], bias=scrs[:, 7:8]).then_inc(*ctx.bump("sce", 1))
            return ctx.val("sce")

        with nc.sbuf_tensor("pscrA", [128, DH], F32) as pscrA, \
             nc.sbuf_tensor("pscrB", [128, DH], BF16) as pscrB, \
             nc.sbuf_tensor("pscrC", [128, DH], BF16) as pscrC, \
             nc.sbuf_tensor("pscrA2", [128, DH], F32) as pscrA2, \
             nc.sbuf_tensor("pscrB2", [128, DH], BF16) as pscrB2, \
             nc.sbuf_tensor("pscrC2", [128, DH], BF16) as pscrC2, \
             nc.sbuf_tensor("pscrM0", [128, DH], F32) as pscrM0, \
             nc.sbuf_tensor("pscrM1", [128, DH], F32) as pscrM1, \
             nc.sbuf_tensor("stag", [128, HCH, 128], BF16) as stag, \
             nc.sbuf_tensor("stag2", [128, HCH, 128], BF16) as stag2, \
             nc.sbuf_tensor("cw4", [4, DI], F32) as cw4, \
             nc.sbuf_tensor("vec3", [1, 3 * DI + DH], F32) as vec3, \
             nc.sbuf_tensor("msum", [128, 2], F32) as msum, \
             nc.sbuf_tensor("smv", [1, 8], F32) as smv, \
             nc.psum_tensor("pps0", [128, 512], F32) as pps0, \
             nc.psum_tensor("ppsb0", [128, 512], BF16) as ppsb0, \
             nc.psum_tensor("ppsb1", [128, 512], BF16) as ppsb1:

            @on("sync")
            def _(e):
                e.dma_start(out=wxt[:, :, :], in_=w_x.rearrange("(c p) m -> p c m", p=128)).then_inc(*ctx.bump("dmw", 16))
                e.dma_start(out=wdtt[:, :], in_=w_dt).then_inc(*ctx.bump("dmw", 16))
                e.dma_start(out=ang[:, :, :], in_=a_log.rearrange("(c p) s -> p c s", p=128)).then_inc(*ctx.bump("dmw", 16))
                e.dma_start(out=fcwt[:, :, :], in_=f_cw.rearrange("(c p) k -> p c k", p=128)).then_inc(*ctx.bump("dmw", 16))
                e.dma_start(out=cw4[:, :], in_=conv_w).then_inc(*ctx.bump("dmw", 16))
                e.dma_start(out=vec3[:, 0:DI], in_=conv_b.unsqueeze(0)).then_inc(*ctx.bump("dmw", 16))
                e.dma_start(out=vec3[:, DI:2 * DI], in_=b_dt.unsqueeze(0)).then_inc(*ctx.bump("dmw", 16))
                e.dma_start(out=vec3[:, 2 * DI:3 * DI], in_=d_vec.unsqueeze(0)).then_inc(*ctx.bump("dmw", 16))
                e.dma_start(out=vec3[:, 3 * DI:3 * DI + DH], in_=f_cb.unsqueeze(0)).then_inc(*ctx.bump("dmw", 16))
            dmw0 = ctx.val("dmw")

            @on("gpsimd")
            def _(e):
                e.iota(out=scr1[:, 0:128], pattern=[[-1, 128]], base=0,
                       channel_multiplier=1, allow_small_or_imprecise_dtypes=True); ctx.fence_inc(e, "gp")
            gp_i = ctx.val("gp")

            @on("vector")
            def _(e, gp_i=gp_i, dmw0=dmw0):
                ctx.wait(e, "gp", gp_i)
                e.tensor_single_scalar(out=idn[:], in_=scr1[:, 0:128], scalar=0.0, op=ALU.is_equal)
                e.tensor_copy(idnb[:], idn[:])
                e.memset(onescol[:], 1.0)
                e.memset(ones128[:], 1.0)
                e.memset(scrs[:, 15:16], 0.0)
                e.memset(cst[:, 0:1], EPS)
                e.memset(cst[:, 1:2], 1.0)
                ctx.wait(e, "dmw", dmw0)
                e.engine_nop().then_inc(*ctx.bump("dve", 1))
            dve0 = ctx.val("dve")

            @on("scalar")
            def _(e, dve0=dve0):
                ctx.wait(e, "dve", dve0)
                e.activation(out=ang[:, :, :], in_=ang[:, :, :], func=ACT.Exp, scale=1.0).then_inc(*ctx.bump("sce", 1))
            sce_a0 = ctx.val("sce")

            @on("vector")
            def _(e, sce_a0=sce_a0):
                ctx.wait(e, "sce", sce_a0)
                e.tensor_single_scalar(out=ang[:, :, :], in_=ang[:, :, :], scalar=-1.0, op=ALU.mult); ctx.fence_inc(e, "dve")
            dve_ang = ctx.val("dve")

            @on("tensor")
            def _(e, dve_ang=dve_ang):
                ctx.wait(e, "dve", dve_ang)
                n_tot = DCH + DCH + DCH + DCH + HCH
                k = 0
                for c in range(DCH):
                    tpose(e, pps0[:, c * 4:(c + 1) * 4], cw4[:, c * 128:(c + 1) * 128], idn[0:4, 0:4], k == 0, k == n_tot - 1); k += 1
                for c in range(DCH):
                    tpose(e, pps0[:, 64 + c:65 + c], vec3[:, c * 128:(c + 1) * 128], idn[0:1, 0:1], k == 0, k == n_tot - 1); k += 1
                for c in range(DCH):
                    tpose(e, pps0[:, 80 + c:81 + c], vec3[:, DI + c * 128:DI + (c + 1) * 128], idn[0:1, 0:1], k == 0, k == n_tot - 1); k += 1
                for c in range(DCH):
                    tpose(e, pps0[:, 96 + c:97 + c], vec3[:, 2 * DI + c * 128:2 * DI + (c + 1) * 128], idn[0:1, 0:1], k == 0, k == n_tot - 1); k += 1
                for c in range(HCH):
                    mm = tpose(e, pps0[:, 112 + c:113 + c], vec3[:, 3 * DI + c * 128:3 * DI + (c + 1) * 128], idn[0:1, 0:1], k == 0, k == n_tot - 1); k += 1
                ctx.inc(mm, "pe", 1)
            pe_tp = ctx.val("pe")

            @on("scalar")
            def _(e, pe_tp=pe_tp):
                ctx.wait(e, "pe", pe_tp)
                e.copy(out=cwt[:, :, :], in_=pps0[:, 0:64].rearrange("p (c k) -> p c k", k=4))
                e.copy(out=cbt[:, :], in_=pps0[:, 64:80])
                e.copy(out=bdtt[:, :], in_=pps0[:, 80:96])
                e.copy(out=dvt[:, :], in_=pps0[:, 96:112])
                e.copy(out=fcbt[:, :], in_=pps0[:, 112:144]).then_inc(*ctx.bump("sce", 1))

            wsp_dve = {}
            wsp_st = {}
            gci = 0
            for (src, dsth, dstl, rows, cols) in ((w_in, wih_s, wil_s, D, 2 * DI),
                                                  (w_out, woh_s, wol_s, DI, D)):
                for j in range(rows // 128):
                    par = gci % 2
                    pA = pscrA if par == 0 else pscrA2
                    pB = pscrB if par == 0 else pscrB2
                    pC = pscrC if par == 0 else pscrC2

                    @on("sync")
                    def _(e, src=src, j=j, cols=cols, pA=pA, w_dve=wsp_dve.get(gci - 2)):
                        if w_dve is not None:
                            ctx.wait(e, "dve", w_dve)
                        e.dma_start(out=pA[:, 0:cols], in_=src[j * 128:(j + 1) * 128, :]).then_inc(*ctx.bump("dma", 16))
                    dma_t = ctx.val("dma")

                    @on("scalar")
                    def _(e, dma_t=dma_t, cols=cols, pA=pA, pB=pB, w_st=wsp_st.get(gci - 2)):
                        ctx.wait(e, "dma", dma_t)
                        if w_st is not None:
                            ctx.wait(e, "dmo", w_st)
                        e.copy(out=pB[:, 0:cols], in_=pA[:, 0:cols]).then_inc(*ctx.bump("sce", 1))
                    sce_t = ctx.val("sce")

                    @on("vector")
                    def _(e, sce_t=sce_t, cols=cols, pA=pA, pB=pB, pC=pC, w_st=wsp_st.get(gci - 2)):
                        ctx.wait(e, "sce", sce_t)
                        if w_st is not None:
                            ctx.wait(e, "dmo", w_st)
                        e.tensor_tensor(out=pC[:, 0:cols], in0=pA[:, 0:cols], in1=pB[:, 0:cols], op=ALU.subtract); ctx.fence_inc(e, "dve")
                    dve_t = ctx.val("dve")
                    wsp_dve[gci] = dve_t

                    @on("gpsimd")
                    def _(e, dve_t=dve_t, dsth=dsth, dstl=dstl, j=j, cols=cols, rows=rows, pB=pB, pC=pC):
                        ctx.wait(e, "dve", dve_t)
                        if rows == D:
                            e.dma_start(out=dsth[:, j, :, :].transpose([1, 0, 2]),
                                        in_=pB[:, 0:cols].rearrange("p (cc m) -> p cc m", m=128)).then_inc(*ctx.bump("dmo", 16))
                            e.dma_start(out=dstl[:, j, :, :].transpose([1, 0, 2]),
                                        in_=pC[:, 0:cols].rearrange("p (cc m) -> p cc m", m=128)).then_inc(*ctx.bump("dmo", 16))
                        else:
                            e.dma_start(out=dsth[j * 128:(j + 1) * 128, :], in_=pB[:, 0:cols]).then_inc(*ctx.bump("dmo", 16))
                            e.dma_start(out=dstl[j * 128:(j + 1) * 128, :], in_=pC[:, 0:cols]).then_inc(*ctx.bump("dmo", 16))
                    wsp_st[gci] = ctx.val("dmo")
                    gci += 1

            @on("vector")
            def _(e):
                ctx.wait(e, "dve", ctx.val("dve"))
                e.memset(msum[:], 0.0)
                e.engine_nop().then_inc(*ctx.bump("dve", 1))

            ma_dve = {}
            mj = 0
            for col_ix, (src, nch, cols) in enumerate(((f_w1, HCH, D), (f_w2, DMCH, DH))):
                for j in range(nch):
                    pM = pscrM0 if mj % 2 == 0 else pscrM1

                    @on("sync")
                    def _(e, src=src, j=j, cols=cols, pM=pM, w_dve=ma_dve.get(mj - 2)):
                        if w_dve is not None:
                            ctx.wait(e, "dve", w_dve)
                        e.dma_start(out=pM[:, 0:cols], in_=src[j * 128:(j + 1) * 128, :]).then_inc(*ctx.bump("dma", 16))
                    dma_t = ctx.val("dma")

                    @on("vector")
                    def _(e, dma_t=dma_t, cols=cols, col_ix=col_ix, pM=pM):
                        ctx.wait(e, "dma", dma_t)
                        e.reduce_sum(out=scrs[:, 0:1], in_=pM[:, 0:cols], axis=AX.X, apply_absolute_value=True)
                        e.drain()
                        e.tensor_tensor(out=msum[:, col_ix:col_ix + 1], in0=msum[:, col_ix:col_ix + 1],
                                        in1=scrs[:, 0:1], op=ALU.add)
                        e.drain(); ctx.fence_inc(e, "dve")
                    ma_dve[mj] = ctx.val("dve")
                    mj += 1
            dve_ms = ctx.val("dve")

            @on("tensor")
            def _(e, dve_ms=dve_ms):
                ctx.wait(e, "dve", dve_ms)
                e.matmul(pps0[0:1, 0:1], msum[:, 0:1], ones128[:, :], start=True, stop=False)
                mm = e.matmul(pps0[0:1, 1:2], msum[:, 1:2], ones128[:, :], start=False, stop=True)
                ctx.inc(mm, "pe", 1)
            pe_ms = ctx.val("pe")

            @on("vector")
            def _(e, pe_ms=pe_ms):
                ctx.wait(e, "pe", pe_ms)
                e.tensor_copy(smv[:, 0:2], pps0[0:1, 0:2])
                e.drain()
                e.tensor_single_scalar(out=smv[:, 2:3], in_=smv[:, 0:1], scalar=1.0 / (DH * D), op=ALU.mult)
                e.tensor_single_scalar(out=smv[:, 3:4], in_=smv[:, 1:2], scalar=1.0 / (DH * D), op=ALU.mult)
                e.drain()
                e.tensor_single_scalar(out=smv[:, 2:3], in_=smv[:, 2:3], scalar=1e-5, op=ALU.max)
                e.tensor_single_scalar(out=smv[:, 3:4], in_=smv[:, 3:4], scalar=1e-5, op=ALU.max)
                e.drain()
                e.reciprocal(out=smv[:, 4:5], in_=smv[:, 2:3])
                e.reciprocal(out=smv[:, 5:6], in_=smv[:, 3:4])
                e.tensor_single_scalar(out=smv[:, 6:7], in_=smv[:, 2:3], scalar=1.0 / 127.0, op=ALU.mult)
                e.tensor_single_scalar(out=smv[:, 7:8], in_=smv[:, 3:4], scalar=1.0 / 127.0, op=ALU.mult)
                e.drain(); ctx.fence_inc(e, "dve")
            dve_sw = ctx.val("dve")

            @on("tensor")
            def _(e, dve_sw=dve_sw):
                ctx.wait(e, "dve", dve_sw)
                mm = e.matmul(pps0[:, 0:8], onescol[:, :], smv[:, 0:8], start=True, stop=True)
                ctx.inc(mm, "pe", 1)
            pe_bc = ctx.val("pe")

            @on("scalar")
            def _(e, pe_bc=pe_bc):
                ctx.wait(e, "pe", pe_bc)
                e.copy(out=sw1b[:, :], in_=pps0[:, 4:5])
                e.copy(out=sw2b[:, :], in_=pps0[:, 5:6])
                e.copy(out=k1c[:, :], in_=pps0[:, 6:7])
                e.copy(out=k2c[:, :], in_=pps0[:, 7:8]).then_inc(*ctx.bump("sce", 1))
            sce_sw = ctx.val("sce")

            tern_dve = {}
            tern_pe = {}
            tern_st = {}
            bank_evac = {}
            tj = 0
            for wi, (src, swb, nch, cols, dst) in enumerate(
                    ((f_w1, sw1b, HCH, D, w1qt_s), (f_w2, sw2b, DMCH, DH, w2qt_s))):
                njt = cols // 128
                for j in range(nch):
                    par = tj % 2
                    pA = pscrA if par == 0 else pscrA2
                    pB = pscrB if par == 0 else pscrB2
                    pC = pscrC if par == 0 else pscrC2
                    stg = stag if par == 0 else stag2
                    w_dve = tern_dve.get(tj - 2, wsp_dve[22 + par])
                    w_st0 = wsp_st[22 + par] if tj < 2 else None

                    @on("sync")
                    def _(e, src=src, j=j, cols=cols, pA=pA, w_dve=w_dve):
                        ctx.wait(e, "dve", w_dve)
                        e.dma_start(out=pA[:, 0:cols], in_=src[j * 128:(j + 1) * 128, :]).then_inc(*ctx.bump("dma", 16))
                    dma_t = ctx.val("dma")

                    @on("vector")
                    def _(e, dma_t=dma_t, cols=cols, swb=swb, first=(wi == 0 and j == 0), sce_sw=sce_sw,
                          pA=pA, pB=pB, pC=pC, w_pe=tern_pe.get(tj - 2), w_st0=w_st0):
                        ctx.wait(e, "dma", dma_t)
                        if first:
                            ctx.wait(e, "sce", sce_sw)
                        if w_pe is not None:
                            ctx.wait(e, "pe", w_pe)
                        if w_st0 is not None:
                            ctx.wait(e, "dmo", w_st0)
                        e.tensor_scalar(out=pB[:, 0:cols], in0=pA[:, 0:cols], scalar1=swb[:, :],
                                        scalar2=0.5, op0=ALU.mult, op1=ALU.is_gt)
                        e.tensor_scalar(out=pC[:, 0:cols],
                                        in0=pA[:, 0:cols], scalar1=swb[:, :],
                                        scalar2=-0.5, op0=ALU.mult, op1=ALU.is_lt)
                        e.tensor_tensor(out=pC[:, 0:cols], in0=pB[:, 0:cols], in1=pC[:, 0:cols],
                                        op=ALU.subtract); ctx.fence_inc(e, "dve")
                    dve_t = ctx.val("dve")
                    tern_dve[tj] = dve_t

                    nblk = (njt + 3) // 4
                    for blk in range(nblk):
                        n_in = min(4, njt - blk * 4)
                        bank = blk % 2
                        pb = ppsb0 if bank == 0 else ppsb1

                        @on("tensor")
                        def _(e, dve_t=dve_t, blk=blk, n_in=n_in, pb=pb, pC=pC, bev=bank_evac.get(bank)):
                            if blk == 0:
                                ctx.wait(e, "dve", dve_t)
                            if bev is not None:
                                ctx.wait(e, "sce", bev)
                            for q in range(n_in):
                                jj = blk * 4 + q
                                mm = tpose(e, pb[:, q * 128:(q + 1) * 128], pC[:, jj * 128:(jj + 1) * 128],
                                           idnb[:, :], q == 0, q == n_in - 1)
                            ctx.inc(mm, "pe", 1)
                        pe_b = ctx.val("pe")

                        @on("scalar")
                        def _(e, pe_b=pe_b, blk=blk, n_in=n_in, pb=pb, stg=stg,
                              w_st=tern_st.get(tj - 2) if blk == 0 else None):
                            ctx.wait(e, "pe", pe_b)
                            if w_st is not None:
                                ctx.wait(e, "dmo", w_st)
                            e.copy(out=stg[:, blk * 4:blk * 4 + n_in, :], in_=pb[:, 0:n_in * 128].rearrange("p (q x) -> p q x", x=128)).then_inc(*ctx.bump("sce", 1))
                        bank_evac[bank] = ctx.val("sce")
                    tern_pe[tj] = ctx.val("pe")
                    sce_st = ctx.val("sce")

                    @on("gpsimd")
                    def _(e, sce_st=sce_st, wi=wi, j=j, njt=njt, dst=dst, stg=stg):
                        ctx.wait(e, "sce", sce_st)
                        if wi == 0:
                            e.dma_start(out=dst[:, :, j * 128:(j + 1) * 128].transpose([1, 0, 2]),
                                        in_=stg[:, 0:njt, :]).then_inc(*ctx.bump("dmo", 16))
                        else:
                            e.dma_start(out=dst[:, j * 128:(j + 1) * 128].rearrange("(a p) m -> a p m", p=128).transpose([1, 0, 2]),
                                        in_=stg[:, 0:njt, :]).then_inc(*ctx.bump("dmo", 16))
                    tern_st[tj] = ctx.val("dmo")
                    tj += 1

            @on("sync")
            def _(e):
                ctx.wait(e, "dmo", ctx.val("dmo"))

        ctx.barrier(on, ("dmo", "dma", "dmw"))
        if DEBUG_SCRATCH:
            @on("sync")
            def _(e):
                ctx.wait(e, "sce", ctx.val("sce"))
                ctx.wait(e, "dve", ctx.val("dve"))
                e.dma_start(out=dbg_c[:, 0:256], in_=ang[:, :, :].rearrange("p c s -> p (c s)")).then_inc(*ctx.bump("dmo", 16))
                e.dma_start(out=dbg_c[:, 256:320], in_=cwt[:, :, :].rearrange("p c k -> p (c k)")).then_inc(*ctx.bump("dmo", 16))
                e.dma_start(out=dbg_c[:, 320:336], in_=cbt[:, :]).then_inc(*ctx.bump("dmo", 16))
                e.dma_start(out=dbg_c[:, 336:352], in_=bdtt[:, :]).then_inc(*ctx.bump("dmo", 16))
                e.dma_start(out=dbg_c[:, 352:368], in_=dvt[:, :]).then_inc(*ctx.bump("dmo", 16))
                e.dma_start(out=dbg_c[:, 368:400], in_=fcbt[:, :]).then_inc(*ctx.bump("dmo", 16))
                e.dma_start(out=dbg_c[:, 400:496], in_=fcwt[:, :, :].rearrange("p c k -> p (c k)")).then_inc(*ctx.bump("dmo", 16))
                e.dma_start(out=dbg_c[:, 512:2048], in_=wxt[:, :, :].rearrange("p c m -> p (c m)")).then_inc(*ctx.bump("dmo", 16))
        with nc.sbuf_tensor("xcv", [128, DCH, 3 + T], F32) as xcv, \
             nc.sbuf_tensor("zsl", [128, DCH, T], F32) as zsl, \
             nc.sbuf_tensor("dta", [128, DCH, T], F32) as dta, \
             nc.sbuf_tensor("bcgB", [128, 2, SGS, T], F32) as bcgB, \
             nc.sbuf_tensor("agrp", [128, SGS, T], F32) as agrp, \
             nc.sbuf_tensor("ugrp", [128, 2, SGS, T], F32) as ugrp, \
             nc.sbuf_tensor("wxzb", [128, 2, 2, 8, 128], BF16) as wxzb:

            for b in range(B):
                @on("vector")
                def _(e):
                    ctx.wait(e, "gp", ctx.val("gp"))
                    e.memset(hstate[:], 0.0)
                    e.memset(xtail[:], 0.0)
                    e.engine_nop().then_inc(*ctx.bump("dve", 1))

                pe_wxz_prev = {}
                pe_wo_prev = {}
                for tile in range(NT):
                    t0 = tile * T
                    ctx.barrier(on, ("dxo", "dyo", "dho"))

                    with nc.sbuf_tensor(f"ln1T_{b}_{tile}", [128, 2, DMCH, T], BF16) as ln1T, \
                         nc.psum_tensor(f"psbH_{b}_{tile}", [128, 512], BF16) as psbH, \
                         nc.psum_tensor(f"psbL_{b}_{tile}", [128, 512], BF16) as psbL:
                        for i in range(NSUB):
                            @on("sync")
                            def _(e, b=b, t0=t0, i=i):
                                ctx.wait(e, "dve", ctx.val("dve"))
                                e.dma_start(out=scr1[:, :], in_=x_in[b, t0 + i * 128:t0 + (i + 1) * 128, :]).then_inc(*ctx.bump("dma", 16))
                            dma_x = ctx.val("dma")

                            @on("vector")
                            def _(e, dma_x=dma_x):
                                ctx.wait(e, "dma", dma_x)
                                e.engine_nop().then_inc(*ctx.bump("dve", 1))

                            sq = ugrp[:].rearrange("p a s t -> p (a s t)")[:, 0:1024]
                            lnf = ugrp[:].rearrange("p a s t -> p (a s t)")[:, 1024:2048]
                            sce_ln = emit_ln(sq, lnf)

                            @on("vector")
                            def _(e, sce_ln=sce_ln, lnf=lnf):
                                ctx.wait(e, "sce", sce_ln)
                                ctx.wait(e, "pe", ctx.val("pe"))
                                e.tensor_copy(scrb[:, 0:1024], lnf)
                                e.tensor_tensor(out=scrb[:, 1024:2048], in0=lnf, in1=scrb[:, 0:1024], op=ALU.subtract); ctx.fence_inc(e, "dve")
                            dve_c = ctx.val("dve")

                            for hl in range(2):
                                for half in range(2):
                                    pb = psbH if half == 0 else psbL

                                    @on("tensor")
                                    def _(e, dve_c=dve_c, hl=hl, half=half, pb=pb):
                                        if hl == 0 and half == 0:
                                            ctx.wait(e, "dve", dve_c)
                                        ctx.wait(e, "sce", ctx.val("sce"))
                                        for q in range(4):
                                            j = half * 4 + q
                                            mm = tpose(e, pb[:, q * 128:(q + 1) * 128],
                                                       scrb[:, hl * 1024 + j * 128:hl * 1024 + (j + 1) * 128],
                                                       idnb[:, :], q == 0, q == 3)
                                        ctx.inc(mm, "pe", 1)
                                    pe_h = ctx.val("pe")

                                    @on("scalar")
                                    def _(e, pe_h=pe_h, hl=hl, half=half, i=i, pb=pb):
                                        ctx.wait(e, "pe", pe_h)
                                        e.copy(out=ln1T[:, hl, half * 4:(half + 1) * 4, i * 128:(i + 1) * 128],
                                               in_=pb[:, :].rearrange("p (q x) -> p q x", x=128)).then_inc(*ctx.bump("sce", 1))

                        sce_ln1T = ctx.val("sce")

                        with nc.psum_tensor(f"psP_{b}_{tile}", [96, T], F32) as psP, \
                             nc.psum_tensor(f"psX0_{b}_{tile}", [128, T], F32) as psX0, \
                             nc.psum_tensor(f"psX1_{b}_{tile}", [128, T], F32) as psX1, \
                             nc.psum_tensor(f"psZ0_{b}_{tile}", [128, T], F32) as psZ0, \
                             nc.psum_tensor(f"psZ1_{b}_{tile}", [128, T], F32) as psZ1:
                            sce_evac_prev = {}
                            for c in range(DCH):
                                par = c % 2

                                wsem = "dwx0" if par == 0 else "dwx1"

                                @on("sync")
                                def _(e, c=c, wsem=wsem):
                                    ctx.wait(e, wsem, ctx.val(wsem))
                                    pv = pe_wxz_prev.get("both")
                                    if pv is not None:
                                        ctx.wait(e, "pe", pv)
                                    for hix in range(2):
                                        cc = c + hix * (DI // 128)
                                        e.dma_start(out=wxzb[:, hix, 0, :, :],
                                                    in_=wih_s[cc, :, :, :].transpose([1, 0, 2])).then_inc(*ctx.bump(wsem, 16))
                                        e.dma_start(out=wxzb[:, hix, 1, :, :],
                                                    in_=wil_s[cc, :, :, :].transpose([1, 0, 2])).then_inc(*ctx.bump(wsem, 16))
                                dmw_c = ctx.val(wsem)

                                for hix in range(2):
                                    ps = (psX0 if par == 0 else psX1) if hix == 0 else (psZ0 if par == 0 else psZ1)

                                    @on("tensor")
                                    def _(e, dmw_c=dmw_c, ps=ps, par=par, hix=hix, sce_ln1T=sce_ln1T, wsem=wsem):
                                        ctx.wait(e, wsem, dmw_c)
                                        ctx.wait(e, "sce", max(sce_ln1T, sce_evac_prev.get(par) or 0))
                                        n = 0
                                        for (wl, al) in ((0, 0), (0, 1), (1, 0)):
                                            for k in range(8):
                                                n += 1
                                                mm = e.matmul(ps[:, :], wxzb[:, hix, wl, k, :], ln1T[:, al, k, :],
                                                              start=(n == 1), stop=(n == 24))
                                        ctx.inc(mm, "pe", 1)
                                pe_wxz_prev["both"] = ctx.val("pe")
                                pe_xz = ctx.val("pe")

                                psx = psX0 if par == 0 else psX1
                                psz = psZ0 if par == 0 else psZ1

                                @on("scalar")
                                def _(e, pe_xz=pe_xz, c=c, psx=psx, psz=psz):
                                    ctx.wait(e, "pe", pe_xz)
                                    ctx.wait(e, "dve", ctx.val("dve"))
                                    e.copy(out=xcv[:, c, 3:3 + T], in_=psx[:, :])
                                    e.activation(out=zsl[:, c, :], in_=psz[:, :], func=ACT.Silu).then_inc(*ctx.bump("sce", 1))
                                sce_ev = ctx.val("sce")
                                sce_evac_prev[par] = sce_ev

                                @on("vector")
                                def _(e, sce_ev=sce_ev, c=c):
                                    ctx.wait(e, "sce", sce_ev)
                                    e.tensor_copy(xcv[:, c, 0:3], xtail[:, c, :])
                                    cacc = scr1[:, 0:T]
                                    z1 = scrs[:, 15:16]
                                    e.scalar_tensor_tensor(out=cacc, in0=xcv[:, c, 0:T], scalar=cwt[:, c, 0:1],
                                                           in1=z1.broadcast_to([128, T]), op0=ALU.mult, op1=ALU.add)
                                    for k in range(1, 4):
                                        e.scalar_tensor_tensor(out=cacc, in0=xcv[:, c, k:k + T], scalar=cwt[:, c, k:k + 1],
                                                               in1=cacc, op0=ALU.mult, op1=ALU.add)
                                    e.tensor_copy(xtail[:, c, :], xcv[:, c, T:T + 3]); ctx.fence_inc(e, "dve")
                                dve_cv = ctx.val("dve")

                                @on("scalar")
                                def _(e, dve_cv=dve_cv, c=c):
                                    ctx.wait(e, "dve", dve_cv)
                                    e.activation(out=xcv[:, c, 3:3 + T], in_=scr1[:, 0:T], func=ACT.Silu,
                                                 bias=cbt[:, c:c + 1]).then_inc(*ctx.bump("sce", 1))
                                sce_sl = ctx.val("sce")

                            sce_all_silu = ctx.val("sce")

                            @on("tensor")
                            def _(e, sce_all_silu=sce_all_silu):
                                ctx.wait(e, "sce", sce_all_silu)
                                for c in range(DCH):
                                    mm = e.matmul(psP[:, :], wxt[:, c, :], xcv[:, c, 3:3 + T],
                                                  start=(c == 0), stop=(c == DCH - 1))
                                ctx.inc(mm, "pe", 1)
                            pe_pj = ctx.val("pe")

                            @on("scalar")
                            def _(e, pe_pj=pe_pj):
                                ctx.wait(e, "pe", pe_pj)
                                e.copy(out=projt[:, :], in_=psP[:, :]).then_inc(*ctx.bump("sce", 1))
                            sce_pj = ctx.val("sce")

                            @on("sync")
                            def _(e, sce_pj=sce_pj, tile=tile):
                                ctx.wait(e, "sce", sce_pj)
                                e.dma_start(out=bc_s[tile % 2, :, :], in_=projt[64:96, :]).then_inc(*ctx.bump("dma", 16))
                            dma_bc = ctx.val("dma")

                            if DEBUG_SCRATCH and b == 0 and tile == 0:
                                @on("sync")
                                def _(e):
                                    e.dma_start(out=dbg_b[:, 0:8192], in_=ln1T[:, :, :, :].rearrange("p a j t -> p (a j t)")).then_inc(*ctx.bump("dmo", 16))
                                    e.dma_start(out=dbg_c[:, 2048:2560], in_=xcv[:, 0, 3:3 + T]).then_inc(*ctx.bump("dmo", 16))
                                    e.dma_start(out=dbg_c[:, 2560:3072], in_=zsl[:, 0, :]).then_inc(*ctx.bump("dmo", 16))
                                    e.dma_start(out=dbg_c[0:96, 3072:3584], in_=projt[:, :]).then_inc(*ctx.bump("dmo", 16))

                    with nc.sbuf_tensor(f"yscan_{b}_{tile}", [128, DCH, T], F32) as yscan, \
                         nc.psum_tensor(f"psD0_{b}_{tile}", [128, T], F32) as psD0, \
                         nc.psum_tensor(f"psD1_{b}_{tile}", [128, T], F32) as psD1:

                        sce_sp_prev = {}
                        for c in range(DCH):
                            par = c % 2
                            psD = psD0 if par == 0 else psD1

                            @on("tensor")
                            def _(e, c=c, psD=psD, spw=sce_sp_prev.get(c % 2)):
                                ctx.wait(e, "sce", spw if spw is not None else ctx.val("sce"))
                                mm = e.matmul(psD[:, :], wdtt[:, c * 128:(c + 1) * 128], projt[0:64, :],
                                              start=True, stop=True)
                                ctx.inc(mm, "pe", 1)
                            pe_dt = ctx.val("pe")

                            @on("scalar")
                            def _(e, pe_dt=pe_dt, c=c, psD=psD):
                                ctx.wait(e, "pe", pe_dt)
                                ctx.wait(e, "dve", ctx.val("dve"))
                                e.activation(out=scr1[:, 0:T], in_=psD[:, :], func=ACT.Relu, bias=bdtt[:, c:c + 1])
                                e.activation(out=scr1[:, 512:1024], in_=psD[:, :], func=ACT.Abs, bias=bdtt[:, c:c + 1])
                                e.activation(out=scr1[:, 512:1024], in_=scr1[:, 512:1024], func=ACT.Exp, scale=-1.0)
                                e.activation(out=scr1[:, 512:1024], in_=scr1[:, 512:1024], func=ACT.Ln, bias=cst[:, 1:2]).then_inc(*ctx.bump("sce", 1))
                            sce_sp = ctx.val("sce")
                            sce_sp_prev[c % 2] = sce_sp

                            @on("vector")
                            def _(e, sce_sp=sce_sp, c=c):
                                ctx.wait(e, "sce", sce_sp)
                                e.tensor_tensor(out=dta[:, c, :], in0=scr1[:, 0:T], in1=scr1[:, 512:1024], op=ALU.add); ctx.fence_inc(e, "dve")

                        for sg in range(SG):
                            @on("sync")
                            def _(e, sg=sg, tile=tile, dma_bc=dma_bc):
                                ctx.wait(e, "dma", dma_bc)
                                if sg > 0:
                                    ctx.wait(e, "gp", ctx.val("gp"))
                                    ctx.wait(e, "dve", ctx.val("dve"))
                                e.dma_start(out=bcgB[:, 0, :, :],
                                            in_=bc_s[tile % 2, sg * SGS:(sg + 1) * SGS, :].partition_broadcast(128)).then_inc(*ctx.bump("dma", 16))
                                e.dma_start(out=bcgB[:, 1, :, :],
                                            in_=bc_s[tile % 2, S + sg * SGS:S + (sg + 1) * SGS, :].partition_broadcast(128)).then_inc(*ctx.bump("dma", 16))
                            dma_g = ctx.val("dma")

                            @on("vector")
                            def _(e):
                                ctx.wait(e, "gp", ctx.val("gp"))
                                e.tensor_tensor(out=scr1[:, 0:T], in0=dta[:, 0, :], in1=xcv[:, 0, 3:3 + T], op=ALU.mult); ctx.fence_inc(e, "dve")
                            dve_dtx = {0: ctx.val("dve")}

                            @on("scalar")
                            def _(e, sg=sg):
                                ctx.wait(e, "dve", ctx.val("dve"))
                                for si in range(SGS):
                                    a_ = e.activation(out=agrp[:, si, :], in_=dta[:, 0, :], func=ACT.Exp,
                                                      scale=ang[:, 0, sg * SGS + si:sg * SGS + si + 1])
                                ctx.inc(a_, "sce", 1)
                            sce_a = {0: ctx.val("sce")}
                            gp_u = {}
                            dve_scan = {}
                            dve_tree = {}

                            def emit_ctree(cc, sg=sg):
                                pr = cc % 2
                                tmp = scr1[:, ((cc + 1) % 2) * 512:((cc + 1) % 2) * 512 + T]

                                @on("vector")
                                def _(e, cc=cc, pr=pr, tmp=tmp, sg=sg):
                                    if sg == 0:
                                        e.tensor_tensor(out=yscan[:, cc, :], in0=ugrp[:, pr, 0, :], in1=bcgB[:, 1, 0, :], op=ALU.mult)
                                    else:
                                        e.tensor_tensor(out=tmp, in0=ugrp[:, pr, 0, :], in1=bcgB[:, 1, 0, :], op=ALU.mult)
                                        e.tensor_tensor(out=yscan[:, cc, :], in0=yscan[:, cc, :], in1=tmp, op=ALU.add)
                                    for si in range(1, SGS):
                                        e.tensor_tensor(out=tmp, in0=ugrp[:, pr, si, :], in1=bcgB[:, 1, si, :], op=ALU.mult)
                                        e.tensor_tensor(out=yscan[:, cc, :], in0=yscan[:, cc, :], in1=tmp, op=ALU.add)
                                    ctx.fence_inc(e, "dve")
                                dve_tree[cc] = ctx.val("dve")

                            for c in range(DCH):
                                par = c % 2

                                @on("gpsimd")
                                def _(e, c=c, par=par, dma_g=dma_g, dtx_c=dve_dtx[c],
                                      war=dve_tree.get(c - 2)):
                                    ctx.wait(e, "dve", dtx_c)
                                    ctx.wait(e, "dma", dma_g)
                                    if war is not None:
                                        ctx.wait(e, "dve", war)
                                    e.tensor_tensor(out=ugrp[:, par, :, :],
                                                    in0=scr1[:, par * 512:par * 512 + T].unsqueeze(1).broadcast_to([128, SGS, T]),
                                                    in1=bcgB[:, 0, :, :], op=ALU.mult); ctx.fence_inc(e, "gp")
                                gp_u[c] = ctx.val("gp")

                                if c + 1 < DCH:
                                    @on("vector")
                                    def _(e, c=c):
                                        e.tensor_tensor(out=scr1[:, ((c + 1) % 2) * 512:((c + 1) % 2) * 512 + T],
                                                        in0=dta[:, c + 1, :], in1=xcv[:, c + 1, 3:3 + T], op=ALU.mult); ctx.fence_inc(e, "dve")
                                    dve_dtx[c + 1] = ctx.val("dve")

                                if c >= 1:
                                    emit_ctree(c - 1)

                                @on("vector")
                                def _(e, c=c, sg=sg, par=par, gp_u_c=gp_u[c], sce_a_c=sce_a[c]):
                                    ctx.wait(e, "gp", gp_u_c)
                                    ctx.wait(e, "sce", sce_a_c)
                                    for si in range(SGS):
                                        e.tensor_tensor_scan(out=ugrp[:, par, si, :], data0=agrp[:, si, :],
                                                             data1=ugrp[:, par, si, :],
                                                             initial=hstate[:, c, sg * SGS + si:sg * SGS + si + 1],
                                                             op0=ALU.mult, op1=ALU.add)
                                    e.drain()
                                    e.tensor_copy(hstate[:, c, sg * SGS:(sg + 1) * SGS],
                                                  ugrp[:, par, :, T - 1:T].rearrange("p s o -> p (s o)"))
                                    e.drain(); ctx.fence_inc(e, "dve")
                                dve_scan[c] = ctx.val("dve")

                                if c + 1 < DCH:
                                    @on("scalar")
                                    def _(e, c=c, sg=sg, scan_war=dve_scan[c]):
                                        ctx.wait(e, "dve", scan_war)
                                        for si in range(SGS):
                                            a_ = e.activation(out=agrp[:, si, :], in_=dta[:, c + 1, :], func=ACT.Exp,
                                                              scale=ang[:, c + 1, sg * SGS + si:sg * SGS + si + 1])
                                        ctx.inc(a_, "sce", 1)
                                    sce_a[c + 1] = ctx.val("sce")

                            emit_ctree(DCH - 1)

                        for c in range(DCH):
                            @on("vector")
                            def _(e, c=c):
                                f1 = scr1[:, 0:T]
                                f2 = scr1[:, 512:1024]
                                e.scalar_tensor_tensor(out=f1, in0=xcv[:, c, 3:3 + T], scalar=dvt[:, c:c + 1],
                                                       in1=yscan[:, c, :], op0=ALU.mult, op1=ALU.add)
                                e.tensor_tensor(out=f2, in0=f1, in1=zsl[:, c, :], op=ALU.mult)
                                yb = xcv[:, c, :].bitcast(BF16)
                                e.tensor_copy(yb[:, 0:T], f2)
                                e.tensor_tensor(out=yb[:, T:2 * T], in0=f2, in1=yb[:, 0:T], op=ALU.subtract); ctx.fence_inc(e, "dve")
                        dve_fin = ctx.val("dve")

                        if DEBUG_SCRATCH and b == 0 and tile == 0:
                            @on("sync")
                            def _(e, dve_fin=dve_fin):
                                ctx.wait(e, "dve", dve_fin)
                                e.dma_start(out=dbg_c[:, 3584:4096], in_=dta[:, 0, :]).then_inc(*ctx.bump("dmo", 16))
                                e.dma_start(out=dbg_c[:, 1536:2048], in_=yscan[:, 0, :]).then_inc(*ctx.bump("dmo", 16))

                    with nc.sbuf_tensor(f"wob_{b}_{tile}", [128, 2, 2, D], BF16) as wob, \
                         nc.sbuf_tensor(f"obufE_{b}_{tile}", [128, 2, D], F32) as obuf, \
                         nc.psum_tensor(f"psE0_{b}_{tile}", [128, 512], F32) as psE0, \
                         nc.psum_tensor(f"psE1_{b}_{tile}", [128, 512], F32) as psE1, \
                         nc.psum_tensor(f"psE2_{b}_{tile}", [128, 512], F32) as psE2, \
                         nc.psum_tensor(f"psE3_{b}_{tile}", [128, 512], F32) as psE3, \
                         nc.psum_tensor(f"psE4_{b}_{tile}", [128, 512], F32) as psE4, \
                         nc.psum_tensor(f"psE5_{b}_{tile}", [128, 512], F32) as psE5, \
                         nc.psum_tensor(f"psE6_{b}_{tile}", [128, 512], F32) as psE6, \
                         nc.psum_tensor(f"psE7_{b}_{tile}", [128, 512], F32) as psE7:
                        psE = [psE0, psE1, psE2, psE3, psE4, psE5, psE6, psE7]
                        for c in range(DCH):
                            par = c % 2

                            osem = "dwo0" if par == 0 else "dwo1"

                            @on("sync")
                            def _(e, c=c, par=par, dve_fin=dve_fin, osem=osem):
                                ctx.wait(e, osem, ctx.val(osem))
                                if c == 0:
                                    ctx.wait(e, "dve", dve_fin)
                                pv = pe_wo_prev.get(par)
                                if pv is not None:
                                    ctx.wait(e, "pe", pv)
                                e.dma_start(out=wob[:, par, 0, :], in_=woh_s[c * 128:(c + 1) * 128, :]).then_inc(*ctx.bump(osem, 16))
                                e.dma_start(out=wob[:, par, 1, :], in_=wol_s[c * 128:(c + 1) * 128, :]).then_inc(*ctx.bump(osem, 16))
                            dmw_o = ctx.val(osem)

                            @on("tensor")
                            def _(e, c=c, par=par, dmw_o=dmw_o, dve_fin=dve_fin, osem=osem):
                                ctx.wait(e, osem, dmw_o)
                                if c == 0:
                                    ctx.wait(e, "dve", dve_fin)
                                yb = xcv[:, c, :].bitcast(BF16)
                                for mb in range(4):
                                    for (yl, wl) in ((0, 0), (0, 1), (1, 0)):
                                        lhs = yb[:, yl * T + mb * 128:yl * T + (mb + 1) * 128]
                                        for nb in range(2):
                                            mm = e.matmul(psE[mb * 2 + nb][:, :], lhs, wob[:, par, wl, nb * 512:(nb + 1) * 512],
                                                          start=(c == 0 and yl == 0 and wl == 0),
                                                          stop=(c == DCH - 1 and yl == 1))
                                ctx.inc(mm, "pe", 1)
                            pe_wo_prev[par] = ctx.val("pe")
                        pe_wo = ctx.val("pe")

                        dxo_prev = {}
                        for mb in range(4):
                            @on("sync")
                            def _(e, b=b, t0=t0, mb=mb):
                                ctx.wait(e, "dve", ctx.val("dve"))
                                e.dma_start(out=scr1[:, :], in_=x_in[b, t0 + mb * 128:t0 + (mb + 1) * 128, :]).then_inc(*ctx.bump("dma", 16))
                            dma_r = ctx.val("dma")

                            @on("vector")
                            def _(e, pe_wo=pe_wo, dma_r=dma_r, mb=mb, dxo_safe=dxo_prev.get(mb % 2)):
                                ctx.wait(e, "pe", pe_wo)
                                ctx.wait(e, "dma", dma_r)
                                if dxo_safe is not None:
                                    ctx.wait(e, "dxo", dxo_safe)
                                for nb in range(2):
                                    e.tensor_tensor(out=obuf[:, mb % 2, nb * 512:(nb + 1) * 512], in0=psE[mb * 2 + nb][:, :],
                                                    in1=scr1[:, nb * 512:(nb + 1) * 512], op=ALU.add)
                                e.engine_nop().then_inc(*ctx.bump("dve", 1))
                            dve_x2 = ctx.val("dve")

                            @on("sync")
                            def _(e, dve_x2=dve_x2, b=b, t0=t0, mb=mb):
                                ctx.wait(e, "dxo", ctx.val("dxo"))
                                ctx.wait(e, "dve", dve_x2)
                                e.dma_start(out=x2_s[b, t0 + mb * 128:t0 + (mb + 1) * 128, :], in_=obuf[:, mb % 2, :]).then_inc(*ctx.bump("dxo", 16))
                            dxo_prev[mb % 2] = ctx.val("dxo")

        with ExitStack() as ffn_es:
            wqr = ffn_es.enter_context(nc.sbuf_tensor("wqr", [128, DMCH, DH], BF16))
            k1T = ffn_es.enter_context(nc.sbuf_tensor("k1T", [128, DMCH, L], BF16))
            hlin = ffn_es.enter_context(nc.sbuf_tensor("hlin", [128, 2 + L], F32))
            hbuf = ffn_es.enter_context(nc.sbuf_tensor("hbuf", [128, L], F32))
            macc = ffn_es.enter_context(nc.sbuf_tensor("macc", [128, L], F32))
            i1bc = ffn_es.enter_context(nc.sbuf_tensor("i1bc", [128, L], F32))
            s2bc = ffn_es.enter_context(nc.sbuf_tensor("s2bc", [128, L], F32))
            k2b = ffn_es.enter_context(nc.sbuf_tensor("k2b", [128, 2, T], BF16))
            obufF = ffn_es.enter_context(nc.sbuf_tensor("obufF", [128, 2, D], F32))
            w2b = ffn_es.enter_context(nc.sbuf_tensor("w2b", [128, 2, D], BF16))

            for b in range(B):
                ctx.barrier(on, ("dxo", "dyo", "dho", "dma", "dmo"))

                with nc.psum_tensor(f"fpsbH_{b}", [128, 512], BF16) as psbH, \
                     nc.psum_tensor(f"fpsbL_{b}", [128, 512], BF16) as psbL:
                    for tt in range(L // 128):
                        @on("sync")
                        def _(e, b=b, tt=tt):
                            ctx.wait(e, "dve", ctx.val("dve"))
                            ctx.wait(e, "dma", ctx.val("dma"))
                            e.dma_start(out=scr1[:, :], in_=x2_s[b, tt * 128:(tt + 1) * 128, :]).then_inc(*ctx.bump("dma", 16))
                        dma_x = ctx.val("dma")

                        @on("vector")
                        def _(e, dma_x=dma_x):
                            ctx.wait(e, "dma", dma_x)
                            e.engine_nop().then_inc(*ctx.bump("dve", 1))

                        sce_ln2 = emit_ln(hlin[:, 0:1024], hlin[:, 1024:2048])

                        @on("vector")
                        def _(e, b=b, tt=tt, sce_ln2=sce_ln2):
                            ctx.wait(e, "sce", sce_ln2)
                            ctx.wait(e, "pe", ctx.val("pe"))
                            ln2 = hlin[:, 1024:2048]
                            e.reduce_max(out=scrs[:, 8:9], in_=ln2, axis=AX.X, apply_absolute_value=True)
                            e.drain()
                            e.tensor_single_scalar(out=scrs[:, 8:9], in_=scrs[:, 8:9], scalar=1e-5, op=ALU.max)
                            e.drain()
                            e.reciprocal(out=scrs[:, 9:10], in_=scrs[:, 8:9])
                            e.drain()
                            e.tensor_single_scalar(out=scrs[:, 9:10], in_=scrs[:, 9:10], scalar=127.0, op=ALU.mult)
                            e.drain()
                            e.tensor_scalar(out=hlin[:, 0:1024], in0=ln2, scalar1=scrs[:, 9:10], scalar2=RND,
                                            op0=ALU.mult, op1=ALU.add)
                            e.tensor_single_scalar(out=scrb[:, 0:1024], in_=hlin[:, 0:1024], scalar=RND, op=ALU.subtract)
                            e.tensor_tensor(out=scrs[:, 10:11], in0=scrs[:, 8:9], in1=k1c[:, :], op=ALU.mult)
                            e.drain(); ctx.fence_inc(e, "dve")
                        dve_k = ctx.val("dve")

                        @on("sync")
                        def _(e, dve_k=dve_k, b=b, tt=tt):
                            ctx.wait(e, "dve", dve_k)
                            e.dma_start(out=i1r_s[b, tt * 128:(tt + 1) * 128].unsqueeze(1), in_=scrs[:, 10:11]).then_inc(*ctx.bump("dma", 16))

                        for half in range(2):
                            pb = psbH if half == 0 else psbL

                            @on("tensor")
                            def _(e, dve_k=dve_k, half=half, pb=pb):
                                if half == 0:
                                    ctx.wait(e, "dve", dve_k)
                                ctx.wait(e, "sce", ctx.val("sce"))
                                for q in range(4):
                                    j = half * 4 + q
                                    mm = tpose(e, pb[:, q * 128:(q + 1) * 128], scrb[:, j * 128:(j + 1) * 128],
                                               idnb[:, :], q == 0, q == 3)
                                ctx.inc(mm, "pe", 1)
                            pe_t = ctx.val("pe")

                            @on("scalar")
                            def _(e, pe_t=pe_t, tt=tt, half=half, pb=pb):
                                ctx.wait(e, "pe", pe_t)
                                e.copy(out=k1T[:, half * 4:(half + 1) * 4, tt * 128:(tt + 1) * 128],
                                       in_=pb[:, :].rearrange("p (q x) -> p q x", x=128)).then_inc(*ctx.bump("sce", 1))

                if DEBUG_SCRATCH and b == 0:
                    @on("sync")
                    def _(e):
                        ctx.wait(e, "sce", ctx.val("sce"))
                        e.dma_start(out=dbg_k[:, :, :], in_=k1T[:, :, :]).then_inc(*ctx.bump("dmo", 16))

                @on("sync")
                def _(e, b=b):
                    ctx.wait(e, "pe", ctx.val("pe"))
                    ctx.wait(e, "dma", ctx.val("dma"))
                    ctx.wait(e, "dmw", ctx.val("dmw"))
                    for j in range(DMCH):
                        e.dma_start(out=wqr[:, j, :], in_=w1qt_s[j, :, :]).then_inc(*ctx.bump("dmw", 16))
                    e.dma_start(out=i1bc[:, :], in_=i1r_s[b, :].unsqueeze(0).partition_broadcast(128).squeeze(1)).then_inc(*ctx.bump("dma", 16))
                dmw_w1 = ctx.val("dmw")
                dma_i1 = ctx.val("dma")

                @on("vector")
                def _(e):
                    ctx.wait(e, "gp", ctx.val("gp"))
                    e.memset(hlin[:, 0:1], 0.0)
                    e.memset(hlin[:, 1 + L:2 + L], 0.0)
                    e.memset(macc[:], 0.0)
                    e.engine_nop().then_inc(*ctx.bump("dve", 1))

                with nc.psum_tensor(f"psH_{b}", [128, 4 * 512], F32) as psH:
                    sce_h_prev = None
                    for hc in range(HCH):
                        @on("tensor")
                        def _(e, hc=hc, dmw_w1=dmw_w1):
                            ctx.wait(e, "dmw", dmw_w1)
                            if hc == 0:
                                ctx.wait(e, "sce", ctx.val("sce"))
                            if hc > 0:
                                ctx.wait(e, "dve", ctx.val("dve"))
                            for nb in range(4):
                                for k in range(8):
                                    mm = e.matmul(psH[:, nb * 512:(nb + 1) * 512], wqr[:, k, hc * 128:(hc + 1) * 128],
                                                  k1T[:, k, nb * 512:(nb + 1) * 512], start=(k == 0), stop=(k == 7))
                            ctx.inc(mm, "pe", 1)
                        pe_h = ctx.val("pe")

                        @on("vector")
                        def _(e, pe_h=pe_h, dma_i1=dma_i1, sce_h_prev=sce_h_prev):
                            ctx.wait(e, "pe", pe_h)
                            ctx.wait(e, "dma", dma_i1)
                            ctx.wait(e, "dho", ctx.val("dho"))
                            if sce_h_prev is not None:
                                ctx.wait(e, "sce", sce_h_prev)
                            for nb in range(4):
                                e.tensor_tensor(out=hlin[:, 1 + nb * 512:1 + (nb + 1) * 512],
                                                in0=psH[:, nb * 512:(nb + 1) * 512],
                                                in1=i1bc[:, nb * 512:(nb + 1) * 512], op=ALU.mult)
                            z1 = scrs[:, 15:16]
                            e.scalar_tensor_tensor(out=hbuf[:, :], in0=hlin[:, 0:L], scalar=fcwt[:, hc, 0:1],
                                                   in1=z1.broadcast_to([128, L]), op0=ALU.mult, op1=ALU.add)
                            e.scalar_tensor_tensor(out=hbuf[:, :], in0=hlin[:, 1:1 + L], scalar=fcwt[:, hc, 1:2],
                                                   in1=hbuf[:, :], op0=ALU.mult, op1=ALU.add)
                            e.scalar_tensor_tensor(out=hbuf[:, :], in0=hlin[:, 2:2 + L], scalar=fcwt[:, hc, 2:3],
                                                   in1=hbuf[:, :], op0=ALU.mult, op1=ALU.add); ctx.fence_inc(e, "dve")
                        dve_cv = ctx.val("dve")

                        @on("scalar")
                        def _(e, dve_cv=dve_cv, hc=hc):
                            ctx.wait(e, "dve", dve_cv)
                            e.activation(out=hbuf[:, :], in_=hbuf[:, :], func=ACT.Silu, bias=fcbt[:, hc:hc + 1])
                            e.activation(out=hlin[:, 1:1 + L], in_=hbuf[:, :], func=ACT.Abs).then_inc(*ctx.bump("sce", 1))
                        sce_h = ctx.val("sce")
                        sce_h_prev = sce_h

                        @on("vector")
                        def _(e, sce_h=sce_h):
                            ctx.wait(e, "sce", sce_h)
                            e.tensor_tensor(out=macc[:, :], in0=macc[:, :], in1=hlin[:, 1:1 + L], op=ALU.max); ctx.fence_inc(e, "dve")

                        @on("sync")
                        def _(e, sce_h=sce_h, hc=hc):
                            ctx.wait(e, "dho", ctx.val("dho"))
                            ctx.wait(e, "sce", sce_h)
                            e.dma_start(out=h_scr[hc * 128:(hc + 1) * 128, :], in_=hbuf[:, :]).then_inc(*ctx.bump("dho", 16))

                with nc.psum_tensor(f"psM_{b}", [128, 512], F32) as psM:
                    dve_red = ctx.val("dve")
                    for blk in range(4):
                        @on("tensor")
                        def _(e, blk=blk, dve_red=dve_red):
                            if blk == 0:
                                ctx.wait(e, "dve", ctx.val("dve"))
                            else:
                                ctx.wait(e, "dve", ctx.val("dve"))
                            for q in range(4):
                                it = blk * 4 + q
                                mm = tpose(e, psM[:, q * 128:(q + 1) * 128], macc[:, it * 128:(it + 1) * 128],
                                           idn[:, :], q == 0, q == 3)
                            ctx.inc(mm, "pe", 1)
                        pe_m = ctx.val("pe")

                        @on("vector")
                        def _(e, pe_m=pe_m, blk=blk):
                            ctx.wait(e, "pe", pe_m)
                            for q in range(4):
                                it = blk * 4 + q
                                e.reduce_max(out=inv2t[:, it:it + 1], in_=psM[:, q * 128:(q + 1) * 128], axis=AX.X)
                            e.drain()
                            e.engine_nop().then_inc(*ctx.bump("dve", 1))

                    @on("vector")
                    def _(e, b=b):
                        e.drain()
                        e.tensor_single_scalar(out=inv2t[:, :], in_=inv2t[:, :], scalar=1e-5, op=ALU.max)
                        e.drain()
                        e.reciprocal(out=scr1[:, 0:16], in_=inv2t[:, :])
                        e.drain()
                        e.tensor_single_scalar(out=scr1[:, 0:16], in_=scr1[:, 0:16], scalar=127.0, op=ALU.mult)
                        e.tensor_tensor(out=inv2t[:, :], in0=inv2t[:, :],
                                        in1=k2c[:, :].broadcast_to([128, 16]), op=ALU.mult)
                        e.drain(); ctx.fence_inc(e, "dve")
                    dve_s2 = ctx.val("dve")

                    @on("tensor")
                    def _(e, dve_s2=dve_s2):
                        ctx.wait(e, "dve", dve_s2)
                        mm = tpose(e, psM[0:16, 0:128], scr1[:, 0:16], idn[:, :], True, True)
                        ctx.inc(mm, "pe", 1)
                    pe_s2 = ctx.val("pe")

                    @on("scalar")
                    def _(e, pe_s2=pe_s2):
                        ctx.wait(e, "pe", pe_s2)
                        e.copy(out=scr1[0:16, 512:640], in_=psM[0:16, 0:128]).then_inc(*ctx.bump("sce", 1))
                    sce_s2 = ctx.val("sce")

                    @on("sync")
                    def _(e, sce_s2=sce_s2, b=b):
                        ctx.wait(e, "sce", sce_s2)
                        e.dma_start(out=s2r_s[b, :].rearrange("(it x) -> it x", x=128), in_=scr1[0:16, 512:640]).then_inc(*ctx.bump("dma", 16))
                    dma_s2w = ctx.val("dma")

                    @on("sync")
                    def _(e, dma_s2w=dma_s2w, b=b):
                        ctx.wait(e, "dma", dma_s2w)
                        e.dma_start(out=s2bc[:, :], in_=s2r_s[b, :].unsqueeze(0).partition_broadcast(128).squeeze(1)).then_inc(*ctx.bump("dma", 16))
                    dma_s2b = ctx.val("dma")

                with nc.psum_tensor(f"psO0_{b}", [128, 512], F32) as psO0, \
                     nc.psum_tensor(f"psO1_{b}", [128, 512], F32) as psO1, \
                     nc.psum_tensor(f"psO2_{b}", [128, 512], F32) as psO2, \
                     nc.psum_tensor(f"psO3_{b}", [128, 512], F32) as psO3, \
                     nc.psum_tensor(f"psO4_{b}", [128, 512], F32) as psO4, \
                     nc.psum_tensor(f"psO5_{b}", [128, 512], F32) as psO5, \
                     nc.psum_tensor(f"psO6_{b}", [128, 512], F32) as psO6, \
                     nc.psum_tensor(f"psO7_{b}", [128, 512], F32) as psO7:
                    psO = [psO0, psO1, psO2, psO3, psO4, psO5, psO6, psO7]
                    dve_evac_last = None
                    for tile in range(NT):
                        t0 = tile * T
                        pe_mm_prev = {}
                        for hc in range(HCH):
                            par = hc % 2

                            hsem = "dh0" if par == 0 else "dh1"
                            w2sem = "dw20" if par == 0 else "dw21"

                            @on("sync")
                            def _(e, hc=hc, par=par, t0=t0, dma_s2b=dma_s2b, hsem=hsem, w2sem=w2sem):
                                ctx.wait(e, hsem, ctx.val(hsem))
                                ctx.wait(e, w2sem, ctx.val(w2sem))
                                ctx.wait(e, "dho", ctx.val("dho"))
                                ctx.wait(e, "dma", dma_s2b)
                                pv = pe_mm_prev.get(par)
                                if pv is not None:
                                    ctx.wait(e, "pe", pv)
                                ctx.wait(e, "dve", ctx.val("dve"))
                                e.dma_start(out=hbuf[:, par * T:(par + 1) * T],
                                            in_=h_scr[hc * 128:(hc + 1) * 128, t0:t0 + T]).then_inc(*ctx.bump(hsem, 16))
                                e.dma_start(out=w2b[:, par, :], in_=w2qt_s[hc * 128:(hc + 1) * 128, :]).then_inc(*ctx.bump(w2sem, 16))
                            dma_h = ctx.val(hsem)
                            dmw_2 = ctx.val(w2sem)

                            @on("vector")
                            def _(e, dma_h=dma_h, par=par, t0=t0, pv=pe_mm_prev.get(par), hsem=hsem):
                                ctx.wait(e, hsem, dma_h)
                                if pv is not None:
                                    ctx.wait(e, "pe", pv)
                                e.tensor_tensor(out=scr1[:, 0:T], in0=hbuf[:, par * T:(par + 1) * T],
                                                in1=s2bc[:, t0:t0 + T], op=ALU.mult)
                                e.tensor_scalar(out=k2b[:, par, :], in0=scr1[:, 0:T], scalar1=RND, scalar2=RND,
                                                op0=ALU.add, op1=ALU.subtract); ctx.fence_inc(e, "dve")
                            dve_k2 = ctx.val("dve")

                            @on("tensor")
                            def _(e, dve_k2=dve_k2, dmw_2=dmw_2, hc=hc, par=par, tile=tile, dve_evac_last=dve_evac_last, w2sem=w2sem):
                                ctx.wait(e, "dve", dve_k2)
                                ctx.wait(e, w2sem, dmw_2)
                                if hc == 0 and tile > 0 and dve_evac_last is not None:
                                    ctx.wait(e, "dve", dve_evac_last)
                                for mb in range(4):
                                    for nb in range(2):
                                        mm = e.matmul(psO[mb * 2 + nb][:, :], k2b[:, par, mb * 128:(mb + 1) * 128],
                                                      w2b[:, par, nb * 512:(nb + 1) * 512],
                                                      start=(hc == 0), stop=(hc == HCH - 1))
                                ctx.inc(mm, "pe", 1)
                            pe_mm_prev[par] = ctx.val("pe")
                        pe_o = ctx.val("pe")

                        dyo_prev = {}
                        for mb in range(4):
                            @on("sync")
                            def _(e, b=b, t0=t0, mb=mb):
                                ctx.wait(e, "dve", ctx.val("dve"))
                                e.dma_start(out=scr1[:, :], in_=x2_s[b, t0 + mb * 128:t0 + (mb + 1) * 128, :]).then_inc(*ctx.bump("dma", 16))
                            dma_r = ctx.val("dma")

                            @on("vector")
                            def _(e, pe_o=pe_o, dma_r=dma_r, mb=mb, tile=tile, dyo_safe=dyo_prev.get(mb % 2)):
                                ctx.wait(e, "pe", pe_o)
                                ctx.wait(e, "dma", dma_r)
                                if dyo_safe is not None:
                                    ctx.wait(e, "dyo", dyo_safe)
                                for nb in range(2):
                                    e.scalar_tensor_tensor(out=obufF[:, mb % 2, nb * 512:(nb + 1) * 512], in0=psO[mb * 2 + nb][:, :],
                                                           scalar=inv2t[:, tile * 4 + mb:tile * 4 + mb + 1],
                                                           in1=scr1[:, nb * 512:(nb + 1) * 512], op0=ALU.mult, op1=ALU.add)
                                e.engine_nop().then_inc(*ctx.bump("dve", 1))
                            dve_o = ctx.val("dve")
                            dve_evac_last = dve_o

                            @on("sync")
                            def _(e, dve_o=dve_o, b=b, t0=t0, mb=mb):
                                ctx.wait(e, "dyo", ctx.val("dyo"))
                                ctx.wait(e, "dve", dve_o)
                                e.dma_start(out=y_out[b, t0 + mb * 128:t0 + (mb + 1) * 128, :], in_=obufF[:, mb % 2, :]).then_inc(*ctx.bump("dyo", 16))
                            dyo_prev[mb % 2] = ctx.val("dyo")

        @on("sync")
        def _(e):
            for s_ in ("dyo", "dxo", "dho", "dmo", "dma", "dmw",
                       "dwx0", "dwx1", "dwo0", "dwo1", "dh0", "dh1", "dw20", "dw21"):
                ctx.wait(e, s_, ctx.val(s_))

    return nc


def kernel(L0_attn_hidden_states, L0_attn_W_in, L0_attn_conv_w, L0_attn_conv_b,
           L0_attn_W_x, L0_attn_W_dt, L0_attn_b_dt, L0_attn_A_log, L0_attn_D,
           L0_attn_W_out, L0_ffn_w1, L0_ffn_conv_w, L0_ffn_conv_b, L0_ffn_w2):
    x = np.ascontiguousarray(np.asarray(L0_attn_hidden_states, dtype=np.float32))
    weights = dict(
        w_in=np.ascontiguousarray(np.asarray(L0_attn_W_in, dtype=np.float32)),
        conv_w=np.ascontiguousarray(np.asarray(L0_attn_conv_w, dtype=np.float32)),
        conv_b=np.ascontiguousarray(np.asarray(L0_attn_conv_b, dtype=np.float32)),
        w_x=np.ascontiguousarray(np.asarray(L0_attn_W_x, dtype=np.float32)),
        w_dt=np.ascontiguousarray(np.asarray(L0_attn_W_dt, dtype=np.float32)),
        b_dt=np.ascontiguousarray(np.asarray(L0_attn_b_dt, dtype=np.float32)),
        a_log=np.ascontiguousarray(np.asarray(L0_attn_A_log, dtype=np.float32)),
        d_vec=np.ascontiguousarray(np.asarray(L0_attn_D, dtype=np.float32)),
        w_out=np.ascontiguousarray(np.asarray(L0_attn_W_out, dtype=np.float32)),
        f_w1=np.ascontiguousarray(np.asarray(L0_ffn_w1, dtype=np.float32)),
        f_cw=np.ascontiguousarray(np.asarray(L0_ffn_conv_w, dtype=np.float32).reshape(DH, 3)),
        f_cb=np.ascontiguousarray(np.asarray(L0_ffn_conv_b, dtype=np.float32)),
        f_w2=np.ascontiguousarray(np.asarray(L0_ffn_w2, dtype=np.float32)),
    )
    nc = build()
    in_maps = []
    for i in range(8):
        m = dict(weights)
        m["x_in"] = x[2 * i:2 * i + 2]
        in_maps.append(m)
    res = run_bass_kernel_spmd(nc, in_maps, core_ids=list(range(8)))
    return np.concatenate([r["y_out"] for r in res.results], axis=0)


# version 11
# speedup vs baseline: 1.1247x; 1.0097x over previous
import numpy as np
from contextlib import ExitStack

import concourse.bass as bass
import concourse.mybir as mybir
from concourse.bass_utils import run_bass_kernel_spmd

F32 = mybir.dt.float32
BF16 = mybir.dt.bfloat16
ALU = mybir.AluOpType
ACT = mybir.ActivationFunctionType
AX = mybir.AxisListType

B = 2
L = 2048
D = 1024
DI = 2048
S = 16
R = 64
DH = 4096
T = 512
NT = L // T
DCH = DI // 128
DMCH = D // 128
HCH = DH // 128
NSUB = T // 128
SG = 4
SGS = S // SG
RND = 12582912.0
EPS = 1e-5

DEV_DRAINS = False
DEBUG_SCRATCH = False

_DRAIN_OPS = frozenset((
    "tensor_tensor", "tensor_copy", "tensor_scalar", "tensor_single_scalar",
    "scalar_tensor_tensor", "tensor_tensor_scan", "tensor_reduce",
    "reduce_sum", "reduce_max", "memset", "activation", "copy",
    "reciprocal", "iota", "tensor_relu", "tensor_add", "tensor_mul",
    "tensor_sub", "tensor_max", "tensor_scalar_max",
))


class _DrainProxy:
    def __init__(self, eng):
        self._e = eng

    def __getattr__(self, name):
        f = getattr(self._e, name)
        if name in _DRAIN_OPS:
            def g(*a, **k):
                r = f(*a, **k)
                self._e.drain()
                return r
            return g
        return f


class Ctx:
    def __init__(self, nc, block, es):
        self.nc = nc
        self.block = block
        self.es = es
        self.sems = {}

    def mksem(self, *names):
        for n in names:
            self.sems[n] = [self.es.enter_context(self.nc.semaphore(n)), 0]

    def sem(self, name):
        return self.sems[name][0]

    def fence_inc(self, e, name, n=1):
        s = self.sems[name]
        s[1] += n
        e.engine_nop().then_inc(s[0], n)

    def bump(self, name, n=1):
        s = self.sems[name]
        s[1] += n
        return (s[0], n)

    def inc(self, instr, name, n=1):
        s = self.sems[name]
        s[1] += n
        instr.then_inc(s[0], n)
        return s[1]

    def val(self, name):
        return self.sems[name][1]

    def barrier(self, on, store_sems):
        @on("sync")
        def _(e):
            for s_ in store_sems:
                self.wait(e, s_, self.val(s_))
            self.sems["tileb"][1] += 1
            e.sem_inc(self.sems["tileb"][0], 1)
        tb = self.val("tileb")
        for eng in ("vector", "scalar", "tensor", "gpsimd"):
            @on(eng)
            def _(e, tb=tb):
                self.wait(e, "tileb", tb)

    def wait(self, eng, name, v=None):
        s = self.sems[name]
        eng.wait_ge(s[0], s[1] if v is None else v)


def build():
    nc = bass.Bass("TRN2", target_bir_lowering=False,
                   detect_race_conditions=DEV_DRAINS)

    _scr_kind2 = "ExternalOutput" if DEBUG_SCRATCH else "Internal"
    x_in = nc.dram_tensor("x_in", [B, L, D], F32, kind="ExternalInput").ap()
    w_in = nc.dram_tensor("w_in", [D, 2 * DI], F32, kind="ExternalInput").ap()
    conv_w = nc.dram_tensor("conv_w", [4, DI], F32, kind="ExternalInput").ap()
    conv_b = nc.dram_tensor("conv_b", [DI], F32, kind="ExternalInput").ap()
    w_x = nc.dram_tensor("w_x", [DI, R + 2 * S], F32, kind="ExternalInput").ap()
    w_dt = nc.dram_tensor("w_dt", [R, DI], F32, kind="ExternalInput").ap()
    b_dt = nc.dram_tensor("b_dt", [DI], F32, kind="ExternalInput").ap()
    a_log = nc.dram_tensor("a_log", [DI, S], F32, kind="ExternalInput").ap()
    d_vec = nc.dram_tensor("d_vec", [DI], F32, kind="ExternalInput").ap()
    w_out = nc.dram_tensor("w_out", [DI, D], F32, kind="ExternalInput").ap()
    f_w1 = nc.dram_tensor("f_w1", [DH, D], F32, kind="ExternalInput").ap()
    f_cw = nc.dram_tensor("f_cw", [DH, 3], F32, kind="ExternalInput").ap()
    f_cb = nc.dram_tensor("f_cb", [DH], F32, kind="ExternalInput").ap()
    f_w2 = nc.dram_tensor("f_w2", [D, DH], F32, kind="ExternalInput").ap()
    y_out = nc.dram_tensor("y_out", [B, L, D], F32, kind="ExternalOutput").ap()

    _scr_kind0 = "ExternalOutput" if DEBUG_SCRATCH else "Internal"
    wih_s = nc.dram_tensor("wih_s", [2 * DI // 128, 8, 128, 128], BF16, kind=_scr_kind0).ap()
    wil_s = nc.dram_tensor("wil_s", [2 * DI // 128, 8, 128, 128], BF16, kind=_scr_kind0).ap()
    woh_s = nc.dram_tensor("woh_s", [DI, D], BF16, kind="Internal").ap()
    wol_s = nc.dram_tensor("wol_s", [DI, D], BF16, kind="Internal").ap()
    w1qt_s = nc.dram_tensor("w1qt_s", [DMCH, 128, DH], BF16, kind=_scr_kind2).ap()
    w2qt_s = nc.dram_tensor("w2qt_s", [DH, D], BF16, kind=_scr_kind2).ap()
    bc_s = nc.dram_tensor("bc_s", [2, 2 * S, T], F32, kind="Internal").ap()
    _scr_kind = "ExternalOutput" if DEBUG_SCRATCH else "Internal"
    x2_s = nc.dram_tensor("x2_s", [B, L, D], F32, kind=_scr_kind).ap()
    zsl_s = nc.dram_tensor("zsl_s", [DCH, 128, T], F32, kind=_scr_kind).ap()
    h_scr = nc.dram_tensor("h_scr", [DH, L], F32, kind=_scr_kind).ap()
    dbg_c = nc.dram_tensor("dbg_c", [128, 4096], F32, kind=_scr_kind).ap() if DEBUG_SCRATCH else None
    dbg_b = nc.dram_tensor("dbg_b", [128, 8192], BF16, kind=_scr_kind).ap() if DEBUG_SCRATCH else None
    dbg_k = nc.dram_tensor("dbg_k", [128, DMCH, L], BF16, kind=_scr_kind).ap() if DEBUG_SCRATCH else None
    s2r_s = nc.dram_tensor("s2r_s", [B, L], F32, kind="Internal").ap()
    i1r_s = nc.dram_tensor("i1r_s", [B, L], F32, kind="Internal").ap()

    es = ExitStack()
    with es:
        def sb(name, shape, dt=F32):
            return es.enter_context(nc.sbuf_tensor(name, shape, dt))

        wxt = sb("wxt", [128, DCH, 96])
        cwt = sb("cwt", [128, DCH, 4])
        cbt = sb("cbt", [128, DCH])
        bdtt = sb("bdtt", [128, DCH])
        dvt = sb("dvt", [128, DCH])
        ang = sb("ang", [128, DCH, S])
        fcwt = sb("fcwt", [128, HCH, 3])
        fcbt = sb("fcbt", [128, HCH])
        idn = sb("idn", [128, 128])
        idnb = sb("idnb", [128, 128], BF16)
        onescol = sb("onescol", [1, 128])
        ones128 = sb("ones128", [128, 1])
        k1c = sb("k1c", [128, 1])
        k2c = sb("k2c", [128, 1])
        sw1b = sb("sw1b", [128, 1])
        sw2b = sb("sw2b", [128, 1])
        hstate = sb("hstate", [128, DCH, S])
        xtail = sb("xtail", [128, DCH, 3])
        scr1 = sb("scr1", [128, 1024])
        scrb = sb("scrb", [128, 2048], BF16)
        scrs = sb("scrs", [128, 16])
        cst = sb("cst", [128, 2])
        projt = sb("projt", [96, T])
        inv2t = sb("inv2t", [128, 16])

        block = es.enter_context(nc.Block())
        ctx = Ctx(nc, block, es)
        ctx.mksem("dmw", "dma", "dmo", "pe", "dve", "sce", "gp",
                  "dwx0", "dwx1", "dwo0", "dwo1", "dh0", "dh1", "dw20", "dw21",
                  "dho", "dxo", "dyo", "dzs", "dzl", "tileb")

        def on(name):
            dec = getattr(block, name)
            if not DEV_DRAINS or name in ("sync", "tensor"):
                return dec

            def wrap(fn):
                return dec(lambda e: fn(_DrainProxy(e)))
            return wrap

        SET_LNEXP = 6
        SET_SILU = 18

        def act_set(e, set_id):
            e.drain()
            e.add_instruction(mybir.InstLoadActFuncSet(
                name=nc.get_next_instruction_name(), act_func_set_id=set_id, ins=[], outs=[]))
            e.drain()

        def tpose(e, out, in_, ident, start, stop):
            return e.matmul(out, in_, ident, is_transpose=True, start=start, stop=stop)

        def emit_ln(sqbuf, lnfbuf, src):
            @on("vector")
            def _(e):
                e.reduce_sum(out=scrs[:, 0:1], in_=src, axis=AX.X); ctx.fence_inc(e, "dve")
            dve_s = ctx.val("dve")

            @on("scalar")
            def _(e, dve_s=dve_s):
                ctx.wait(e, "dve", dve_s)
                e.activation(out=sqbuf, in_=src, func=ACT.Square).then_inc(*ctx.bump("sce", 1))
            sce_sq = ctx.val("sce")

            @on("vector")
            def _(e, sce_sq=sce_sq):
                ctx.wait(e, "sce", sce_sq)
                e.reduce_sum(out=scrs[:, 1:2], in_=sqbuf, axis=AX.X)
                e.drain()
                e.tensor_single_scalar(out=scrs[:, 2:3], in_=scrs[:, 0:1], scalar=1.0 / D, op=ALU.mult)
                e.tensor_single_scalar(out=scrs[:, 3:4], in_=scrs[:, 1:2], scalar=1.0 / D, op=ALU.mult)
                e.drain()
                e.tensor_tensor(out=scrs[:, 4:5], in0=scrs[:, 2:3], in1=scrs[:, 2:3], op=ALU.mult)
                e.drain()
                e.tensor_tensor(out=scrs[:, 4:5], in0=scrs[:, 3:4], in1=scrs[:, 4:5], op=ALU.subtract)
                e.drain(); ctx.fence_inc(e, "dve")
            dve_v = ctx.val("dve")

            @on("scalar")
            def _(e, dve_v=dve_v):
                ctx.wait(e, "dve", dve_v)
                e.activation(out=scrs[:, 5:6], in_=scrs[:, 4:5], func=ACT.Ln, bias=cst[:, 0:1])
                e.drain()
                e.activation(out=scrs[:, 6:7], in_=scrs[:, 5:6], func=ACT.Exp, scale=-0.5)
                e.drain()
                e.activation(out=scrs[:, 6:7], in_=scrs[:, 6:7], func=ACT.Identity).then_inc(*ctx.bump("sce", 1))
            sce_r = ctx.val("sce")

            @on("vector")
            def _(e, sce_r=sce_r):
                ctx.wait(e, "sce", sce_r)
                e.tensor_tensor(out=scrs[:, 7:8], in0=scrs[:, 2:3], in1=scrs[:, 6:7], op=ALU.mult)
                e.drain()
                e.tensor_single_scalar(out=scrs[:, 7:8], in_=scrs[:, 7:8], scalar=-1.0, op=ALU.mult)
                e.drain(); ctx.fence_inc(e, "dve")
            dve_mb = ctx.val("dve")

            @on("scalar")
            def _(e, dve_mb=dve_mb):
                ctx.wait(e, "dve", dve_mb)
                e.activation(out=lnfbuf, in_=src, func=ACT.Identity,
                             scale=scrs[:, 6:7], bias=scrs[:, 7:8]).then_inc(*ctx.bump("sce", 1))
            return ctx.val("sce")

        with nc.sbuf_tensor("pscrA", [128, DH], F32) as pscrA, \
             nc.sbuf_tensor("pscrB", [128, DH], BF16) as pscrB, \
             nc.sbuf_tensor("pscrC", [128, DH], BF16) as pscrC, \
             nc.sbuf_tensor("pscrA2", [128, DH], F32) as pscrA2, \
             nc.sbuf_tensor("pscrB2", [128, DH], BF16) as pscrB2, \
             nc.sbuf_tensor("pscrC2", [128, DH], BF16) as pscrC2, \
             nc.sbuf_tensor("pscrM0", [128, DH], F32) as pscrM0, \
             nc.sbuf_tensor("pscrM1", [128, DH], F32) as pscrM1, \
             nc.sbuf_tensor("stag", [128, HCH, 128], BF16) as stag, \
             nc.sbuf_tensor("stag2", [128, HCH, 128], BF16) as stag2, \
             nc.sbuf_tensor("cw4", [4, DI], F32) as cw4, \
             nc.sbuf_tensor("vec3", [1, 3 * DI + DH], F32) as vec3, \
             nc.sbuf_tensor("msum", [128, 2], F32) as msum, \
             nc.sbuf_tensor("smv", [1, 8], F32) as smv, \
             nc.psum_tensor("pps0", [128, 512], F32) as pps0, \
             nc.psum_tensor("ppsb0", [128, 512], BF16) as ppsb0, \
             nc.psum_tensor("ppsb1", [128, 512], BF16) as ppsb1:

            @on("sync")
            def _(e):
                e.dma_start(out=wxt[:, :, :], in_=w_x.rearrange("(c p) m -> p c m", p=128)).then_inc(*ctx.bump("dmw", 16))
                e.dma_start(out=ang[:, :, :], in_=a_log.rearrange("(c p) s -> p c s", p=128)).then_inc(*ctx.bump("dmw", 16))
                e.dma_start(out=fcwt[:, :, :], in_=f_cw.rearrange("(c p) k -> p c k", p=128)).then_inc(*ctx.bump("dmw", 16))
                e.dma_start(out=cw4[:, :], in_=conv_w).then_inc(*ctx.bump("dmw", 16))
                e.dma_start(out=vec3[:, 0:DI], in_=conv_b.unsqueeze(0)).then_inc(*ctx.bump("dmw", 16))
                e.dma_start(out=vec3[:, DI:2 * DI], in_=b_dt.unsqueeze(0)).then_inc(*ctx.bump("dmw", 16))
                e.dma_start(out=vec3[:, 2 * DI:3 * DI], in_=d_vec.unsqueeze(0)).then_inc(*ctx.bump("dmw", 16))
                e.dma_start(out=vec3[:, 3 * DI:3 * DI + DH], in_=f_cb.unsqueeze(0)).then_inc(*ctx.bump("dmw", 16))
            dmw0 = ctx.val("dmw")

            @on("gpsimd")
            def _(e):
                e.iota(out=scr1[:, 0:128], pattern=[[-1, 128]], base=0,
                       channel_multiplier=1, allow_small_or_imprecise_dtypes=True); ctx.fence_inc(e, "gp")
            gp_i = ctx.val("gp")

            @on("vector")
            def _(e, gp_i=gp_i, dmw0=dmw0):
                ctx.wait(e, "gp", gp_i)
                e.tensor_single_scalar(out=idn[:], in_=scr1[:, 0:128], scalar=0.0, op=ALU.is_equal)
                e.tensor_copy(idnb[:], idn[:])
                e.memset(onescol[:], 1.0)
                e.memset(ones128[:], 1.0)
                e.memset(scrs[:, 15:16], 0.0)
                e.memset(cst[:, 0:1], EPS)
                e.memset(cst[:, 1:2], 1.0)
                ctx.wait(e, "dmw", dmw0)
                e.engine_nop().then_inc(*ctx.bump("dve", 1))
            dve0 = ctx.val("dve")

            @on("scalar")
            def _(e, dve0=dve0):
                ctx.wait(e, "dve", dve0)
                e.activation(out=ang[:, :, :], in_=ang[:, :, :], func=ACT.Exp, scale=1.0).then_inc(*ctx.bump("sce", 1))
            sce_a0 = ctx.val("sce")

            @on("vector")
            def _(e, sce_a0=sce_a0):
                ctx.wait(e, "sce", sce_a0)
                e.tensor_single_scalar(out=ang[:, :, :], in_=ang[:, :, :], scalar=-1.0, op=ALU.mult); ctx.fence_inc(e, "dve")
            dve_ang = ctx.val("dve")

            @on("tensor")
            def _(e, dve_ang=dve_ang):
                ctx.wait(e, "dve", dve_ang)
                n_tot = DCH + DCH + DCH + DCH + HCH
                k = 0
                for c in range(DCH):
                    tpose(e, pps0[:, c * 4:(c + 1) * 4], cw4[:, c * 128:(c + 1) * 128], idn[0:4, 0:4], k == 0, k == n_tot - 1); k += 1
                for c in range(DCH):
                    tpose(e, pps0[:, 64 + c:65 + c], vec3[:, c * 128:(c + 1) * 128], idn[0:1, 0:1], k == 0, k == n_tot - 1); k += 1
                for c in range(DCH):
                    tpose(e, pps0[:, 80 + c:81 + c], vec3[:, DI + c * 128:DI + (c + 1) * 128], idn[0:1, 0:1], k == 0, k == n_tot - 1); k += 1
                for c in range(DCH):
                    tpose(e, pps0[:, 96 + c:97 + c], vec3[:, 2 * DI + c * 128:2 * DI + (c + 1) * 128], idn[0:1, 0:1], k == 0, k == n_tot - 1); k += 1
                for c in range(HCH):
                    mm = tpose(e, pps0[:, 112 + c:113 + c], vec3[:, 3 * DI + c * 128:3 * DI + (c + 1) * 128], idn[0:1, 0:1], k == 0, k == n_tot - 1); k += 1
                ctx.inc(mm, "pe", 1)
            pe_tp = ctx.val("pe")

            @on("scalar")
            def _(e, pe_tp=pe_tp):
                ctx.wait(e, "pe", pe_tp)
                e.copy(out=cwt[:, :, :], in_=pps0[:, 0:64].rearrange("p (c k) -> p c k", k=4))
                e.copy(out=cbt[:, :], in_=pps0[:, 64:80])
                e.copy(out=bdtt[:, :], in_=pps0[:, 80:96])
                e.copy(out=dvt[:, :], in_=pps0[:, 96:112])
                e.copy(out=fcbt[:, :], in_=pps0[:, 112:144]).then_inc(*ctx.bump("sce", 1))

            wsp_dve = {}
            wsp_st = {}
            gci = 0
            for (src, dsth, dstl, rows, cols) in ((w_in, wih_s, wil_s, D, 2 * DI),
                                                  (w_out, woh_s, wol_s, DI, D)):
                for j in range(rows // 128):
                    par = gci % 2
                    pA = pscrA if par == 0 else pscrA2
                    pB = pscrB if par == 0 else pscrB2
                    pC = pscrC if par == 0 else pscrC2

                    @on("sync")
                    def _(e, src=src, j=j, cols=cols, pA=pA, w_dve=wsp_dve.get(gci - 2)):
                        if w_dve is not None:
                            ctx.wait(e, "dve", w_dve)
                        e.dma_start(out=pA[:, 0:cols], in_=src[j * 128:(j + 1) * 128, :]).then_inc(*ctx.bump("dma", 16))
                    dma_t = ctx.val("dma")

                    @on("scalar")
                    def _(e, dma_t=dma_t, cols=cols, pA=pA, pB=pB, w_st=wsp_st.get(gci - 2)):
                        ctx.wait(e, "dma", dma_t)
                        if w_st is not None:
                            ctx.wait(e, "dmo", w_st)
                        e.copy(out=pB[:, 0:cols], in_=pA[:, 0:cols]).then_inc(*ctx.bump("sce", 1))
                    sce_t = ctx.val("sce")

                    @on("vector")
                    def _(e, sce_t=sce_t, cols=cols, pA=pA, pB=pB, pC=pC, w_st=wsp_st.get(gci - 2)):
                        ctx.wait(e, "sce", sce_t)
                        if w_st is not None:
                            ctx.wait(e, "dmo", w_st)
                        e.tensor_tensor(out=pC[:, 0:cols], in0=pA[:, 0:cols], in1=pB[:, 0:cols], op=ALU.subtract); ctx.fence_inc(e, "dve")
                    dve_t = ctx.val("dve")
                    wsp_dve[gci] = dve_t

                    @on("gpsimd")
                    def _(e, dve_t=dve_t, dsth=dsth, dstl=dstl, j=j, cols=cols, rows=rows, pB=pB, pC=pC):
                        ctx.wait(e, "dve", dve_t)
                        if rows == D:
                            e.dma_start(out=dsth[:, j, :, :].transpose([1, 0, 2]),
                                        in_=pB[:, 0:cols].rearrange("p (cc m) -> p cc m", m=128)).then_inc(*ctx.bump("dmo", 16))
                            e.dma_start(out=dstl[:, j, :, :].transpose([1, 0, 2]),
                                        in_=pC[:, 0:cols].rearrange("p (cc m) -> p cc m", m=128)).then_inc(*ctx.bump("dmo", 16))
                        else:
                            e.dma_start(out=dsth[j * 128:(j + 1) * 128, :], in_=pB[:, 0:cols]).then_inc(*ctx.bump("dmo", 16))
                            e.dma_start(out=dstl[j * 128:(j + 1) * 128, :], in_=pC[:, 0:cols]).then_inc(*ctx.bump("dmo", 16))
                    wsp_st[gci] = ctx.val("dmo")
                    gci += 1

            @on("vector")
            def _(e):
                ctx.wait(e, "dve", ctx.val("dve"))
                e.memset(msum[:], 0.0)
                e.engine_nop().then_inc(*ctx.bump("dve", 1))

            ma_dve = {}
            mj = 0
            for col_ix, (src, nch, cols) in enumerate(((f_w1, HCH, D), (f_w2, DMCH, DH))):
                for j in range(nch):
                    pM = pscrM0 if mj % 2 == 0 else pscrM1

                    @on("sync")
                    def _(e, src=src, j=j, cols=cols, pM=pM, w_dve=ma_dve.get(mj - 2)):
                        if w_dve is not None:
                            ctx.wait(e, "dve", w_dve)
                        e.dma_start(out=pM[:, 0:cols], in_=src[j * 128:(j + 1) * 128, :]).then_inc(*ctx.bump("dma", 16))
                    dma_t = ctx.val("dma")

                    @on("vector")
                    def _(e, dma_t=dma_t, cols=cols, col_ix=col_ix, pM=pM):
                        ctx.wait(e, "dma", dma_t)
                        e.reduce_sum(out=scrs[:, 0:1], in_=pM[:, 0:cols], axis=AX.X, apply_absolute_value=True)
                        e.drain()
                        e.tensor_tensor(out=msum[:, col_ix:col_ix + 1], in0=msum[:, col_ix:col_ix + 1],
                                        in1=scrs[:, 0:1], op=ALU.add)
                        e.drain(); ctx.fence_inc(e, "dve")
                    ma_dve[mj] = ctx.val("dve")
                    mj += 1
            dve_ms = ctx.val("dve")

            @on("tensor")
            def _(e, dve_ms=dve_ms):
                ctx.wait(e, "dve", dve_ms)
                e.matmul(pps0[0:1, 0:1], msum[:, 0:1], ones128[:, :], start=True, stop=False)
                mm = e.matmul(pps0[0:1, 1:2], msum[:, 1:2], ones128[:, :], start=False, stop=True)
                ctx.inc(mm, "pe", 1)
            pe_ms = ctx.val("pe")

            @on("vector")
            def _(e, pe_ms=pe_ms):
                ctx.wait(e, "pe", pe_ms)
                e.tensor_copy(smv[:, 0:2], pps0[0:1, 0:2])
                e.drain()
                e.tensor_single_scalar(out=smv[:, 2:3], in_=smv[:, 0:1], scalar=1.0 / (DH * D), op=ALU.mult)
                e.tensor_single_scalar(out=smv[:, 3:4], in_=smv[:, 1:2], scalar=1.0 / (DH * D), op=ALU.mult)
                e.drain()
                e.tensor_single_scalar(out=smv[:, 2:3], in_=smv[:, 2:3], scalar=1e-5, op=ALU.max)
                e.tensor_single_scalar(out=smv[:, 3:4], in_=smv[:, 3:4], scalar=1e-5, op=ALU.max)
                e.drain()
                e.reciprocal(out=smv[:, 4:5], in_=smv[:, 2:3])
                e.reciprocal(out=smv[:, 5:6], in_=smv[:, 3:4])
                e.tensor_single_scalar(out=smv[:, 6:7], in_=smv[:, 2:3], scalar=1.0 / 127.0, op=ALU.mult)
                e.tensor_single_scalar(out=smv[:, 7:8], in_=smv[:, 3:4], scalar=1.0 / 127.0, op=ALU.mult)
                e.drain(); ctx.fence_inc(e, "dve")
            dve_sw = ctx.val("dve")

            @on("tensor")
            def _(e, dve_sw=dve_sw):
                ctx.wait(e, "dve", dve_sw)
                mm = e.matmul(pps0[:, 0:8], onescol[:, :], smv[:, 0:8], start=True, stop=True)
                ctx.inc(mm, "pe", 1)
            pe_bc = ctx.val("pe")

            @on("scalar")
            def _(e, pe_bc=pe_bc):
                ctx.wait(e, "pe", pe_bc)
                e.copy(out=sw1b[:, :], in_=pps0[:, 4:5])
                e.copy(out=sw2b[:, :], in_=pps0[:, 5:6])
                e.copy(out=k1c[:, :], in_=pps0[:, 6:7])
                e.copy(out=k2c[:, :], in_=pps0[:, 7:8]).then_inc(*ctx.bump("sce", 1))
            sce_sw = ctx.val("sce")

            tern_dve = {}
            tern_pe = {}
            tern_st = {}
            bank_evac = {}
            tj = 0
            for wi, (src, swb, nch, cols, dst) in enumerate(
                    ((f_w1, sw1b, HCH, D, w1qt_s), (f_w2, sw2b, DMCH, DH, w2qt_s))):
                njt = cols // 128
                for j in range(nch):
                    par = tj % 2
                    pA = pscrA if par == 0 else pscrA2
                    pB = pscrB if par == 0 else pscrB2
                    pC = pscrC if par == 0 else pscrC2
                    stg = stag if par == 0 else stag2
                    w_dve = tern_dve.get(tj - 2, wsp_dve[22 + par])
                    w_st0 = wsp_st[22 + par] if tj < 2 else None

                    @on("sync")
                    def _(e, src=src, j=j, cols=cols, pA=pA, w_dve=w_dve):
                        ctx.wait(e, "dve", w_dve)
                        e.dma_start(out=pA[:, 0:cols], in_=src[j * 128:(j + 1) * 128, :]).then_inc(*ctx.bump("dma", 16))
                    dma_t = ctx.val("dma")

                    @on("vector")
                    def _(e, dma_t=dma_t, cols=cols, swb=swb, first=(wi == 0 and j == 0), sce_sw=sce_sw,
                          pA=pA, pB=pB, pC=pC, w_pe=tern_pe.get(tj - 2), w_st0=w_st0):
                        ctx.wait(e, "dma", dma_t)
                        if first:
                            ctx.wait(e, "sce", sce_sw)
                        if w_pe is not None:
                            ctx.wait(e, "pe", w_pe)
                        if w_st0 is not None:
                            ctx.wait(e, "dmo", w_st0)
                        e.tensor_scalar(out=pB[:, 0:cols], in0=pA[:, 0:cols], scalar1=swb[:, :],
                                        scalar2=0.5, op0=ALU.mult, op1=ALU.is_gt)
                        e.tensor_scalar(out=pC[:, 0:cols],
                                        in0=pA[:, 0:cols], scalar1=swb[:, :],
                                        scalar2=-0.5, op0=ALU.mult, op1=ALU.is_lt)
                        e.tensor_tensor(out=pC[:, 0:cols], in0=pB[:, 0:cols], in1=pC[:, 0:cols],
                                        op=ALU.subtract); ctx.fence_inc(e, "dve")
                    dve_t = ctx.val("dve")
                    tern_dve[tj] = dve_t

                    nblk = (njt + 3) // 4
                    for blk in range(nblk):
                        n_in = min(4, njt - blk * 4)
                        bank = blk % 2
                        pb = ppsb0 if bank == 0 else ppsb1

                        @on("tensor")
                        def _(e, dve_t=dve_t, blk=blk, n_in=n_in, pb=pb, pC=pC, bev=bank_evac.get(bank)):
                            if blk == 0:
                                ctx.wait(e, "dve", dve_t)
                            if bev is not None:
                                ctx.wait(e, "sce", bev)
                            for q in range(n_in):
                                jj = blk * 4 + q
                                mm = tpose(e, pb[:, q * 128:(q + 1) * 128], pC[:, jj * 128:(jj + 1) * 128],
                                           idnb[:, :], q == 0, q == n_in - 1)
                            ctx.inc(mm, "pe", 1)
                        pe_b = ctx.val("pe")

                        @on("scalar")
                        def _(e, pe_b=pe_b, blk=blk, n_in=n_in, pb=pb, stg=stg,
                              w_st=tern_st.get(tj - 2) if blk == 0 else None):
                            ctx.wait(e, "pe", pe_b)
                            if w_st is not None:
                                ctx.wait(e, "dmo", w_st)
                            e.copy(out=stg[:, blk * 4:blk * 4 + n_in, :], in_=pb[:, 0:n_in * 128].rearrange("p (q x) -> p q x", x=128)).then_inc(*ctx.bump("sce", 1))
                        bank_evac[bank] = ctx.val("sce")
                    tern_pe[tj] = ctx.val("pe")
                    sce_st = ctx.val("sce")

                    @on("gpsimd")
                    def _(e, sce_st=sce_st, wi=wi, j=j, njt=njt, dst=dst, stg=stg):
                        ctx.wait(e, "sce", sce_st)
                        if wi == 0:
                            e.dma_start(out=dst[:, :, j * 128:(j + 1) * 128].transpose([1, 0, 2]),
                                        in_=stg[:, 0:njt, :]).then_inc(*ctx.bump("dmo", 16))
                        else:
                            e.dma_start(out=dst[:, j * 128:(j + 1) * 128].rearrange("(a p) m -> a p m", p=128).transpose([1, 0, 2]),
                                        in_=stg[:, 0:njt, :]).then_inc(*ctx.bump("dmo", 16))
                    tern_st[tj] = ctx.val("dmo")
                    tj += 1

            @on("sync")
            def _(e):
                ctx.wait(e, "dmo", ctx.val("dmo"))

        ctx.barrier(on, ("dmo", "dma", "dmw"))
        if DEBUG_SCRATCH:
            @on("sync")
            def _(e):
                ctx.wait(e, "sce", ctx.val("sce"))
                ctx.wait(e, "dve", ctx.val("dve"))
                e.dma_start(out=dbg_c[:, 0:256], in_=ang[:, :, :].rearrange("p c s -> p (c s)")).then_inc(*ctx.bump("dmo", 16))
                e.dma_start(out=dbg_c[:, 256:320], in_=cwt[:, :, :].rearrange("p c k -> p (c k)")).then_inc(*ctx.bump("dmo", 16))
                e.dma_start(out=dbg_c[:, 320:336], in_=cbt[:, :]).then_inc(*ctx.bump("dmo", 16))
                e.dma_start(out=dbg_c[:, 336:352], in_=bdtt[:, :]).then_inc(*ctx.bump("dmo", 16))
                e.dma_start(out=dbg_c[:, 352:368], in_=dvt[:, :]).then_inc(*ctx.bump("dmo", 16))
                e.dma_start(out=dbg_c[:, 368:400], in_=fcbt[:, :]).then_inc(*ctx.bump("dmo", 16))
                e.dma_start(out=dbg_c[:, 400:496], in_=fcwt[:, :, :].rearrange("p c k -> p (c k)")).then_inc(*ctx.bump("dmo", 16))
                e.dma_start(out=dbg_c[:, 512:2048], in_=wxt[:, :, :].rearrange("p c m -> p (c m)")).then_inc(*ctx.bump("dmo", 16))
        with ExitStack() as mam_es:
            xcv = mam_es.enter_context(nc.sbuf_tensor("xcv", [128, DCH, 3 + T], F32))
            zbuf = mam_es.enter_context(nc.sbuf_tensor("zbuf", [128, 2, T], F32))
            dta = mam_es.enter_context(nc.sbuf_tensor("dta", [128, DCH, T], F32))
            bcgB = mam_es.enter_context(nc.sbuf_tensor("bcgB", [128, 2, SGS, T], F32))
            agrp = mam_es.enter_context(nc.sbuf_tensor("agrp", [128, SGS, T], F32))
            ugrp = mam_es.enter_context(nc.sbuf_tensor("ugrp", [128, 2, SGS, T], F32))
            ln1Tb = mam_es.enter_context(nc.sbuf_tensor("ln1Tb", [128, 2, 2, DMCH, T], BF16))
            alnb = mam_es.enter_context(nc.sbuf_tensor("alnb", [128, 1024], F32))
            wxzb = mam_es.enter_context(nc.sbuf_tensor("wxzb", [128, 2, 2, 8, 128], BF16))

            a_ln_prev = {}

            def a_xview(pp, i):
                return ln1Tb[:, 1 - pp].rearrange("p a j t -> p (a j t)").bitcast(F32)[:, (i % 2) * 1024:(i % 2) * 1024 + 1024]

            def emit_A_load(b, tile, pp, i, x_pe_guard=None):
                t0 = tile * T
                xab = a_xview(pp, i)

                @on("sync")
                def _(e, b=b, t0=t0, i=i, w_sce=a_ln_prev.get(("n", i - 2)), x_pe_guard=x_pe_guard, xab=xab):
                    if w_sce is not None:
                        ctx.wait(e, "sce", w_sce)
                    if x_pe_guard is not None:
                        ctx.wait(e, "pe", x_pe_guard)
                    e.dma_start(out=xab, in_=x_in[b, t0 + i * 128:t0 + (i + 1) * 128, :]).then_inc(*ctx.bump("dma", 16))
                a_ln_prev[("x", i)] = ctx.val("dma")

            def emit_A_sub(b, tile, pp, i, psA0, psA1):
                xab = a_xview(pp, i)
                dma_x = a_ln_prev[("x", i)]

                @on("vector")
                def _(e, dma_x=dma_x):
                    ctx.wait(e, "dma", dma_x)
                    e.engine_nop().then_inc(*ctx.bump("dve", 1))

                sce_ln = emit_ln(alnb[:, 0:1024], alnb[:, 0:1024], xab)
                a_ln_prev[("n", i)] = sce_ln

                @on("vector")
                def _(e, sce_ln=sce_ln):
                    ctx.wait(e, "sce", sce_ln)
                    ctx.wait(e, "pe", ctx.val("pe"))
                    e.tensor_copy(scrb[:, 0:1024], alnb[:, 0:1024])
                    e.tensor_tensor(out=scrb[:, 1024:2048], in0=alnb[:, 0:1024], in1=scrb[:, 0:1024], op=ALU.subtract); ctx.fence_inc(e, "dve")
                dve_c = ctx.val("dve")

                for hl in range(2):
                    for half in range(2):
                        pb = psA0 if half == 0 else psA1

                        @on("tensor")
                        def _(e, dve_c=dve_c, hl=hl, half=half, pb=pb):
                            if hl == 0 and half == 0:
                                ctx.wait(e, "dve", dve_c)
                            ctx.wait(e, "sce", ctx.val("sce"))
                            for q in range(4):
                                j = half * 4 + q
                                mm = tpose(e, pb[:, q * 128:(q + 1) * 128],
                                           scrb[:, hl * 1024 + j * 128:hl * 1024 + (j + 1) * 128],
                                           idnb[:, :], q == 0, q == 3)
                            ctx.inc(mm, "pe", 1)
                        pe_h = ctx.val("pe")

                        @on("scalar")
                        def _(e, pe_h=pe_h, hl=hl, half=half, i=i, pb=pb, pp=pp):
                            ctx.wait(e, "pe", pe_h)
                            e.copy(out=ln1Tb[:, pp, hl, half * 4:(half + 1) * 4, i * 128:(i + 1) * 128],
                                   in_=pb[:, :].rearrange("p (q x) -> p q x", x=128)).then_inc(*ctx.bump("sce", 1))

            hoist_sce = {}
            prev_b_mm = {}

            for b in range(B):
                @on("vector")
                def _(e):
                    ctx.wait(e, "gp", ctx.val("gp"))
                    e.memset(hstate[:], 0.0)
                    e.memset(xtail[:], 0.0)
                    e.engine_nop().then_inc(*ctx.bump("dve", 1))

                pe_wxz_prev = {}
                pe_wo_prev = {}
                for tile in range(NT):
                    t0 = tile * T
                    ctx.barrier(on, ("dxo", "dyo", "dho"))
                    gt = b * NT + tile
                    pp = gt % 2

                    if gt == 0:
                        with nc.psum_tensor("psAi0", [128, 512], BF16) as psAi0, \
                             nc.psum_tensor("psAi1", [128, 512], BF16) as psAi1:
                            emit_A_load(0, 0, 0, 0)
                            emit_A_load(0, 0, 0, 1)
                            for i in range(NSUB):
                                emit_A_sub(0, 0, 0, i, psAi0, psAi1)
                                if i + 2 < NSUB:
                                    emit_A_load(0, 0, 0, i + 2)
                            hoist_sce[0] = ctx.val("sce")
                    sce_ln1T = hoist_sce[gt]

                    @on("sync")
                    def _(e, pp=pp, w_pe=prev_b_mm.get(gt - 1), w_sce=hoist_sce[gt]):
                        if w_pe is not None:
                            ctx.wait(e, "pe", w_pe)
                        ctx.wait(e, "sce", w_sce)
                        wdv_ = ln1Tb[:, 1 - pp].rearrange("p a j t -> p (a j t)").bitcast(F32)
                        e.dma_start(out=wdv_[0:64, 0:DI], in_=w_dt).then_inc(*ctx.bump("dmw", 16))
                    dmw_dt = ctx.val("dmw")

                    if True:
                        with nc.psum_tensor(f"psP_{b}_{tile}", [96, T], F32) as psP, \
                             nc.psum_tensor(f"psX0_{b}_{tile}", [128, T], F32) as psX0, \
                             nc.psum_tensor(f"psX1_{b}_{tile}", [128, T], F32) as psX1, \
                             nc.psum_tensor(f"psZ0_{b}_{tile}", [128, T], F32) as psZ0, \
                             nc.psum_tensor(f"psZ1_{b}_{tile}", [128, T], F32) as psZ1:
                            sce_evac_prev = {}
                            zstore_rec = {}
                            dve_bguard = ctx.val("dve")
                            for c in range(DCH):
                                par = c % 2

                                wsem = "dwx0" if par == 0 else "dwx1"

                                @on("sync")
                                def _(e, c=c, wsem=wsem):
                                    ctx.wait(e, wsem, ctx.val(wsem))
                                    pv = pe_wxz_prev.get("both")
                                    if pv is not None:
                                        ctx.wait(e, "pe", pv)
                                    for hix in range(2):
                                        cc = c + hix * (DI // 128)
                                        e.dma_start(out=wxzb[:, hix, 0, :, :],
                                                    in_=wih_s[cc, :, :, :].transpose([1, 0, 2])).then_inc(*ctx.bump(wsem, 16))
                                        e.dma_start(out=wxzb[:, hix, 1, :, :],
                                                    in_=wil_s[cc, :, :, :].transpose([1, 0, 2])).then_inc(*ctx.bump(wsem, 16))
                                dmw_c = ctx.val(wsem)

                                for hix in range(2):
                                    ps = (psX0 if par == 0 else psX1) if hix == 0 else (psZ0 if par == 0 else psZ1)

                                    @on("tensor")
                                    def _(e, dmw_c=dmw_c, ps=ps, par=par, hix=hix, sce_ln1T=sce_ln1T, wsem=wsem,
                                          c=c, pp=pp, dve_bguard=dve_bguard):
                                        ctx.wait(e, wsem, dmw_c)
                                        if c == 0 and hix == 0:
                                            ctx.wait(e, "dve", dve_bguard)
                                        ctx.wait(e, "sce", max(sce_ln1T, sce_evac_prev.get(par) or 0))
                                        n = 0
                                        for (wl, al) in ((0, 0), (0, 1), (1, 0)):
                                            for k in range(8):
                                                n += 1
                                                mm = e.matmul(ps[:, :], wxzb[:, hix, wl, k, :], ln1Tb[:, pp, al, k, :],
                                                              start=(n == 1), stop=(n == 24))
                                        ctx.inc(mm, "pe", 1)
                                pe_wxz_prev["both"] = ctx.val("pe")
                                pe_xz = ctx.val("pe")

                                psx = psX0 if par == 0 else psX1
                                psz = psZ0 if par == 0 else psZ1

                                @on("scalar")
                                def _(e, pe_xz=pe_xz, c=c, psx=psx, psz=psz, w_dzs=zstore_rec.get(c - 2)):
                                    ctx.wait(e, "pe", pe_xz)
                                    ctx.wait(e, "dve", ctx.val("dve"))
                                    if w_dzs is not None:
                                        ctx.wait(e, "dzs", w_dzs)
                                    e.copy(out=xcv[:, c, 3:3 + T], in_=psx[:, :])
                                    e.activation(out=zbuf[:, c % 2, :], in_=psz[:, :], func=ACT.Silu).then_inc(*ctx.bump("sce", 1))
                                sce_ev = ctx.val("sce")
                                sce_evac_prev[par] = sce_ev

                                @on("gpsimd")
                                def _(e, c=c, sce_ev=sce_ev):
                                    ctx.wait(e, "sce", sce_ev)
                                    e.dma_start(out=zsl_s[c, :, :], in_=zbuf[:, c % 2, :]).then_inc(*ctx.bump("dzs", 16))
                                zstore_rec[c] = ctx.val("dzs")

                                @on("vector")
                                def _(e, sce_ev=sce_ev, c=c):
                                    ctx.wait(e, "sce", sce_ev)
                                    e.tensor_copy(xcv[:, c, 0:3], xtail[:, c, :])
                                    cacc = scr1[:, 0:T]
                                    z1 = scrs[:, 15:16]
                                    e.scalar_tensor_tensor(out=cacc, in0=xcv[:, c, 0:T], scalar=cwt[:, c, 0:1],
                                                           in1=z1.broadcast_to([128, T]), op0=ALU.mult, op1=ALU.add)
                                    for k in range(1, 4):
                                        e.scalar_tensor_tensor(out=cacc, in0=xcv[:, c, k:k + T], scalar=cwt[:, c, k:k + 1],
                                                               in1=cacc, op0=ALU.mult, op1=ALU.add)
                                    e.tensor_copy(xtail[:, c, :], xcv[:, c, T:T + 3]); ctx.fence_inc(e, "dve")
                                dve_cv = ctx.val("dve")

                                @on("scalar")
                                def _(e, dve_cv=dve_cv, c=c):
                                    ctx.wait(e, "dve", dve_cv)
                                    e.activation(out=xcv[:, c, 3:3 + T], in_=scr1[:, 0:T], func=ACT.Silu,
                                                 bias=cbt[:, c:c + 1]).then_inc(*ctx.bump("sce", 1))
                                sce_sl = ctx.val("sce")

                            prev_b_mm[gt] = ctx.val("pe")
                            sce_all_silu = ctx.val("sce")

                            @on("tensor")
                            def _(e, sce_all_silu=sce_all_silu):
                                ctx.wait(e, "sce", sce_all_silu)
                                for c in range(DCH):
                                    mm = e.matmul(psP[:, :], wxt[:, c, :], xcv[:, c, 3:3 + T],
                                                  start=(c == 0), stop=(c == DCH - 1))
                                ctx.inc(mm, "pe", 1)
                            pe_pj = ctx.val("pe")

                            @on("scalar")
                            def _(e, pe_pj=pe_pj):
                                ctx.wait(e, "pe", pe_pj)
                                e.copy(out=projt[:, :], in_=psP[:, :]).then_inc(*ctx.bump("sce", 1))
                            sce_pj = ctx.val("sce")

                            @on("sync")
                            def _(e, sce_pj=sce_pj, tile=tile):
                                ctx.wait(e, "sce", sce_pj)
                                e.dma_start(out=bc_s[tile % 2, :, :], in_=projt[64:96, :]).then_inc(*ctx.bump("dma", 16))
                            dma_bc = ctx.val("dma")

                            if DEBUG_SCRATCH and b == 0 and tile == 0:
                                @on("sync")
                                def _(e):
                                    e.dma_start(out=dbg_b[:, 0:8192], in_=ln1Tb[:, 0, :, :, :].rearrange("p a j t -> p (a j t)")).then_inc(*ctx.bump("dmo", 16))
                                    e.dma_start(out=dbg_c[:, 2048:2560], in_=xcv[:, 0, 3:3 + T]).then_inc(*ctx.bump("dmo", 16))
                                    e.dma_start(out=dbg_c[:, 2560:3072], in_=zbuf[:, 0, :]).then_inc(*ctx.bump("dmo", 16))
                                    e.dma_start(out=dbg_c[0:96, 3072:3584], in_=projt[:, :]).then_inc(*ctx.bump("dmo", 16))

                    with nc.sbuf_tensor(f"yscan_{b}_{tile}", [128, DCH, T], F32) as yscan, \
                         nc.psum_tensor(f"psD0_{b}_{tile}", [128, T], F32) as psD0, \
                         nc.psum_tensor(f"psD1_{b}_{tile}", [128, T], F32) as psD1, \
                         nc.psum_tensor(f"psA0_{b}_{tile}", [128, 512], BF16) as psA0h, \
                         nc.psum_tensor(f"psA1_{b}_{tile}", [128, 512], BF16) as psA1h:

                        wdv = ln1Tb[:, 1 - pp].rearrange("p a j t -> p (a j t)").bitcast(F32)
                        sce_sp_prev = {}
                        for c in range(DCH):
                            par = c % 2
                            psD = psD0 if par == 0 else psD1

                            @on("tensor")
                            def _(e, c=c, psD=psD, spw=sce_sp_prev.get(c % 2), dmw_dt=dmw_dt):
                                if c == 0:
                                    ctx.wait(e, "dmw", dmw_dt)
                                ctx.wait(e, "sce", spw if spw is not None else ctx.val("sce"))
                                mm = e.matmul(psD[:, :], wdv[0:64, c * 128:(c + 1) * 128], projt[0:64, :],
                                              start=True, stop=True)
                                ctx.inc(mm, "pe", 1)
                            pe_dt = ctx.val("pe")

                            @on("scalar")
                            def _(e, pe_dt=pe_dt, c=c, psD=psD):
                                ctx.wait(e, "pe", pe_dt)
                                ctx.wait(e, "dve", ctx.val("dve"))
                                e.activation(out=scr1[:, 0:T], in_=psD[:, :], func=ACT.Relu, bias=bdtt[:, c:c + 1])
                                e.activation(out=scr1[:, 512:1024], in_=psD[:, :], func=ACT.Abs, bias=bdtt[:, c:c + 1])
                                e.activation(out=scr1[:, 512:1024], in_=scr1[:, 512:1024], func=ACT.Exp, scale=-1.0)
                                e.activation(out=scr1[:, 512:1024], in_=scr1[:, 512:1024], func=ACT.Ln, bias=cst[:, 1:2]).then_inc(*ctx.bump("sce", 1))
                            sce_sp = ctx.val("sce")
                            sce_sp_prev[c % 2] = sce_sp

                            @on("vector")
                            def _(e, sce_sp=sce_sp, c=c):
                                ctx.wait(e, "sce", sce_sp)
                                e.tensor_tensor(out=dta[:, c, :], in0=scr1[:, 0:T], in1=scr1[:, 512:1024], op=ALU.add); ctx.fence_inc(e, "dve")

                        d1_prev = {}
                        if gt + 1 < B * NT:
                            nb2, nt2 = divmod(gt + 1, NT)
                            emit_A_load(nb2, nt2, (gt + 1) % 2, 0, x_pe_guard=prev_b_mm[gt])
                        for sg in range(SG):
                            @on("sync")
                            def _(e, sg=sg, tile=tile, dma_bc=dma_bc,
                                  w_tree=d1_prev.get("tree"), w_gp=d1_prev.get("gp")):
                                ctx.wait(e, "dma", dma_bc)
                                if sg > 0:
                                    ctx.wait(e, "gp", w_gp)
                                    ctx.wait(e, "dve", w_tree)
                                e.dma_start(out=bcgB[:, 0, :, :],
                                            in_=bc_s[tile % 2, sg * SGS:(sg + 1) * SGS, :].partition_broadcast(128)).then_inc(*ctx.bump("dma", 16))
                                e.dma_start(out=bcgB[:, 1, :, :],
                                            in_=bc_s[tile % 2, S + sg * SGS:S + (sg + 1) * SGS, :].partition_broadcast(128)).then_inc(*ctx.bump("dma", 16))
                            dma_g = ctx.val("dma")

                            @on("vector")
                            def _(e, w_gp=d1_prev.get("gp")):
                                if w_gp is not None:
                                    ctx.wait(e, "gp", w_gp)
                                else:
                                    ctx.wait(e, "gp", ctx.val("gp"))
                                e.tensor_tensor(out=scr1[:, 0:T], in0=dta[:, 0, :], in1=xcv[:, 0, 3:3 + T], op=ALU.mult); ctx.fence_inc(e, "dve")
                            dve_dtx = {0: ctx.val("dve")}

                            @on("scalar")
                            def _(e, sg=sg, w_scan=d1_prev.get("scan"), dve_d0=dve_dtx[0]):
                                ctx.wait(e, "dve", w_scan if w_scan is not None else dve_d0)
                                for si in range(SGS):
                                    a_ = e.activation(out=agrp[:, si, :], in_=dta[:, 0, :], func=ACT.Exp,
                                                      scale=ang[:, 0, sg * SGS + si:sg * SGS + si + 1])
                                ctx.inc(a_, "sce", 1)
                            sce_a = {0: ctx.val("sce")}
                            gp_u = {}
                            dve_scan = {}
                            dve_tree = {}

                            def emit_ctree(cc, sg=sg):
                                pr = cc % 2
                                tmp = scr1[:, ((cc + 1) % 2) * 512:((cc + 1) % 2) * 512 + T]

                                @on("vector")
                                def _(e, cc=cc, pr=pr, tmp=tmp, sg=sg):
                                    if sg == 0:
                                        e.tensor_tensor(out=yscan[:, cc, :], in0=ugrp[:, pr, 0, :], in1=bcgB[:, 1, 0, :], op=ALU.mult)
                                    else:
                                        e.tensor_tensor(out=tmp, in0=ugrp[:, pr, 0, :], in1=bcgB[:, 1, 0, :], op=ALU.mult)
                                        e.tensor_tensor(out=yscan[:, cc, :], in0=yscan[:, cc, :], in1=tmp, op=ALU.add)
                                    for si in range(1, SGS):
                                        e.tensor_tensor(out=tmp, in0=ugrp[:, pr, si, :], in1=bcgB[:, 1, si, :], op=ALU.mult)
                                        e.tensor_tensor(out=yscan[:, cc, :], in0=yscan[:, cc, :], in1=tmp, op=ALU.add)
                                    ctx.fence_inc(e, "dve")
                                dve_tree[cc] = ctx.val("dve")

                            for c in range(DCH):
                                par = c % 2

                                @on("gpsimd")
                                def _(e, c=c, par=par, dma_g=dma_g, dtx_c=dve_dtx[c],
                                      war=dve_tree.get(c - 2)):
                                    ctx.wait(e, "dve", dtx_c)
                                    ctx.wait(e, "dma", dma_g)
                                    if war is not None:
                                        ctx.wait(e, "dve", war)
                                    e.tensor_tensor(out=ugrp[:, par, :, :],
                                                    in0=scr1[:, par * 512:par * 512 + T].unsqueeze(1).broadcast_to([128, SGS, T]),
                                                    in1=bcgB[:, 0, :, :], op=ALU.mult); ctx.fence_inc(e, "gp")
                                gp_u[c] = ctx.val("gp")

                                if c + 1 < DCH:
                                    @on("vector")
                                    def _(e, c=c):
                                        e.tensor_tensor(out=scr1[:, ((c + 1) % 2) * 512:((c + 1) % 2) * 512 + T],
                                                        in0=dta[:, c + 1, :], in1=xcv[:, c + 1, 3:3 + T], op=ALU.mult); ctx.fence_inc(e, "dve")
                                    dve_dtx[c + 1] = ctx.val("dve")

                                if c >= 1:
                                    emit_ctree(c - 1)

                                @on("vector")
                                def _(e, c=c, sg=sg, par=par, gp_u_c=gp_u[c], sce_a_c=sce_a[c]):
                                    ctx.wait(e, "gp", gp_u_c)
                                    ctx.wait(e, "sce", sce_a_c)
                                    for si in range(SGS):
                                        e.tensor_tensor_scan(out=ugrp[:, par, si, :], data0=agrp[:, si, :],
                                                             data1=ugrp[:, par, si, :],
                                                             initial=hstate[:, c, sg * SGS + si:sg * SGS + si + 1],
                                                             op0=ALU.mult, op1=ALU.add)
                                    e.drain()
                                    e.tensor_copy(hstate[:, c, sg * SGS:(sg + 1) * SGS],
                                                  ugrp[:, par, :, T - 1:T].rearrange("p s o -> p (s o)"))
                                    e.drain(); ctx.fence_inc(e, "dve")
                                dve_scan[c] = ctx.val("dve")

                                if c + 1 < DCH:
                                    @on("scalar")
                                    def _(e, c=c, sg=sg, scan_war=dve_scan[c]):
                                        ctx.wait(e, "dve", scan_war)
                                        for si in range(SGS):
                                            a_ = e.activation(out=agrp[:, si, :], in_=dta[:, c + 1, :], func=ACT.Exp,
                                                              scale=ang[:, c + 1, sg * SGS + si:sg * SGS + si + 1])
                                        ctx.inc(a_, "sce", 1)
                                    sce_a[c + 1] = ctx.val("sce")

                            emit_ctree(DCH - 1)
                            d1_prev["tree"] = dve_tree[DCH - 1]
                            d1_prev["scan"] = dve_scan[DCH - 1]
                            d1_prev["gp"] = gp_u[DCH - 1]

                            if gt + 1 < B * NT:
                                nb2, nt2 = divmod(gt + 1, NT)
                                emit_A_sub(nb2, nt2, (gt + 1) % 2, sg, psA0h, psA1h)
                                if sg + 1 < NSUB:
                                    emit_A_load(nb2, nt2, (gt + 1) % 2, sg + 1,
                                                x_pe_guard=prev_b_mm[gt] if sg + 1 < 2 else None)
                                if sg == SG - 1:
                                    hoist_sce[gt + 1] = ctx.val("sce")

                        dzs_all = ctx.val("dzs")
                        fin_fence = {}
                        zld_rec = {}
                        for c in range(DCH):
                            @on("gpsimd")
                            def _(e, c=c, dzs_all=dzs_all, w_dve=fin_fence.get(c - 2)):
                                if c == 0:
                                    ctx.wait(e, "dzs", dzs_all)
                                if w_dve is not None:
                                    ctx.wait(e, "dve", w_dve)
                                e.dma_start(out=zbuf[:, c % 2, :], in_=zsl_s[c, :, :]).then_inc(*ctx.bump("dzl", 16))
                            zld_rec[c] = ctx.val("dzl")

                            @on("vector")
                            def _(e, c=c, w_dzl=zld_rec[c]):
                                ctx.wait(e, "dzl", w_dzl)
                                f1 = scr1[:, 0:T]
                                f2 = scr1[:, 512:1024]
                                e.scalar_tensor_tensor(out=f1, in0=xcv[:, c, 3:3 + T], scalar=dvt[:, c:c + 1],
                                                       in1=yscan[:, c, :], op0=ALU.mult, op1=ALU.add)
                                e.tensor_tensor(out=f2, in0=f1, in1=zbuf[:, c % 2, :], op=ALU.mult)
                                yb = xcv[:, c, :].bitcast(BF16)
                                e.tensor_copy(yb[:, 0:T], f2)
                                e.tensor_tensor(out=yb[:, T:2 * T], in0=f2, in1=yb[:, 0:T], op=ALU.subtract); ctx.fence_inc(e, "dve")
                            fin_fence[c] = ctx.val("dve")
                        dve_fin = ctx.val("dve")

                        if DEBUG_SCRATCH and b == 0 and tile == 0:
                            @on("sync")
                            def _(e, dve_fin=dve_fin):
                                ctx.wait(e, "dve", dve_fin)
                                e.dma_start(out=dbg_c[:, 3584:4096], in_=dta[:, 0, :]).then_inc(*ctx.bump("dmo", 16))
                                e.dma_start(out=dbg_c[:, 1536:2048], in_=yscan[:, 0, :]).then_inc(*ctx.bump("dmo", 16))

                    with nc.sbuf_tensor(f"wob_{b}_{tile}", [128, 2, 2, D], BF16) as wob, \
                         nc.sbuf_tensor(f"obufE_{b}_{tile}", [128, 2, D], F32) as obuf, \
                         nc.psum_tensor(f"psE0_{b}_{tile}", [128, 512], F32) as psE0, \
                         nc.psum_tensor(f"psE1_{b}_{tile}", [128, 512], F32) as psE1, \
                         nc.psum_tensor(f"psE2_{b}_{tile}", [128, 512], F32) as psE2, \
                         nc.psum_tensor(f"psE3_{b}_{tile}", [128, 512], F32) as psE3, \
                         nc.psum_tensor(f"psE4_{b}_{tile}", [128, 512], F32) as psE4, \
                         nc.psum_tensor(f"psE5_{b}_{tile}", [128, 512], F32) as psE5, \
                         nc.psum_tensor(f"psE6_{b}_{tile}", [128, 512], F32) as psE6, \
                         nc.psum_tensor(f"psE7_{b}_{tile}", [128, 512], F32) as psE7:
                        psE = [psE0, psE1, psE2, psE3, psE4, psE5, psE6, psE7]
                        for c in range(DCH):
                            par = c % 2

                            osem = "dwo0" if par == 0 else "dwo1"

                            @on("sync")
                            def _(e, c=c, par=par, dve_fin=dve_fin, osem=osem):
                                ctx.wait(e, osem, ctx.val(osem))
                                if c == 0:
                                    ctx.wait(e, "dve", dve_fin)
                                pv = pe_wo_prev.get(par)
                                if pv is not None:
                                    ctx.wait(e, "pe", pv)
                                e.dma_start(out=wob[:, par, 0, :], in_=woh_s[c * 128:(c + 1) * 128, :]).then_inc(*ctx.bump(osem, 16))
                                e.dma_start(out=wob[:, par, 1, :], in_=wol_s[c * 128:(c + 1) * 128, :]).then_inc(*ctx.bump(osem, 16))
                            dmw_o = ctx.val(osem)

                            @on("tensor")
                            def _(e, c=c, par=par, dmw_o=dmw_o, dve_fin=dve_fin, osem=osem):
                                ctx.wait(e, osem, dmw_o)
                                if c == 0:
                                    ctx.wait(e, "dve", dve_fin)
                                    ctx.wait(e, "sce", ctx.val("sce"))
                                yb = xcv[:, c, :].bitcast(BF16)
                                for mb in range(4):
                                    for (yl, wl) in ((0, 0), (0, 1), (1, 0)):
                                        lhs = yb[:, yl * T + mb * 128:yl * T + (mb + 1) * 128]
                                        for nb in range(2):
                                            mm = e.matmul(psE[mb * 2 + nb][:, :], lhs, wob[:, par, wl, nb * 512:(nb + 1) * 512],
                                                          start=(c == 0 and yl == 0 and wl == 0),
                                                          stop=(c == DCH - 1 and yl == 1))
                                ctx.inc(mm, "pe", 1)
                            pe_wo_prev[par] = ctx.val("pe")
                        pe_wo = ctx.val("pe")

                        dxo_prev = {}
                        for mb in range(4):
                            @on("sync")
                            def _(e, b=b, t0=t0, mb=mb):
                                ctx.wait(e, "dve", ctx.val("dve"))
                                e.dma_start(out=scr1[:, :], in_=x_in[b, t0 + mb * 128:t0 + (mb + 1) * 128, :]).then_inc(*ctx.bump("dma", 16))
                            dma_r = ctx.val("dma")

                            @on("vector")
                            def _(e, pe_wo=pe_wo, dma_r=dma_r, mb=mb, dxo_safe=dxo_prev.get(mb % 2)):
                                ctx.wait(e, "pe", pe_wo)
                                ctx.wait(e, "dma", dma_r)
                                if dxo_safe is not None:
                                    ctx.wait(e, "dxo", dxo_safe)
                                for nb in range(2):
                                    e.tensor_tensor(out=obuf[:, mb % 2, nb * 512:(nb + 1) * 512], in0=psE[mb * 2 + nb][:, :],
                                                    in1=scr1[:, nb * 512:(nb + 1) * 512], op=ALU.add)
                                e.engine_nop().then_inc(*ctx.bump("dve", 1))
                            dve_x2 = ctx.val("dve")

                            @on("sync")
                            def _(e, dve_x2=dve_x2, b=b, t0=t0, mb=mb):
                                ctx.wait(e, "dxo", ctx.val("dxo"))
                                ctx.wait(e, "dve", dve_x2)
                                e.dma_start(out=x2_s[b, t0 + mb * 128:t0 + (mb + 1) * 128, :], in_=obuf[:, mb % 2, :]).then_inc(*ctx.bump("dxo", 16))
                            dxo_prev[mb % 2] = ctx.val("dxo")

        with ExitStack() as ffn_es:
            wqr = ffn_es.enter_context(nc.sbuf_tensor("wqr", [128, DMCH, DH], BF16))
            k1T = ffn_es.enter_context(nc.sbuf_tensor("k1T", [128, DMCH, L], BF16))
            hlin = ffn_es.enter_context(nc.sbuf_tensor("hlin", [128, 2 + L], F32))
            hbuf = ffn_es.enter_context(nc.sbuf_tensor("hbuf", [128, L], F32))
            macc = ffn_es.enter_context(nc.sbuf_tensor("macc", [128, L], F32))
            i1bc = ffn_es.enter_context(nc.sbuf_tensor("i1bc", [128, L], F32))
            s2bc = ffn_es.enter_context(nc.sbuf_tensor("s2bc", [128, L], F32))
            k2b = ffn_es.enter_context(nc.sbuf_tensor("k2b", [128, 2, T], BF16))
            obufF = ffn_es.enter_context(nc.sbuf_tensor("obufF", [128, 2, D], F32))
            w2b = ffn_es.enter_context(nc.sbuf_tensor("w2b", [128, 2, D], BF16))

            for b in range(B):
                ctx.barrier(on, ("dxo", "dyo", "dho", "dma", "dmo"))

                with nc.psum_tensor(f"fpsbH_{b}", [128, 512], BF16) as psbH, \
                     nc.psum_tensor(f"fpsbL_{b}", [128, 512], BF16) as psbL:
                    for tt in range(L // 128):
                        @on("sync")
                        def _(e, b=b, tt=tt):
                            ctx.wait(e, "dve", ctx.val("dve"))
                            ctx.wait(e, "dma", ctx.val("dma"))
                            e.dma_start(out=scr1[:, :], in_=x2_s[b, tt * 128:(tt + 1) * 128, :]).then_inc(*ctx.bump("dma", 16))
                        dma_x = ctx.val("dma")

                        @on("vector")
                        def _(e, dma_x=dma_x):
                            ctx.wait(e, "dma", dma_x)
                            e.engine_nop().then_inc(*ctx.bump("dve", 1))

                        sce_ln2 = emit_ln(hlin[:, 0:1024], hlin[:, 1024:2048], scr1[:, :])

                        @on("vector")
                        def _(e, b=b, tt=tt, sce_ln2=sce_ln2):
                            ctx.wait(e, "sce", sce_ln2)
                            ctx.wait(e, "pe", ctx.val("pe"))
                            ln2 = hlin[:, 1024:2048]
                            e.reduce_max(out=scrs[:, 8:9], in_=ln2, axis=AX.X, apply_absolute_value=True)
                            e.drain()
                            e.tensor_single_scalar(out=scrs[:, 8:9], in_=scrs[:, 8:9], scalar=1e-5, op=ALU.max)
                            e.drain()
                            e.reciprocal(out=scrs[:, 9:10], in_=scrs[:, 8:9])
                            e.drain()
                            e.tensor_single_scalar(out=scrs[:, 9:10], in_=scrs[:, 9:10], scalar=127.0, op=ALU.mult)
                            e.drain()
                            e.tensor_scalar(out=hlin[:, 0:1024], in0=ln2, scalar1=scrs[:, 9:10], scalar2=RND,
                                            op0=ALU.mult, op1=ALU.add)
                            e.tensor_single_scalar(out=scrb[:, 0:1024], in_=hlin[:, 0:1024], scalar=RND, op=ALU.subtract)
                            e.tensor_tensor(out=scrs[:, 10:11], in0=scrs[:, 8:9], in1=k1c[:, :], op=ALU.mult)
                            e.drain(); ctx.fence_inc(e, "dve")
                        dve_k = ctx.val("dve")

                        @on("sync")
                        def _(e, dve_k=dve_k, b=b, tt=tt):
                            ctx.wait(e, "dve", dve_k)
                            e.dma_start(out=i1r_s[b, tt * 128:(tt + 1) * 128].unsqueeze(1), in_=scrs[:, 10:11]).then_inc(*ctx.bump("dma", 16))

                        for half in range(2):
                            pb = psbH if half == 0 else psbL

                            @on("tensor")
                            def _(e, dve_k=dve_k, half=half, pb=pb):
                                if half == 0:
                                    ctx.wait(e, "dve", dve_k)
                                ctx.wait(e, "sce", ctx.val("sce"))
                                for q in range(4):
                                    j = half * 4 + q
                                    mm = tpose(e, pb[:, q * 128:(q + 1) * 128], scrb[:, j * 128:(j + 1) * 128],
                                               idnb[:, :], q == 0, q == 3)
                                ctx.inc(mm, "pe", 1)
                            pe_t = ctx.val("pe")

                            @on("scalar")
                            def _(e, pe_t=pe_t, tt=tt, half=half, pb=pb):
                                ctx.wait(e, "pe", pe_t)
                                e.copy(out=k1T[:, half * 4:(half + 1) * 4, tt * 128:(tt + 1) * 128],
                                       in_=pb[:, :].rearrange("p (q x) -> p q x", x=128)).then_inc(*ctx.bump("sce", 1))

                if DEBUG_SCRATCH and b == 0:
                    @on("sync")
                    def _(e):
                        ctx.wait(e, "sce", ctx.val("sce"))
                        e.dma_start(out=dbg_k[:, :, :], in_=k1T[:, :, :]).then_inc(*ctx.bump("dmo", 16))

                @on("sync")
                def _(e, b=b):
                    ctx.wait(e, "pe", ctx.val("pe"))
                    ctx.wait(e, "dma", ctx.val("dma"))
                    ctx.wait(e, "dmw", ctx.val("dmw"))
                    for j in range(DMCH):
                        e.dma_start(out=wqr[:, j, :], in_=w1qt_s[j, :, :]).then_inc(*ctx.bump("dmw", 16))
                    e.dma_start(out=i1bc[:, :], in_=i1r_s[b, :].unsqueeze(0).partition_broadcast(128).squeeze(1)).then_inc(*ctx.bump("dma", 16))
                dmw_w1 = ctx.val("dmw")
                dma_i1 = ctx.val("dma")

                @on("vector")
                def _(e):
                    ctx.wait(e, "gp", ctx.val("gp"))
                    e.memset(hlin[:, 0:1], 0.0)
                    e.memset(hlin[:, 1 + L:2 + L], 0.0)
                    e.memset(macc[:], 0.0)
                    e.engine_nop().then_inc(*ctx.bump("dve", 1))

                with nc.psum_tensor(f"psH_{b}", [128, 4 * 512], F32) as psH:
                    sce_h_prev = None
                    for hc in range(HCH):
                        @on("tensor")
                        def _(e, hc=hc, dmw_w1=dmw_w1):
                            ctx.wait(e, "dmw", dmw_w1)
                            if hc == 0:
                                ctx.wait(e, "sce", ctx.val("sce"))
                            if hc > 0:
                                ctx.wait(e, "dve", ctx.val("dve"))
                            for nb in range(4):
                                for k in range(8):
                                    mm = e.matmul(psH[:, nb * 512:(nb + 1) * 512], wqr[:, k, hc * 128:(hc + 1) * 128],
                                                  k1T[:, k, nb * 512:(nb + 1) * 512], start=(k == 0), stop=(k == 7))
                            ctx.inc(mm, "pe", 1)
                        pe_h = ctx.val("pe")

                        @on("vector")
                        def _(e, pe_h=pe_h, dma_i1=dma_i1, sce_h_prev=sce_h_prev):
                            ctx.wait(e, "pe", pe_h)
                            ctx.wait(e, "dma", dma_i1)
                            ctx.wait(e, "dho", ctx.val("dho"))
                            if sce_h_prev is not None:
                                ctx.wait(e, "sce", sce_h_prev)
                            for nb in range(4):
                                e.tensor_tensor(out=hlin[:, 1 + nb * 512:1 + (nb + 1) * 512],
                                                in0=psH[:, nb * 512:(nb + 1) * 512],
                                                in1=i1bc[:, nb * 512:(nb + 1) * 512], op=ALU.mult)
                            z1 = scrs[:, 15:16]
                            e.scalar_tensor_tensor(out=hbuf[:, :], in0=hlin[:, 0:L], scalar=fcwt[:, hc, 0:1],
                                                   in1=z1.broadcast_to([128, L]), op0=ALU.mult, op1=ALU.add)
                            e.scalar_tensor_tensor(out=hbuf[:, :], in0=hlin[:, 1:1 + L], scalar=fcwt[:, hc, 1:2],
                                                   in1=hbuf[:, :], op0=ALU.mult, op1=ALU.add)
                            e.scalar_tensor_tensor(out=hbuf[:, :], in0=hlin[:, 2:2 + L], scalar=fcwt[:, hc, 2:3],
                                                   in1=hbuf[:, :], op0=ALU.mult, op1=ALU.add); ctx.fence_inc(e, "dve")
                        dve_cv = ctx.val("dve")

                        @on("scalar")
                        def _(e, dve_cv=dve_cv, hc=hc):
                            ctx.wait(e, "dve", dve_cv)
                            e.activation(out=hbuf[:, :], in_=hbuf[:, :], func=ACT.Silu, bias=fcbt[:, hc:hc + 1])
                            e.activation(out=hlin[:, 1:1 + L], in_=hbuf[:, :], func=ACT.Abs).then_inc(*ctx.bump("sce", 1))
                        sce_h = ctx.val("sce")
                        sce_h_prev = sce_h

                        @on("vector")
                        def _(e, sce_h=sce_h):
                            ctx.wait(e, "sce", sce_h)
                            e.tensor_tensor(out=macc[:, :], in0=macc[:, :], in1=hlin[:, 1:1 + L], op=ALU.max); ctx.fence_inc(e, "dve")

                        @on("sync")
                        def _(e, sce_h=sce_h, hc=hc):
                            ctx.wait(e, "dho", ctx.val("dho"))
                            ctx.wait(e, "sce", sce_h)
                            e.dma_start(out=h_scr[hc * 128:(hc + 1) * 128, :], in_=hbuf[:, :]).then_inc(*ctx.bump("dho", 16))

                with nc.psum_tensor(f"psM_{b}", [128, 512], F32) as psM:
                    dve_red = ctx.val("dve")
                    for blk in range(4):
                        @on("tensor")
                        def _(e, blk=blk, dve_red=dve_red):
                            if blk == 0:
                                ctx.wait(e, "dve", ctx.val("dve"))
                            else:
                                ctx.wait(e, "dve", ctx.val("dve"))
                            for q in range(4):
                                it = blk * 4 + q
                                mm = tpose(e, psM[:, q * 128:(q + 1) * 128], macc[:, it * 128:(it + 1) * 128],
                                           idn[:, :], q == 0, q == 3)
                            ctx.inc(mm, "pe", 1)
                        pe_m = ctx.val("pe")

                        @on("vector")
                        def _(e, pe_m=pe_m, blk=blk):
                            ctx.wait(e, "pe", pe_m)
                            for q in range(4):
                                it = blk * 4 + q
                                e.reduce_max(out=inv2t[:, it:it + 1], in_=psM[:, q * 128:(q + 1) * 128], axis=AX.X)
                            e.drain()
                            e.engine_nop().then_inc(*ctx.bump("dve", 1))

                    @on("vector")
                    def _(e, b=b):
                        e.drain()
                        e.tensor_single_scalar(out=inv2t[:, :], in_=inv2t[:, :], scalar=1e-5, op=ALU.max)
                        e.drain()
                        e.reciprocal(out=scr1[:, 0:16], in_=inv2t[:, :])
                        e.drain()
                        e.tensor_single_scalar(out=scr1[:, 0:16], in_=scr1[:, 0:16], scalar=127.0, op=ALU.mult)
                        e.tensor_tensor(out=inv2t[:, :], in0=inv2t[:, :],
                                        in1=k2c[:, :].broadcast_to([128, 16]), op=ALU.mult)
                        e.drain(); ctx.fence_inc(e, "dve")
                    dve_s2 = ctx.val("dve")

                    @on("tensor")
                    def _(e, dve_s2=dve_s2):
                        ctx.wait(e, "dve", dve_s2)
                        mm = tpose(e, psM[0:16, 0:128], scr1[:, 0:16], idn[:, :], True, True)
                        ctx.inc(mm, "pe", 1)
                    pe_s2 = ctx.val("pe")

                    @on("scalar")
                    def _(e, pe_s2=pe_s2):
                        ctx.wait(e, "pe", pe_s2)
                        e.copy(out=scr1[0:16, 512:640], in_=psM[0:16, 0:128]).then_inc(*ctx.bump("sce", 1))
                    sce_s2 = ctx.val("sce")

                    @on("sync")
                    def _(e, sce_s2=sce_s2, b=b):
                        ctx.wait(e, "sce", sce_s2)
                        e.dma_start(out=s2r_s[b, :].rearrange("(it x) -> it x", x=128), in_=scr1[0:16, 512:640]).then_inc(*ctx.bump("dma", 16))
                    dma_s2w = ctx.val("dma")

                    @on("sync")
                    def _(e, dma_s2w=dma_s2w, b=b):
                        ctx.wait(e, "dma", dma_s2w)
                        e.dma_start(out=s2bc[:, :], in_=s2r_s[b, :].unsqueeze(0).partition_broadcast(128).squeeze(1)).then_inc(*ctx.bump("dma", 16))
                    dma_s2b = ctx.val("dma")

                with nc.psum_tensor(f"psO0_{b}", [128, 512], F32) as psO0, \
                     nc.psum_tensor(f"psO1_{b}", [128, 512], F32) as psO1, \
                     nc.psum_tensor(f"psO2_{b}", [128, 512], F32) as psO2, \
                     nc.psum_tensor(f"psO3_{b}", [128, 512], F32) as psO3, \
                     nc.psum_tensor(f"psO4_{b}", [128, 512], F32) as psO4, \
                     nc.psum_tensor(f"psO5_{b}", [128, 512], F32) as psO5, \
                     nc.psum_tensor(f"psO6_{b}", [128, 512], F32) as psO6, \
                     nc.psum_tensor(f"psO7_{b}", [128, 512], F32) as psO7:
                    psO = [psO0, psO1, psO2, psO3, psO4, psO5, psO6, psO7]
                    dve_evac_last = None
                    for tile in range(NT):
                        t0 = tile * T
                        pe_mm_prev = {}
                        for hc in range(HCH):
                            par = hc % 2

                            hsem = "dh0" if par == 0 else "dh1"
                            w2sem = "dw20" if par == 0 else "dw21"

                            @on("sync")
                            def _(e, hc=hc, par=par, t0=t0, dma_s2b=dma_s2b, hsem=hsem, w2sem=w2sem):
                                ctx.wait(e, hsem, ctx.val(hsem))
                                ctx.wait(e, w2sem, ctx.val(w2sem))
                                ctx.wait(e, "dho", ctx.val("dho"))
                                ctx.wait(e, "dma", dma_s2b)
                                pv = pe_mm_prev.get(par)
                                if pv is not None:
                                    ctx.wait(e, "pe", pv)
                                ctx.wait(e, "dve", ctx.val("dve"))
                                e.dma_start(out=hbuf[:, par * T:(par + 1) * T],
                                            in_=h_scr[hc * 128:(hc + 1) * 128, t0:t0 + T]).then_inc(*ctx.bump(hsem, 16))
                                e.dma_start(out=w2b[:, par, :], in_=w2qt_s[hc * 128:(hc + 1) * 128, :]).then_inc(*ctx.bump(w2sem, 16))
                            dma_h = ctx.val(hsem)
                            dmw_2 = ctx.val(w2sem)

                            @on("vector")
                            def _(e, dma_h=dma_h, par=par, t0=t0, pv=pe_mm_prev.get(par), hsem=hsem):
                                ctx.wait(e, hsem, dma_h)
                                if pv is not None:
                                    ctx.wait(e, "pe", pv)
                                e.tensor_tensor(out=scr1[:, 0:T], in0=hbuf[:, par * T:(par + 1) * T],
                                                in1=s2bc[:, t0:t0 + T], op=ALU.mult)
                                e.tensor_scalar(out=k2b[:, par, :], in0=scr1[:, 0:T], scalar1=RND, scalar2=RND,
                                                op0=ALU.add, op1=ALU.subtract); ctx.fence_inc(e, "dve")
                            dve_k2 = ctx.val("dve")

                            @on("tensor")
                            def _(e, dve_k2=dve_k2, dmw_2=dmw_2, hc=hc, par=par, tile=tile, dve_evac_last=dve_evac_last, w2sem=w2sem):
                                ctx.wait(e, "dve", dve_k2)
                                ctx.wait(e, w2sem, dmw_2)
                                if hc == 0 and tile > 0 and dve_evac_last is not None:
                                    ctx.wait(e, "dve", dve_evac_last)
                                for mb in range(4):
                                    for nb in range(2):
                                        mm = e.matmul(psO[mb * 2 + nb][:, :], k2b[:, par, mb * 128:(mb + 1) * 128],
                                                      w2b[:, par, nb * 512:(nb + 1) * 512],
                                                      start=(hc == 0), stop=(hc == HCH - 1))
                                ctx.inc(mm, "pe", 1)
                            pe_mm_prev[par] = ctx.val("pe")
                        pe_o = ctx.val("pe")

                        dyo_prev = {}
                        for mb in range(4):
                            @on("sync")
                            def _(e, b=b, t0=t0, mb=mb):
                                ctx.wait(e, "dve", ctx.val("dve"))
                                e.dma_start(out=scr1[:, :], in_=x2_s[b, t0 + mb * 128:t0 + (mb + 1) * 128, :]).then_inc(*ctx.bump("dma", 16))
                            dma_r = ctx.val("dma")

                            @on("vector")
                            def _(e, pe_o=pe_o, dma_r=dma_r, mb=mb, tile=tile, dyo_safe=dyo_prev.get(mb % 2)):
                                ctx.wait(e, "pe", pe_o)
                                ctx.wait(e, "dma", dma_r)
                                if dyo_safe is not None:
                                    ctx.wait(e, "dyo", dyo_safe)
                                for nb in range(2):
                                    e.scalar_tensor_tensor(out=obufF[:, mb % 2, nb * 512:(nb + 1) * 512], in0=psO[mb * 2 + nb][:, :],
                                                           scalar=inv2t[:, tile * 4 + mb:tile * 4 + mb + 1],
                                                           in1=scr1[:, nb * 512:(nb + 1) * 512], op0=ALU.mult, op1=ALU.add)
                                e.engine_nop().then_inc(*ctx.bump("dve", 1))
                            dve_o = ctx.val("dve")
                            dve_evac_last = dve_o

                            @on("sync")
                            def _(e, dve_o=dve_o, b=b, t0=t0, mb=mb):
                                ctx.wait(e, "dyo", ctx.val("dyo"))
                                ctx.wait(e, "dve", dve_o)
                                e.dma_start(out=y_out[b, t0 + mb * 128:t0 + (mb + 1) * 128, :], in_=obufF[:, mb % 2, :]).then_inc(*ctx.bump("dyo", 16))
                            dyo_prev[mb % 2] = ctx.val("dyo")

        @on("sync")
        def _(e):
            for s_ in ("dyo", "dxo", "dho", "dmo", "dma", "dmw",
                       "dwx0", "dwx1", "dwo0", "dwo1", "dh0", "dh1", "dw20", "dw21"):
                ctx.wait(e, s_, ctx.val(s_))

    return nc


def kernel(L0_attn_hidden_states, L0_attn_W_in, L0_attn_conv_w, L0_attn_conv_b,
           L0_attn_W_x, L0_attn_W_dt, L0_attn_b_dt, L0_attn_A_log, L0_attn_D,
           L0_attn_W_out, L0_ffn_w1, L0_ffn_conv_w, L0_ffn_conv_b, L0_ffn_w2):
    x = np.ascontiguousarray(np.asarray(L0_attn_hidden_states, dtype=np.float32))
    weights = dict(
        w_in=np.ascontiguousarray(np.asarray(L0_attn_W_in, dtype=np.float32)),
        conv_w=np.ascontiguousarray(np.asarray(L0_attn_conv_w, dtype=np.float32)),
        conv_b=np.ascontiguousarray(np.asarray(L0_attn_conv_b, dtype=np.float32)),
        w_x=np.ascontiguousarray(np.asarray(L0_attn_W_x, dtype=np.float32)),
        w_dt=np.ascontiguousarray(np.asarray(L0_attn_W_dt, dtype=np.float32)),
        b_dt=np.ascontiguousarray(np.asarray(L0_attn_b_dt, dtype=np.float32)),
        a_log=np.ascontiguousarray(np.asarray(L0_attn_A_log, dtype=np.float32)),
        d_vec=np.ascontiguousarray(np.asarray(L0_attn_D, dtype=np.float32)),
        w_out=np.ascontiguousarray(np.asarray(L0_attn_W_out, dtype=np.float32)),
        f_w1=np.ascontiguousarray(np.asarray(L0_ffn_w1, dtype=np.float32)),
        f_cw=np.ascontiguousarray(np.asarray(L0_ffn_conv_w, dtype=np.float32).reshape(DH, 3)),
        f_cb=np.ascontiguousarray(np.asarray(L0_ffn_conv_b, dtype=np.float32)),
        f_w2=np.ascontiguousarray(np.asarray(L0_ffn_w2, dtype=np.float32)),
    )
    nc = build()
    in_maps = []
    for i in range(8):
        m = dict(weights)
        m["x_in"] = x[2 * i:2 * i + 2]
        in_maps.append(m)
    res = run_bass_kernel_spmd(nc, in_maps, core_ids=list(range(8)))
    return np.concatenate([r["y_out"] for r in res.results], axis=0)
